# version 25 (final)
# baseline (speedup 1.0000x reference)
import numpy as np
from contextlib import ExitStack

import concourse.bass as bass
import concourse.mybir as mybir
from concourse.bass_utils import run_bass_kernel_spmd

F32 = mybir.dt.float32
F16 = mybir.dt.float16
BF16 = mybir.dt.bfloat16
U8 = mybir.dt.uint8
AF = mybir.ActivationFunctionType
ALU = mybir.AluOpType
AXX = mybir.AxisListType.X

NCORES = 8
NEGM_TOK = [0]
NH = 16
DH = 128
H = NH * DH
INV_SQRT_DH = 1.0 / float(np.sqrt(DH))


class Cfg:
    def __init__(self, S=8192, MLP=16384, ffn_tsplit=2):
        self.S, self.MLP = S, MLP
        self.T = S // NCORES
        self.TT = self.T // 128
        self.SC = S // 512
        self.KT = S // 128
        self.BCH = min(512, self.T)
        self.SCB = S // self.BCH
        self.NTL = MLP // 128 // NCORES
        self.NTG = MLP // 128
        self.ffn_tsplit = ffn_tsplit
        self.T2 = self.T // ffn_tsplit
        assert self.T % 128 == 0 and S % 512 == 0 and self.T2 % 128 == 0
        assert self.T2 <= 512 and MLP % (128 * NCORES) == 0
        assert (2 * self.KT) % 4 == 0


def build_program(cfg: Cfg):
    S, T, MLP, TT, SC, KT = cfg.S, cfg.T, cfg.MLP, cfg.TT, cfg.SC, cfg.KT
    NTL, NTG, T2, BCH, SCB = cfg.NTL, cfg.NTG, cfg.T2, cfg.BCH, cfg.SCB
    nc = bass.Bass("TRN2", target_bir_lowering=False)

    d_x = nc.dram_tensor("x", [T, H], F32, kind="ExternalInput").ap()
    d_wqT = nc.dram_tensor("wqT", [H, 256], F32, kind="ExternalInput").ap()
    d_wkT = nc.dram_tensor("wkT", [H, 256], F32, kind="ExternalInput").ap()
    d_wvT = nc.dram_tensor("wvT", [H, 256], F32, kind="ExternalInput").ap()
    d_woT = nc.dram_tensor("woT", [H, H], F32, kind="ExternalInput").ap()
    d_cos = nc.dram_tensor("cosT", [DH, S], F32, kind="ExternalInput").ap()
    d_sin = nc.dram_tensor("sinT", [DH, S], F32, kind="ExternalInput").ap()
    d_gate = nc.dram_tensor("gate2", [128, 2], F32, kind="ExternalInput").ap()
    d_gcod = nc.dram_tensor("gcod", [MLP // NCORES, H // 2], U8, kind="ExternalInput").ap()
    d_ucod = nc.dram_tensor("ucod", [MLP // NCORES, H // 2], U8, kind="ExternalInput").ap()
    d_dcod = nc.dram_tensor("dcod", [H, MLP // 2 // NCORES], U8, kind="ExternalInput").ap()
    d_gsc = nc.dram_tensor("gsc", [MLP // NCORES, H // 32], F32, kind="ExternalInput").ap()
    d_usc = nc.dram_tensor("usc", [MLP // NCORES, H // 32], F32, kind="ExternalInput").ap()
    d_dsc = nc.dram_tensor("dsc", [H, MLP // 32 // NCORES], F32, kind="ExternalInput").ap()
    d_y = nc.dram_tensor("y", [T, H], F32, kind="ExternalOutput").ap()

    AGH = 2 if T // 2 >= BCH else 1
    d_agx_in = [nc.dram_tensor(f"agx_in{h}", [H, T // AGH], BF16) for h in range(AGH)]
    d_agx_out = [nc.dram_tensor(f"agx_out{h}", [NCORES * H, T // AGH], BF16, addr_space="Shared")
                 for h in range(AGH)]
    nblk = 3 * NTL
    d_wag_in = nc.dram_tensor("wag_in", [nblk * 128, H], BF16)
    d_wag_out = nc.dram_tensor("wag_out", [NCORES * nblk * 128, H], BF16, addr_space="Shared")
    d_cmb_in = [nc.dram_tensor(f"cmbi{h}", [128, S], BF16) for h in range(2)]
    d_cmb_out = [nc.dram_tensor(f"cmbo{h}", [NCORES * 128, S], BF16, addr_space="Shared") for h in range(2)]
    d_x1 = nc.dram_tensor("x1d", [T, H], F32)

    ctx = ExitStack()

    class Sems:
        def __init__(self):
            self.counts, self.marks, self.sems = {}, {}, {}

        def new(self, name):
            self.sems[name] = ctx.enter_context(nc.semaphore(name))
            self.counts[name] = 0

        def inc(self, instr, sname, n=1):
            instr.then_inc(self.sems[sname], n)
            self.counts[sname] += n
            return self.counts[sname]

        def dma(self, eng, ename, sname, pairs):
            self.wait(eng, ename, sname, self.counts[sname])
            for (o, i_) in pairs:
                self.inc(eng.dma_start(out=o, in_=i_), sname, 16)
            return self.counts[sname]

        def wait(self, eng, ename, sname, val=None):
            if val is None:
                val = self.counts[sname]
            if val <= 0:
                return
            key = (ename, sname)
            if self.marks.get(key, 0) >= val:
                return
            eng.wait_ge(self.sems[sname], val)
            self.marks[key] = val

    C = Sems()

    def SB(name, shape, dt=F32):
        return ctx.enter_context(nc.sbuf_tensor(name, shape, dt))

    with ctx:
        for sn in ["te", "ve", "se", "ge", "c1", "c2", "rca", "rcl",
                   "sg", "swl", "agx", "cod", "wao", "xa0", "xa1", "xs0", "xs1",
                   "cs0", "cs1", "wg0", "wg1", "wd0", "wd1", "xr0", "xr1",
                   "cst", "swo", "x1s", "st0", "st1", "cmi", "c3", "wdo", "wdo2", "cod2", "wg2",
                   "wd2", "wd3", "cst2", "xb0", "xb1", "xt0", "xt1",
                   "wu0", "wu1", "wu2", "xr2", "xr3", "st2", "st3"]:
            C.new(sn)

        block = ctx.enter_context(nc.Block())

        def on(eng, fn):
            getattr(block, eng)(fn)

        _BARRIER_SKIP = {"c1", "c2", "c3", "rca", "rcl"}

        def phase_barrier():
            snap = {k: v for k, v in C.counts.items() if k not in _BARRIER_SKIP}
            for eng, key in [("sync", "sy"), ("scalar", "s"),
                             ("vector", "v"), ("tensor", "t"), ("gpsimd", "g")]:
                def f(e, key=key, snap=snap):
                    for sn, val in snap.items():
                        C.wait(e, key, sn, val)
                on(eng, f)

        consts = SB("consts", [128, 4], F32)
        ident = SB("ident", [128, 128], BF16)
        iot = SB("iot", [128, 128], F32)
        pidx = SB("pidx", [128, 1], F32)
        ones128 = SB("ones128", [128, 128], BF16)
        s_gate = SB("s_gate", [128, 2], F32)
        gmix = SB("gmix", [128, 4], F32)
        negM = SB("negM", [128, 2], F32)
        nsc = SB("nsc", [128, 4 * SC + 6], F32)

        def f(s):
            C.dma(s, "sy", "sg", [(s_gate[:, :], d_gate[:, :])])
        on("sync", f)

        def f(g):
            C.inc(g.iota(iot[:, :], [[1, 128]], channel_multiplier=0,
                         allow_small_or_imprecise_dtypes=True), "ge")
            C.inc(g.iota(pidx[:, :], [[0, 1]], channel_multiplier=1,
                         allow_small_or_imprecise_dtypes=True), "ge")
        on("gpsimd", f)

        def f(v):
            C.inc(v.memset(consts[:, 0:1], -4.0), "ve")
            C.inc(v.memset(consts[:, 1:2], -6.0), "ve")
            C.inc(v.memset(consts[:, 2:3], 1e-5), "ve")
            C.inc(v.memset(ones128[:, :], 1.0), "ve")
            C.wait(v, "v", "ge", 2)
            C.inc(v.tensor_scalar(out=ident[:, :], in0=iot[:, :], scalar1=pidx[:, 0:1],
                                  scalar2=None, op0=ALU.is_equal), "ve")
        on("vector", f)

        def f(s):
            C.wait(s, "s", "sg", 16)
            C.inc(s.activation(out=gmix[:, 0:1], in_=s_gate[:, 0:1], func=AF.Sigmoid), "se")
            C.inc(s.activation(out=gmix[:, 2:3], in_=s_gate[:, 1:2], func=AF.Sigmoid), "se")
        on("scalar", f)

        def f(v):
            C.wait(v, "v", "se")
            C.inc(v.tensor_scalar(out=gmix[:, 1:2], in0=gmix[:, 0:1], scalar1=-1.0,
                                  scalar2=1.0, op0=ALU.mult, op1=ALU.add), "ve")
            C.inc(v.tensor_scalar(out=gmix[:, 3:4], in0=gmix[:, 2:3], scalar1=-1.0,
                                  scalar2=1.0, op0=ALU.mult, op1=ALU.add), "ve")
        on("vector", f)

        phase_barrier()
        with (nc.sbuf_tensor("qrT", [128, 2 * S], BF16) as qrT,
              nc.sbuf_tensor("krT", [128, 2 * S], BF16) as krT,
              nc.sbuf_tensor("vT", [128, 2 * S], BF16) as vT):

            with ExitStack() as bctx:
                _e = bctx.enter_context
                cod = _e(nc.sbuf_tensor("cod", [128, 2 * 1024], U8))
                nib = _e(nc.sbuf_tensor("nib", [128, 2048], U8))
                fub = _e(nc.sbuf_tensor("fub", [128, 2048], BF16))
                g8b = _e(nc.sbuf_tensor("g8b", [128, 2048], BF16))
                mtb = _e(nc.sbuf_tensor("mtb", [128, 2048], BF16))
                r1b = _e(nc.sbuf_tensor("r1b", [128, 2048], BF16))
                scl = _e(nc.sbuf_tensor("scl", [128, 2 * 64], F32))
                wn = _e(nc.sbuf_tensor("wn", [128, 2048], BF16))
                wasm = _e(nc.sbuf_tensor("wasm", [128, H], BF16))
                wstg = _e(nc.sbuf_tensor("wstg", [128, 2 * 1024], BF16))
                ptC = _e(nc.psum_tensor("ptC", [128, 1024], BF16))
                ptD = _e(nc.psum_tensor("ptD", [128, 1024], BF16))

                ptCD = [ptC, ptD]
                ptw_busy = [0, 0]
                ptw_free = [0, 0]
                cod_sem = ["cod", "cod2"]
                cod_free = [0, 0]
                cod_tok = [0, 0]
                stg_free = [0, 0]
                wd_sem = ["wdo", "wdo2"]

                def cod_load(idx, src_cod, src_scl, kb):
                    sl2 = idx % 2

                    def f(s, sl2=sl2, src_cod=src_cod, src_scl=src_scl, kb=kb):
                        C.wait(s, "sc", "ve", cod_free[sl2])
                        C.wait(s, "sc", "ve", gcod_free[sl2])
                        C.dma(s, "sc", cod_sem[sl2],
                              [(cod[:, sl2 * 1024:sl2 * 1024 + kb], src_cod),
                               (scl[:, sl2 * 64:sl2 * 64 + kb // 16], src_scl)])
                        cod_tok[sl2] = C.counts[cod_sem[sl2]]
                    on("scalar", f)

                gcod_free = [0, 0]

                def dq_block(idx, kb, wn_war=0):
                    sl2 = idx % 2
                    K2 = 2 * kb

                    def f(v, sl2=sl2, kb=kb, K2=K2, wn_war=wn_war):
                        C.wait(v, "v", cod_sem[sl2], cod_tok[sl2])
                        C.wait(v, "v", "ve")
                        cslot = cod[:, sl2 * 1024:sl2 * 1024 + kb]
                        C.inc(v.tensor_scalar(out=nib[:, 0:kb], in0=cslot,
                                              scalar1=15, scalar2=None,
                                              op0=ALU.bitwise_and), "ve")
                        C.inc(v.tensor_scalar(out=nib[:, kb:K2], in0=cslot,
                                              scalar1=4, scalar2=None,
                                              op0=ALU.logical_shift_right), "ve")
                        gcod_free[sl2] = C.counts["ve"]
                        C.wait(v, "v", "ve")
                        C.inc(v.tensor_scalar(out=g8b[:, 0:K2], in0=nib[:, 0:K2],
                                              scalar1=8.0,
                                              scalar2=None, op0=ALU.is_ge), "ve")
                        C.wait(v, "v", "ve")
                        C.inc(v.scalar_tensor_tensor(
                            out=mtb[:, 0:K2], in0=g8b[:, 0:K2],
                            scalar=-8.0, in1=nib[:, 0:K2],
                            op0=ALU.mult, op1=ALU.add), "ve")
                        C.wait(v, "v", "ve")
                        C.inc(v.tensor_scalar(out=g8b[:, 0:K2], in0=g8b[:, 0:K2],
                                              scalar1=-2.0, scalar2=1.0,
                                              op0=ALU.mult, op1=ALU.add), "ve")
                        C.inc(v.tensor_scalar(out=r1b[:, 0:K2], in0=mtb[:, 0:K2],
                                              scalar1=-4.0, scalar2=0.0,
                                              op0=ALU.add, op1=ALU.max), "ve")
                        C.wait(v, "v", "ve")
                        C.inc(v.tensor_tensor(out=fub[:, 0:K2], in0=mtb[:, 0:K2],
                                              in1=r1b[:, 0:K2], op=ALU.add), "ve")
                        C.wait(v, "v", "ve")
                        C.inc(v.tensor_scalar(out=r1b[:, 0:K2], in0=mtb[:, 0:K2],
                                              scalar1=-6.0, scalar2=0.0,
                                              op0=ALU.add, op1=ALU.max), "ve")
                        C.wait(v, "v", "ve")
                        C.inc(v.scalar_tensor_tensor(out=fub[:, 0:K2], in0=fub[:, 0:K2],
                                                     scalar=0.5, in1=r1b[:, 0:K2],
                                                     op0=ALU.mult, op1=ALU.add), "ve")
                        C.wait(v, "v", "ve")
                        C.inc(v.tensor_tensor(out=fub[:, 0:K2], in0=fub[:, 0:K2],
                                              in1=g8b[:, 0:K2], op=ALU.mult), "ve")
                        C.wait(v, "v", "ve")
                        C.wait(v, "v", "te", wn_war)
                        sb = scl[:, sl2 * 64:sl2 * 64 + kb // 16].broadcast_to((128, kb // 16, 16))
                        C.inc(v.scalar_tensor_tensor(
                            out=wn[:, 0:K2:2].rearrange("p (b r) -> p b r", r=16),
                            in0=fub[:, 0:kb].rearrange("p (b r) -> p b r", r=16),
                            scalar=1.0, in1=sb, op0=ALU.mult, op1=ALU.mult), "ve")
                        C.inc(v.scalar_tensor_tensor(
                            out=wn[:, 1:K2:2].rearrange("p (b r) -> p b r", r=16),
                            in0=fub[:, kb:K2].rearrange("p (b r) -> p b r", r=16),
                            scalar=1.0, in1=sb, op0=ALU.mult, op1=ALU.mult), "ve")
                        cod_free[sl2] = C.counts["ve"]
                    on("vector", f)
                    return C.counts["ve"]

                def tr_group(srcs_, dst_ap, wn_tok, war=None):
                    pj = tr_group.n % 2
                    tr_group.n += 1
                    n = len(srcs_)

                    def f(t, pj=pj, wn_tok=wn_tok, srcs_=srcs_):
                        C.wait(t, "t", "ve", wn_tok)
                        C.wait(t, "t", "ve", ptw_free[pj])
                        for j, sap in enumerate(srcs_):
                            C.inc(t.transpose(ptCD[pj][:, j * 128:(j + 1) * 128],
                                              sap, ident[:, :]), "te")
                        ptw_busy[pj] = C.counts["te"]
                    on("tensor", f)

                    def f(v, pj=pj, dst_ap=dst_ap, war=war, n=n):
                        if war:
                            for sn, val in war:
                                C.wait(v, "v", sn, val)
                        C.wait(v, "v", "te", ptw_busy[pj])
                        C.inc(v.tensor_copy(out=dst_ap, in_=ptCD[pj][:, 0:n * 128]), "ve")
                        ptw_free[pj] = C.counts["ve"]
                    on("vector", f)
                tr_group.n = 0

                WDB = MLP // 2 // NCORES

                def w_sources():
                    for mi, (dc, ds) in enumerate([(d_gcod, d_gsc), (d_ucod, d_usc)]):
                        for nl in range(NTL):
                            yield ("gu", mi, nl, 1024,
                                   dc[nl * 128:(nl + 1) * 128, :],
                                   ds[nl * 128:(nl + 1) * 128, :])
                    for ht in range(16):
                        yield ("wd", 0, ht, WDB,
                               d_dcod[ht * 128:(ht + 1) * 128, :],
                               d_dsc[ht * 128:(ht + 1) * 128, :])

                wao_last = [0]
                wd_last = [0, 0]

                def emit_out(kind, mi, which, kb, tok):
                    if kind == "gu":
                        for g8 in range(2):
                            tr_group([wn[:, (8 * g8 + j) * 128:(8 * g8 + j + 1) * 128]
                                      for j in range(8)],
                                     wasm[:, g8 * 1024:(g8 + 1) * 1024], tok,
                                     war=[("wao", wao_last[0])] if g8 == 0 else None)

                        def f(s, mi=mi, which=which):
                            C.wait(s, "sc", "ve")
                            bb = mi * NTL + which
                            C.dma(s, "sc", "wao",
                                  [(d_wag_in.ap()[bb * 128:(bb + 1) * 128, :], wasm[:, :])])
                        on("scalar", f)
                        wao_last[0] = C.counts["wao"]
                    else:
                        ht = which
                        for nl0 in range(0, NTL, 8):
                            nls = list(range(nl0, min(nl0 + 8, NTL)))
                            sg = tr_group.n % 2
                            tr_group([wn[:, nl * 128:(nl + 1) * 128] for nl in nls],
                                     wstg[:, sg * 1024:sg * 1024 + len(nls) * 128], tok,
                                     war=[(wd_sem[sg], wd_last[sg])])

                            def f(s, ht=ht, nls=nls, sg=sg):
                                C.wait(s, "sc", "ve")
                                C.dma(s, "sc", wd_sem[sg],
                                      [(d_wag_in.ap()[(2 * NTL + nl) * 128:
                                                      (2 * NTL + nl + 1) * 128,
                                                      ht * 128:(ht + 1) * 128],
                                        wstg[:, sg * 1024 + jj * 128:
                                             sg * 1024 + (jj + 1) * 128])
                                       for jj, nl in enumerate(nls)])
                            on("scalar", f)
                            wd_last[sg] = C.counts[wd_sem[sg]]

                def wsteps():
                    srcs = list(w_sources())
                    cod_load(0, srcs[0][4], srcs[0][5], srcs[0][3])
                    pend = None
                    for idx, (kind, mi, which, kb, sc_, ss_) in enumerate(srcs):
                        if idx + 1 < len(srcs):
                            cod_load(idx + 1, srcs[idx + 1][4], srcs[idx + 1][5],
                                     srcs[idx + 1][3])
                        if pend is not None:
                            emit_out(*pend)
                        wn_war = C.counts["te"]
                        tok = dq_block(idx, kb, wn_war)
                        pend = (kind, mi, which, kb, tok)
                        yield
                    emit_out(*pend)
                    yield

                wgen = wsteps()

                with (nc.sbuf_tensor("xa", [128, 2 * H], F32) as xa,
                      nc.sbuf_tensor("xln", [128, H], BF16) as xln,
                      nc.sbuf_tensor("xlt", [128, 16 * T], BF16) as xlt,
                      nc.sbuf_tensor("st6", [128, 24], F32) as st6,
                      nc.sbuf_tensor("mrs", [128, 4], F32) as mrs,
                      nc.psum_tensor("ptA", [128, 1024], BF16) as ptA,
                      nc.psum_tensor("ptB", [128, 1024], BF16) as ptB):
                    ptAB = [ptA, ptB]
                    xa_tok = [0, 0]
                    xb_tok = [0, 0]
                    xa_free = [0, 0]
                    pt_busy = [0, 0]
                    pt_free = [0, 0]
                    H2 = H // 2
                    TAH = T // AGH

                    def emit_ag(half):
                        def f(g, half=half):
                            C.wait(g, "g", "ve")
                            C.dma(g, "g", "agx",
                                  [(d_agx_in[half].ap()[j * 128:(j + 1) * 128, :],
                                    xlt[:, j * T + half * TAH:j * T + (half + 1) * TAH])
                                   for j in range(16)])
                        on("gpsimd", f)
                        agx_tok = C.counts["agx"]

                        def f(g, half=half, agx_tok=agx_tok):
                            C.wait(g, "g", "agx", agx_tok)
                            C.inc(g.collective_compute(
                                "AllGather", ALU.bypass, replica_groups=[list(range(NCORES))],
                                ins=[d_agx_in[half].ap().opt()],
                                outs=[d_agx_out[half].ap().opt()]), "c1")
                        on("gpsimd", f)

                    for i in range(TT):
                        sl = i % 2

                        def f(s, i=i, sl=sl):
                            C.wait(s, "sy", "ve", xa_free[sl])
                            xa_tok[sl] = C.dma(s, "sy", f"xa{sl}",
                                               [(xa[:, sl * H:sl * H + H2],
                                                 d_x[i * 128:(i + 1) * 128, 0:H2])])
                        on("sync", f)

                        def f(s, i=i, sl=sl):
                            C.wait(s, "sc", "ve", xa_free[sl])
                            xb_tok[sl] = C.dma(s, "sc", f"xb{sl}",
                                               [(xa[:, sl * H + H2:(sl + 1) * H],
                                                 d_x[i * 128:(i + 1) * 128, H2:H])])
                        on("scalar", f)

                        def f(v, sl=sl):
                            C.wait(v, "v", f"xa{sl}", xa_tok[sl])
                            C.wait(v, "v", f"xb{sl}", xb_tok[sl])
                            C.wait(v, "v", "se")
                            for c4 in range(4):
                                C.inc(v.bn_stats(out=st6[:, 6 * c4:6 * c4 + 6],
                                                 in_=xa[:, sl * H + 512 * c4:sl * H + 512 * (c4 + 1)]), "ve")
                            C.wait(v, "v", "ve")
                            C.inc(v.bn_aggr(out=mrs[:, 0:2], in_=st6[:, :]), "ve")
                        on("vector", f)

                        def f(s):
                            C.wait(s, "s", "ve")
                            C.inc(s.activation(out=mrs[:, 2:3], in_=mrs[:, 1:2], func=AF.Sqrt,
                                               bias=consts[:, 2:3], scale=1.0), "se")
                        on("scalar", f)

                        def f(v, sl=sl):
                            C.wait(v, "v", "se")
                            C.wait(v, "v", "te")
                            C.inc(v.reciprocal(out=mrs[:, 3:4], in_=mrs[:, 2:3]), "ve")
                            C.wait(v, "v", "ve")
                            C.inc(v.tensor_scalar(out=xln[:, :], in0=xa[:, sl * H:(sl + 1) * H],
                                                  scalar1=mrs[:, 0:1], scalar2=mrs[:, 3:4],
                                                  op0=ALU.subtract, op1=ALU.mult), "ve")
                            xa_free[sl] = C.counts["ve"]
                        on("vector", f)
                        ln_tok = C.counts["ve"]

                        for g8 in range(2):
                            def f(t, g8=g8, ln_tok=ln_tok):
                                C.wait(t, "t", "ve", ln_tok)
                                C.wait(t, "t", "ve", pt_free[g8])
                                for j8 in range(8):
                                    j = g8 * 8 + j8
                                    C.inc(t.transpose(ptAB[g8][:, j8 * 128:(j8 + 1) * 128],
                                                      xln[:, j * 128:(j + 1) * 128],
                                                      ident[:, :]), "te")
                                pt_busy[g8] = C.counts["te"]
                            on("tensor", f)

                            def f(v, i=i, g8=g8):
                                C.wait(v, "v", "te", pt_busy[g8])
                                for j8 in range(8):
                                    j = g8 * 8 + j8
                                    C.inc(v.tensor_copy(
                                        out=xlt[:, j * T + i * 128:j * T + (i + 1) * 128],
                                        in_=ptAB[g8][:, j8 * 128:(j8 + 1) * 128]), "ve")
                                pt_free[g8] = C.counts["ve"]
                            on("vector", f)

                        if (i + 1) % (TT // AGH) == 0:
                            emit_ag((i + 1) // (TT // AGH) - 1)



                swq = _e(nc.sbuf_tensor("swq", [128, 16 * 256], BF16))
                swk = _e(nc.sbuf_tensor("swk", [128, 16 * 256], BF16))
                swv = _e(nc.sbuf_tensor("swv", [128, 16 * 256], BF16))
                xs = _e(nc.sbuf_tensor("xs", [128, 2 * 16 * BCH], BF16))
                cs = _e(nc.sbuf_tensor("cs", [128, 2 * 2 * BCH], F16))
                rsc = _e(nc.sbuf_tensor("rsc", [128, 2 * BCH], F16))
                rp16 = _e(nc.sbuf_tensor("rp16", [128, 2 * BCH], F16))
                pp1 = _e(nc.psum_tensor("pp1", [128, 3 * 512], F32))
                pp2 = _e(nc.psum_tensor("pp2", [128, 3 * 512], F32))

                def f(g):
                    C.dma(g, "g", "swl",
                          [(dst[:, :].rearrange("p (a b) -> p a b", a=16),
                            src.rearrange("(a p) b -> p a b", p=128))
                           for dst, src in [(swq, d_wqT), (swk, d_wkT), (swv, d_wvT)]])
                on("gpsimd", f)
                sw_tok = C.counts["swl"]

                xs_tok = [0, 0]
                xs_free = [0, 0]
                cs_tok = [0, 0]
                cs_free = [0, 0]
                pp_busy = [0, 0]
                pp_free = [0, 0]
                pps = [pp1, pp2]
                for _ in range(8):
                    next(wgen, None)
                NWSTEP = (40 + SCB - 1) // SCB

                TAH = T // AGH
                H2 = H // 2
                xt_tok = [0, 0]
                tb_order = sorted(range(SCB), key=lambda tb: ((tb * BCH) % T) // TAH)

                def tb_coords(tb):
                    r, off = (tb * BCH) // T, (tb * BCH) % T
                    half = off // TAH
                    return r, half, off - half * TAH

                for ti_, tb in enumerate(tb_order):
                    sl = ti_ % 2
                    r, half, off2 = tb_coords(tb)
                    nj0 = 16 if ti_ == 0 else 8

                    def f(s, sl=sl, r=r, off2=off2, half=half, nj0=nj0):
                        C.wait(s, "sy", "c1", 1 + half)
                        C.wait(s, "sy", "te", xs_free[sl])
                        xs_tok[sl] = C.dma(s, "sy", f"xs{sl}",
                            [(xs[:, sl * 16 * BCH:sl * 16 * BCH + nj0 * BCH]
                              .rearrange("p (j c) -> p j c", j=nj0),
                              d_agx_out[half].ap()[r * H:r * H + nj0 * 128, off2:off2 + BCH]
                              .rearrange("(j p) c -> p j c", p=128))])
                    on("sync", f)

                    if ti_ + 1 < SCB:
                        sl2 = (ti_ + 1) % 2
                        r2, half2, of2 = tb_coords(tb_order[ti_ + 1])

                        def f(s, sl2=sl2, r2=r2, of2=of2, half2=half2):
                            C.wait(s, "sc", "c1", 1 + half2)
                            C.wait(s, "sc", "te", xs_free[sl2])
                            xt_tok[sl2] = C.dma(s, "sc", f"xt{sl2}",
                                [(xs[:, sl2 * 16 * BCH + 8 * BCH:(sl2 + 1) * 16 * BCH]
                                  .rearrange("p (j c) -> p j c", j=8),
                                  d_agx_out[half2].ap()[r2 * H + H2:(r2 + 1) * H, of2:of2 + BCH]
                                  .rearrange("(j p) c -> p j c", p=128))])
                        on("scalar", f)

                    def f(g, sl=sl, tb=tb):
                        C.wait(g, "g", "ve", cs_free[sl])
                        cs_tok[sl] = C.dma(g, "g", f"cs{sl}",
                            [(cs[:, sl * 2 * BCH:sl * 2 * BCH + BCH],
                              d_cos[:, tb * BCH:(tb + 1) * BCH]),
                             (cs[:, sl * 2 * BCH + BCH:(sl + 1) * 2 * BCH],
                              d_sin[:, tb * BCH:(tb + 1) * BCH])])
                    on("gpsimd", f)

                    for pi in range(2):
                        plist = ([("q", 0), ("q", 1), ("v", 0)] if pi == 0
                                 else [("k", 0), ("k", 1), ("v", 1)])

                        def f(t, sl=sl, pi=pi, plist=plist):
                            C.wait(t, "t", "swl", sw_tok)
                            C.wait(t, "t", f"xs{sl}", xs_tok[sl])
                            C.wait(t, "t", f"xt{sl}", xt_tok[sl])
                            C.wait(t, "t", "ve", pp_free[pi])
                            for j in range(16):
                                rhs = xs[:, (sl * 16 + j) * BCH:(sl * 16 + j + 1) * BCH]
                                for oi, (pr, dt) in enumerate(plist):
                                    w = {"q": swq, "k": swk, "v": swv}[pr]
                                    C.inc(t.matmul(
                                        pps[pi][:, oi * 512:oi * 512 + BCH],
                                        w[:, j * 256 + dt * 128:j * 256 + (dt + 1) * 128],
                                        rhs, start=(j == 0), stop=(j == 15)), "te")
                            pp_busy[pi] = C.counts["te"]
                            if pi == 1:
                                xs_free[sl] = C.counts["te"]
                        on("tensor", f)

                        def f(v, sl=sl, pi=pi, tb=tb):
                            C.wait(v, "v", "te", pp_busy[pi])
                            C.wait(v, "v", f"cs{sl}", cs_tok[sl])
                            C.wait(v, "v", "ve")
                            pp = pps[pi]
                            co = cs[:, sl * 2 * BCH:sl * 2 * BCH + BCH]
                            sn = cs[:, sl * 2 * BCH + BCH:(sl + 1) * 2 * BCH]
                            dst = qrT if pi == 0 else krT
                            a, b = tb * BCH, (tb + 1) * BCH
                            C.inc(v.tensor_copy(out=rp16[:, 0:BCH], in_=pp[:, 0:BCH]), "ve")
                            C.inc(v.tensor_copy(out=rp16[:, BCH:2 * BCH],
                                                in_=pp[:, 512:512 + BCH]), "ve")
                            C.inc(v.tensor_copy(out=vT[:, (0 if pi == 0 else 1) * S + a:
                                                       (0 if pi == 0 else 1) * S + b],
                                                in_=pp[:, 1024:1024 + BCH]), "ve")
                            pp_free[pi] = C.counts["ve"]
                            C.wait(v, "v", "ve")
                            C.inc(v.tensor_tensor(out=rsc[:, 0:BCH], in0=rp16[:, 0:BCH],
                                                  in1=co, op=ALU.mult), "ve")
                            C.inc(v.tensor_tensor(out=rsc[:, BCH:2 * BCH],
                                                  in0=rp16[:, BCH:2 * BCH],
                                                  in1=sn, op=ALU.mult), "ve")
                            C.wait(v, "v", "ve")
                            C.inc(v.tensor_tensor(out=dst[:, a:b], in0=rsc[:, 0:BCH],
                                                  in1=rsc[:, BCH:2 * BCH],
                                                  op=ALU.subtract), "ve")
                            C.wait(v, "v", "ve")
                            C.inc(v.tensor_tensor(out=rsc[:, 0:BCH],
                                                  in0=rp16[:, BCH:2 * BCH],
                                                  in1=co, op=ALU.mult), "ve")
                            C.wait(v, "v", "ve")
                            C.inc(v.tensor_tensor(out=rsc[:, BCH:2 * BCH],
                                                  in0=rp16[:, 0:BCH],
                                                  in1=sn, op=ALU.mult), "ve")
                            C.wait(v, "v", "ve")
                            C.inc(v.tensor_tensor(out=dst[:, S + a:S + b], in0=rsc[:, 0:BCH],
                                                  in1=rsc[:, BCH:2 * BCH], op=ALU.add), "ve")
                            if pi == 1:
                                cs_free[sl] = C.counts["ve"]
                        on("vector", f)

                        if pi == 0:
                            for _ in range(NWSTEP // 2):
                                next(wgen, None)

                    for _ in range(NWSTEP - NWSTEP // 2):
                        next(wgen, None)

                for _ in wgen:
                    pass

                for eng_, key_ in [("vector", "v"), ("scalar", "s"), ("tensor", "t")]:
                    def f(e, key_=key_):
                        C.wait(e, key_, "wao", C.counts["wao"])
                        C.wait(e, key_, "wdo", C.counts["wdo"])
                        C.wait(e, key_, "wdo2", C.counts["wdo2"])
                    on(eng_, f)

                def f(g):
                    C.wait(g, "g", "wao", C.counts["wao"])
                    C.wait(g, "g", "wdo", C.counts["wdo"])
                    C.wait(g, "g", "wdo2", C.counts["wdo2"])
                    C.inc(g.collective_compute(
                        "AllGather", ALU.bypass, replica_groups=[list(range(NCORES))],
                        ins=[d_wag_in.ap().opt()], outs=[d_wag_out.ap().opt()]), "c2")
                on("gpsimd", f)

            with (nc.sbuf_tensor("sqs", [128, 512], BF16) as sqs,
                  nc.psum_tensor("pn", [128, 512], F32) as pn):
                for hi in range(2):
                    for ti, src_ in enumerate([qrT, krT]):
                        for ch in range(SC):
                            def f(s, hi=hi, src_=src_, ch=ch):
                                C.wait(s, "s", "te")
                                C.wait(s, "s", "ve")
                                C.inc(s.activation(
                                    out=sqs[:, :],
                                    in_=src_[:, hi * S + ch * 512:hi * S + (ch + 1) * 512],
                                    func=AF.Square), "se")
                            on("scalar", f)

                            def f(t):
                                C.wait(t, "t", "se")
                                C.wait(t, "t", "ve")
                                C.inc(t.matmul(pn[:, :], ones128[:, :], sqs[:, :],
                                               start=True, stop=True), "te")
                            on("tensor", f)

                            def f(v, hi=hi, ti=ti, ch=ch):
                                C.wait(v, "v", "te")
                                o = (2 * hi + ti) * SC + ch
                                C.inc(v.tensor_reduce(out=nsc[:, o:o + 1], in_=pn[:, :],
                                                      axis=AXX, op=ALU.max), "ve")
                            on("vector", f)

                b = 4 * SC
                for hi in range(2):
                    def f(v, hi=hi, b=b):
                        C.wait(v, "v", "ve")
                        C.inc(v.tensor_reduce(out=nsc[:, b:b + 1],
                                              in_=nsc[:, 2 * hi * SC:(2 * hi + 1) * SC],
                                              axis=AXX, op=ALU.max), "ve")
                        C.inc(v.tensor_reduce(out=nsc[:, b + 1:b + 2],
                                              in_=nsc[:, (2 * hi + 1) * SC:(2 * hi + 2) * SC],
                                              axis=AXX, op=ALU.max), "ve")
                        C.wait(v, "v", "ve")
                        C.inc(v.tensor_tensor(out=nsc[:, b + 2 + hi:b + 3 + hi],
                                              in0=nsc[:, b:b + 1],
                                              in1=nsc[:, b + 1:b + 2],
                                              op=ALU.mult), "ve")
                    on("vector", f)

                def f(s, b=b):
                    C.wait(s, "s", "ve")
                    C.inc(s.activation(out=nsc[:, b + 4:b + 6],
                                       in_=nsc[:, b + 2:b + 4], func=AF.Sqrt), "se")
                on("scalar", f)

                def f(v, b=b):
                    C.wait(v, "v", "se")
                    C.inc(v.tensor_scalar(out=negM[:, 0:2],
                                          in0=nsc[:, b + 4:b + 6],
                                          scalar1=-INV_SQRT_DH, scalar2=None,
                                          op0=ALU.mult), "ve")
                on("vector", f)

            NEGM_TOK[0] = C.counts["ve"]

            with nc.sbuf_tensor("cbT", [128, 2 * S], BF16) as cbT:
                for hi in range(2):
                    attn_head(nc, C, on, cfg, hi, qrT, krT, vT, ident, gmix, negM, cbT)

                    def f(s, hi=hi):
                        C.wait(s, "sy", "ve")
                        C.dma(s, "sy", "cmi",
                              [(d_cmb_in[hi].ap()[:, :], cbT[:, hi * S:(hi + 1) * S])])
                    on("sync", f)
                    cmi_tok = C.counts["cmi"]

                    def f(g, hi=hi, cmi_tok=cmi_tok):
                        C.wait(g, "g", "cmi", cmi_tok)
                        C.inc(g.collective_compute(
                            "AllGather", ALU.bypass, replica_groups=[list(range(NCORES))],
                            ins=[d_cmb_in[hi].ap().opt()], outs=[d_cmb_out[hi].ap().opt()]), "c3")
                    on("gpsimd", f)

        with nc.sbuf_tensor("xlt2", [128, 16 * T], BF16) as xlt2:
            phase_barrier()
            phase_D(nc, C, on, cfg, ident, consts, d_cmb_out, d_woT, d_x, d_x1, xlt2)

            phase_barrier()
            phase_E(nc, C, on, cfg, d_wag_out, d_x1, d_y, xlt2)

        def f(s):
            for sn in ("st0", "st1", "st2", "st3"):
                C.wait(s, "sy", sn, C.counts[sn])
        on("sync", f)

    return nc


def attn_head(nc, C, on, cfg, hi, qrT, krT, vT, ident, gmix, negM, cbT):
    S, T, KT, SC = cfg.S, cfg.T, cfg.KT, cfg.SC
    base = hi * S
    with (nc.sbuf_tensor(f"va{hi}", [128, KT * 132], BF16) as vaug,
          nc.sbuf_tensor(f"mo{hi}", [128, KT * 130], BF16) as mo,
          nc.sbuf_tensor(f"sc{hi}", [128, 1024], F32) as sc,
          nc.sbuf_tensor(f"mem{hi}", [128, 132], BF16) as mem,
          nc.sbuf_tensor(f"pb{hi}", [128, 2 * 1024], BF16) as pbuf,
          nc.sbuf_tensor(f"pvs{hi}", [128, 4 * 132], F32) as pvs,
          nc.sbuf_tensor(f"rc{hi}", [128, 2], F32) as rcp):

        with (nc.sbuf_tensor(f"qf{hi}", [128, S], BF16) as qf,
              nc.sbuf_tensor(f"kfT{hi}", [128, S], BF16) as kfT,
              nc.sbuf_tensor(f"kn4{hi}", [128, 512], BF16) as kfn4,
              nc.psum_tensor(f"pm{hi}", [128, 132], F32) as pmm,
              nc.psum_tensor(f"pm2{hi}", [128, 132], F32) as pmm2,
              nc.psum_tensor(f"pt{hi}", [128, 512], BF16) as ptr,
              nc.psum_tensor(f"pt3{hi}", [128, 512], BF16) as ptrB,
              nc.psum_tensor(f"pt5{hi}", [128, 512], BF16) as ptrC,
              nc.psum_tensor(f"pt6{hi}", [128, 512], BF16) as ptrD):
            pairs = [(ptr, ptrB), (ptrC, ptrD)]
            pair_busy = [0, 0]
            pair_free = [0, 0]

            def f(v):
                C.wait(v, "v", "te")
                C.wait(v, "v", "se")
                C.inc(v.memset(vaug[:, 128:KT * 132:132], 1.0), "ve")
                C.inc(v.tensor_scalar(out=qf[:, :], in0=qrT[:, base:base + S],
                                      scalar1=0.0, scalar2=None, op0=ALU.min), "ve")
                C.inc(v.tensor_scalar(out=kfT[:, :], in0=krT[:, base:base + S],
                                      scalar1=0.0, scalar2=None, op0=ALU.min), "ve")
            on("vector", f)

            def f(s):
                C.wait(s, "s", "ve")
                C.inc(s.activation(out=qf[:, :], in_=qf[:, :], func=AF.Exp), "se")
                C.inc(s.activation(out=kfT[:, :], in_=kfT[:, :], func=AF.Exp), "se")
            on("scalar", f)

            def f(v):
                C.wait(v, "v", "se")
                C.inc(v.scalar_tensor_tensor(out=qf[:, :], in0=qrT[:, base:base + S],
                                             scalar=0.0, in1=qf[:, :],
                                             op0=ALU.max, op1=ALU.add), "ve")
                C.inc(v.scalar_tensor_tensor(out=kfT[:, :], in0=krT[:, base:base + S],
                                             scalar=0.0, in1=kfT[:, :],
                                             op0=ALU.max, op1=ALU.add), "ve")
            on("vector", f)
            kf_tok = C.counts["ve"]
            kn4_free = [0]

            for ch in range(SC):
                def f(t, ch=ch, kf_tok=kf_tok):
                    pA, pB = pairs[ch % 2]
                    C.wait(t, "t", "ve", kf_tok)
                    C.wait(t, "t", "ve", pair_free[ch % 2])
                    for q in range(4):
                        kt = ch * 4 + q
                        C.inc(t.transpose(pA[:, q * 128:(q + 1) * 128],
                                          vT[:, base + kt * 128:base + (kt + 1) * 128],
                                          ident[:, :]), "te")
                        C.inc(t.transpose(pB[:, q * 128:(q + 1) * 128],
                                          kfT[:, kt * 128:(kt + 1) * 128],
                                          ident[:, :]), "te")
                    pair_busy[ch % 2] = C.counts["te"]
                on("tensor", f)

                def f(v, ch=ch):
                    pA, pB = pairs[ch % 2]
                    C.wait(v, "v", "te", pair_busy[ch % 2])
                    C.wait(v, "v", "te", kn4_free[0])
                    for q in range(4):
                        kt = ch * 4 + q
                        C.inc(v.tensor_copy(out=vaug[:, kt * 132:kt * 132 + 128],
                                            in_=pA[:, q * 128:(q + 1) * 128]), "ve")
                    C.inc(v.tensor_copy(out=kfn4[:, :], in_=pB[:, :]), "ve")
                    pair_free[ch % 2] = C.counts["ve"]
                on("vector", f)

                def f(t, ch=ch):
                    C.wait(t, "t", "ve")
                    for q in range(4):
                        kt = ch * 4 + q
                        C.inc(t.matmul(pmm[:, 0:129], kfn4[:, q * 128:(q + 1) * 128],
                                       vaug[:, kt * 132:kt * 132 + 129],
                                       start=(kt == 0), stop=(kt == KT - 1)), "te")
                    kn4_free[0] = C.counts["te"]
                on("tensor", f)

            def f(v):
                C.wait(v, "v", "te")
                C.inc(v.tensor_copy(out=mem[:, 0:129], in_=pmm[:, 0:129]), "ve")
            on("vector", f)

            mo_busy = [0, 0]
            mo_free = [0, 0]
            with (nc.psum_tensor(f"po{hi}", [128, 396], F32) as pmo1,
                  nc.psum_tensor(f"po2{hi}", [128, 396], F32) as pmo2):
                pmos = [pmo1, pmo2]
                for g0 in range(0, KT, 3):
                    g2 = (g0 // 3) % 2
                    kts = tuple(range(g0, min(g0 + 3, KT)))

                    def f(t, g0=g0, g2=g2, kts=kts):
                        if g0 == 0:
                            C.wait(t, "t", "ve")
                        C.wait(t, "t", "ve", mo_free[g2])
                        for j, kt in enumerate(kts):
                            C.inc(t.matmul(pmos[g2][:, j * 132:j * 132 + 129],
                                           qf[:, kt * 128:(kt + 1) * 128],
                                           mem[:, 0:129], start=True, stop=True), "te")
                        mo_busy[g2] = C.counts["te"]
                    on("tensor", f)

                    def f(v, g2=g2, kts=kts):
                        C.wait(v, "v", "te", mo_busy[g2])
                        for j, kt in enumerate(kts):
                            C.inc(v.tensor_copy(out=mo[:, kt * 130:kt * 130 + 129],
                                                in_=pmos[g2][:, j * 132:j * 132 + 129]), "ve")
                        mo_free[g2] = C.counts["ve"]
                    on("vector", f)

        with nc.sbuf_tensor(f"cn{hi}", [128, S], BF16) as cbN:
            attn_softmax(nc, C, on, cfg, hi, qrT, krT, vT, ident, gmix, negM,
                         cbT, vaug, mo, sc, pbuf, rcp, cbN, pvs)


def attn_softmax(nc, C, on, cfg, hi, qrT, krT, vT, ident, gmix, negM,
                 cbT, vaug, mo, sc, pbuf, rcp, cbN, pvs):
    S, T, KT, SC = cfg.S, cfg.T, cfg.KT, cfg.SC
    base = hi * S
    if True:
        with (nc.psum_tensor(f"pSa{hi}", [128, 1024], F32) as pSa,
              nc.psum_tensor(f"pSb{hi}", [128, 1024], F32) as pSb,
              nc.psum_tensor(f"pv0{hi}", [128, 129], F32) as pv0,
              nc.psum_tensor(f"pv1{hi}", [128, 129], F32) as pv1,
              nc.psum_tensor(f"pv2{hi}", [128, 129], F32) as pv2,
              nc.psum_tensor(f"pv3{hi}", [128, 129], F32) as pv3):
            pS = [pSa, pSb]
            ppv = [pv0, pv1, pv2, pv3]
            ppv_free = [0]
            NG = KT // 2

            for qc in range(SC):
                TE0 = C.counts["te"]
                SE0 = C.counts["se"]

                def fT(t, qc=qc, TE0=TE0, SE0=SE0):
                    C.wait(t, "t", "ve", ppv_free[0])
                    if qc == 0:
                        C.wait(t, "t", "ve")
                    C.wait(t, "t", "se", SE0)

                    def smm(kg):
                        gs = kg % 2
                        if kg >= 2:
                            C.wait(t, "t", "se", SE0 + kg - 1)
                        for u in range(2):
                            kt = kg * 2 + u
                            C.inc(t.matmul(pS[gs][:, u * 512:(u + 1) * 512],
                                           krT[:, base + kt * 128:base + (kt + 1) * 128],
                                           qrT[:, base + qc * 512:base + (qc + 1) * 512],
                                           start=True, stop=True), "te")

                    def pv(kg):
                        gs = kg % 2
                        C.wait(t, "t", "se", SE0 + kg + 1)
                        for u in range(2):
                            kt = kg * 2 + u
                            for qs in range(4):
                                C.inc(t.matmul(
                                    ppv[qs][:, 0:129],
                                    pbuf[:, gs * 1024 + u * 512 + qs * 128:
                                         gs * 1024 + u * 512 + (qs + 1) * 128],
                                    vaug[:, kt * 132:kt * 132 + 129],
                                    start=(kt == 0), stop=(kt == KT - 1)), "te")

                    smm(0)
                    for kg in range(1, NG):
                        smm(kg)
                        pv(kg - 1)
                    pv(NG - 1)
                on("tensor", fT)

                def fS(s, hi=hi, TE0=TE0):
                    C.wait(s, "s", "ve", NEGM_TOK[0])
                    for kg in range(NG):
                        gs = kg % 2
                        C.wait(s, "s", "te", TE0 + (2 if kg == 0 else 10 * kg - 6))
                        C.inc(s.activation(out=pbuf[:, gs * 1024:(gs + 1) * 1024],
                                           in_=pS[gs][:, :], func=AF.Exp,
                                           bias=negM[:, hi:hi + 1], scale=INV_SQRT_DH), "se")
                on("scalar", fS)

                def f(v):
                    C.wait(v, "v", "te")
                    C.wait(v, "v", "ve")
                    for qs in range(4):
                        C.inc(v.tensor_copy(out=pvs[:, qs * 132:qs * 132 + 129],
                                            in_=ppv[qs][:, 0:129]), "ve")
                    ppv_free[0] = C.counts["ve"]
                on("vector", f)

                for qs in range(4):
                    tt = qc * 4 + qs

                    def f(v, hi=hi, qs=qs, tt=tt):
                        C.wait(v, "v", "ve")
                        C.inc(v.reciprocal(out=rcp[:, 0:1],
                                           in_=pvs[:, qs * 132 + 128:qs * 132 + 129]), "ve")
                        C.inc(v.reciprocal(out=rcp[:, 1:2],
                                           in_=mo[:, tt * 130 + 128:tt * 130 + 129]), "ve")
                        C.wait(v, "v", "ve")
                        C.inc(v.tensor_scalar(out=sc[:, 0:128],
                                              in0=pvs[:, qs * 132:qs * 132 + 128],
                                              scalar1=rcp[:, 0:1],
                                              scalar2=gmix[:, 2 * hi + 1:2 * hi + 2],
                                              op0=ALU.mult, op1=ALU.mult), "ve")
                        C.inc(v.tensor_scalar(out=sc[:, 128:256],
                                              in0=mo[:, tt * 130:tt * 130 + 128],
                                              scalar1=rcp[:, 1:2],
                                              scalar2=gmix[:, 2 * hi:2 * hi + 1],
                                              op0=ALU.mult, op1=ALU.mult), "ve")
                        C.wait(v, "v", "ve")
                        C.inc(v.tensor_tensor(out=cbN[:, tt * 128:(tt + 1) * 128],
                                              in0=sc[:, 0:128],
                                              in1=sc[:, 128:256], op=ALU.add), "ve")
                    on("vector", f)

        with (nc.psum_tensor(f"pt2{hi}", [128, 1024], BF16) as ptr2,
              nc.psum_tensor(f"pt4{hi}", [128, 1024], BF16) as ptr2b):
            p3 = [ptr2, ptr2b]
            s3_busy = [0, 0]
            s3_free = [0, 0]
            for g in range(KT // 8):
                def f(t, g=g):
                    if g == 0:
                        C.wait(t, "t", "ve")
                    C.wait(t, "t", "ve", s3_free[g % 2])
                    for j in range(8):
                        tt = g * 8 + j
                        C.inc(t.transpose(p3[g % 2][:, j * 128:(j + 1) * 128],
                                          cbN[:, tt * 128:(tt + 1) * 128],
                                          ident[:, :]), "te")
                    s3_busy[g % 2] = C.counts["te"]
                on("tensor", f)

                def f(v, hi=hi, g=g):
                    C.wait(v, "v", "te", s3_busy[g % 2])
                    C.inc(v.tensor_copy(out=cbT[:, hi * S + g * 1024:hi * S + (g + 1) * 1024],
                                        in_=p3[g % 2][:, :]), "ve")
                    s3_free[g % 2] = C.counts["ve"]
                on("vector", f)


def phase_D(nc, C, on, cfg, ident, consts, d_cmb_out, d_woT, d_x, d_x1, xlt2):
    S, T, TT = cfg.S, cfg.T, cfg.TT
    with (nc.sbuf_tensor("cstT", [128, 16 * T], BF16) as cstT,
          nc.sbuf_tensor("swo", [128, 16 * H], BF16) as swo,
          nc.sbuf_tensor("xa2", [128, 2 * H], F32) as xa2,
          nc.sbuf_tensor("x1t", [128, H], F32) as x1t,
          nc.sbuf_tensor("xln2", [128, H], BF16) as xln2,
          nc.sbuf_tensor("st6b", [128, 24], F32) as st6,
          nc.sbuf_tensor("mrs2", [128, 4], F32) as mrs,
          nc.psum_tensor("pwo", [128, 2048], F32) as pwo,
          nc.psum_tensor("ptE", [128, 1024], BF16) as ptE,
          nc.psum_tensor("ptF", [128, 1024], BF16) as ptF):
        ptEF = [ptE, ptF]

        def f(g):
            C.dma(g, "g", "swo",
                  [(swo[:, :].rearrange("p (a b) -> p a b", a=16),
                    d_woT.rearrange("(a p) b -> p a b", p=128))])
            pid = nc.partition_id(engines=[g.engine])
            off = pid * T
            C.wait(g, "g", "c3", 1)
            C.dma(g, "g", "cst",
                  [(cstT[:, r * T:(r + 1) * T],
                    d_cmb_out[0].ap()[r * 128:(r + 1) * 128, bass.ds(off, T)])
                   for r in range(NCORES)])
            C.wait(g, "g", "c3", 2)
            C.dma(g, "g", "cst2",
                  [(cstT[:, (r + 8) * T:(r + 9) * T],
                    d_cmb_out[1].ap()[r * 128:(r + 1) * 128, bass.ds(off, T)])
                   for r in range(NCORES)])
        on("gpsimd", f)
        swo_tok = C.counts["swo"]
        cst_tok = C.counts["cst"]
        cst2_tok = C.counts["cst2"]

        xa_tok = [0, 0]
        xa_free = [0, 0]
        pt_busy = [0, 0]
        pt_free = [0, 0]
        pwo_free = [0]
        x1sp_tok = [0]

        for i in range(TT):
            sl = i % 2

            def f(s, i=i, sl=sl):
                C.wait(s, "sy", "ve", xa_free[sl])
                xa_tok[sl] = C.dma(s, "sy", f"xa{sl}",
                                   [(xa2[:, sl * H:(sl + 1) * H],
                                     d_x[i * 128:(i + 1) * 128, :])])
            on("sync", f)

            def f(t, i=i):
                C.wait(t, "t", "swo", swo_tok)
                C.wait(t, "t", "cst", cst_tok)
                C.wait(t, "t", "ve", pwo_free[0])
                for dt in range(16):
                    if dt == 8:
                        C.wait(t, "t", "cst2", cst2_tok)
                    lhs = cstT[:, dt * T + i * 128:dt * T + (i + 1) * 128]
                    for ec in range(4):
                        C.inc(t.matmul(pwo[:, ec * 512:(ec + 1) * 512], lhs,
                                       swo[:, dt * H + ec * 512:dt * H + (ec + 1) * 512],
                                       start=(dt == 0), stop=(dt == 15)), "te")
            on("tensor", f)

            def f(v, sl=sl):
                C.wait(v, "v", "te")
                C.wait(v, "v", f"xa{sl}", xa_tok[sl])
                C.wait(v, "v", "x1s", x1sp_tok[0])
                C.wait(v, "v", "ve")
                C.wait(v, "v", "se")
                C.inc(v.tensor_tensor(out=x1t[:, :], in0=xa2[:, sl * H:(sl + 1) * H],
                                      in1=pwo[:, :], op=ALU.add), "ve")
                xa_free[sl] = C.counts["ve"]
                pwo_free[0] = C.counts["ve"]
                C.wait(v, "v", "ve")
                for c4 in range(4):
                    C.inc(v.bn_stats(out=st6[:, 6 * c4:6 * c4 + 6],
                                     in_=x1t[:, 512 * c4:512 * (c4 + 1)]), "ve")
                C.wait(v, "v", "ve")
                C.inc(v.bn_aggr(out=mrs[:, 0:2], in_=st6[:, :]), "ve")
            on("vector", f)
            x1_tok = C.counts["ve"] - 2

            def f(s, i=i, x1_tok=x1_tok):
                C.wait(s, "sy", "ve", x1_tok)
                x1sp_tok[0] = C.dma(s, "sy", "x1s",
                                    [(d_x1.ap()[i * 128:(i + 1) * 128, :], x1t[:, :])])
            on("sync", f)

            def f(s):
                C.wait(s, "s", "ve")
                C.inc(s.activation(out=mrs[:, 2:3], in_=mrs[:, 1:2], func=AF.Sqrt,
                                   bias=consts[:, 2:3], scale=1.0), "se")
            on("scalar", f)

            def f(v):
                C.wait(v, "v", "se")
                C.wait(v, "v", "te")
                C.inc(v.reciprocal(out=mrs[:, 3:4], in_=mrs[:, 2:3]), "ve")
                C.wait(v, "v", "ve")
                C.inc(v.tensor_scalar(out=xln2[:, :], in0=x1t[:, :],
                                      scalar1=mrs[:, 0:1], scalar2=mrs[:, 3:4],
                                      op0=ALU.subtract, op1=ALU.mult), "ve")
            on("vector", f)
            ln_tok = C.counts["ve"]

            for g8 in range(2):
                def f(t, g8=g8, ln_tok=ln_tok):
                    C.wait(t, "t", "ve", ln_tok)
                    C.wait(t, "t", "ve", pt_free[g8])
                    for j8 in range(8):
                        j = g8 * 8 + j8
                        C.inc(t.transpose(ptEF[g8][:, j8 * 128:(j8 + 1) * 128],
                                          xln2[:, j * 128:(j + 1) * 128],
                                          ident[:, :]), "te")
                    pt_busy[g8] = C.counts["te"]
                on("tensor", f)

                def f(v, i=i, g8=g8):
                    C.wait(v, "v", "te", pt_busy[g8])
                    for j8 in range(8):
                        j = g8 * 8 + j8
                        C.inc(v.tensor_copy(out=xlt2[:, j * T + i * 128:j * T + (i + 1) * 128],
                                            in_=ptEF[g8][:, j8 * 128:(j8 + 1) * 128]), "ve")
                    pt_free[g8] = C.counts["ve"]
                on("vector", f)


def phase_E(nc, C, on, cfg, d_wag_out, d_x1, d_y, xlt2):
    S, T, MLP, NTL, NTG, T2 = cfg.S, cfg.T, cfg.MLP, cfg.NTL, cfg.NTG, cfg.T2
    nblk = 3 * NTL
    TL2 = T2 // 128
    with (nc.sbuf_tensor("wgu", [128, 6 * H], BF16) as wgu,
          nc.sbuf_tensor("geb", [128, 512], BF16) as geb,
          nc.sbuf_tensor("upb", [128, 512], BF16) as upb,
          nc.sbuf_tensor("it", [128, NTG * T2], BF16) as it,
          nc.sbuf_tensor("wdb", [128, 4 * 512], BF16) as wdb,
          nc.sbuf_tensor("xr", [128, 4 * 512], F32) as xr,
          nc.sbuf_tensor("yo", [128, 4 * 512], F32) as yo):

        def rows_of(ntg, m):
            r, b = ntg // NTL, ntg % NTL
            return 128 * (nblk * r + m * NTL + b)

        wd_tok = [0, 0, 0, 0]
        wd_free = [0, 0, 0, 0]
        xr_tok = [0, 0, 0, 0]
        xr_free = [0, 0, 0, 0]
        for th in range(cfg.ffn_tsplit):
            t0 = th * T2
            wgu_tok = [0, 0, 0]
            wgu_tok2 = [0, 0, 0]
            wgu_free = [C.counts["te"]] * 3
            with (nc.psum_tensor(f"pg{th}", [128, 512], F32) as pg,
                  nc.psum_tensor(f"pu{th}", [128, 512], F32) as pu,
                  nc.psum_tensor(f"pg2{th}", [128, 512], F32) as pg2,
                  nc.psum_tensor(f"pu2{th}", [128, 512], F32) as pu2):
                pgs = [(pg, pu), (pg2, pu2)]
                pp_free = [C.counts["ve"], C.counts["ve"]]
                TE0e = C.counts["te"]
                VE0e = C.counts["ve"]
                SE0e = C.counts["se"]

                def fSg(s, TE0e=TE0e, VE0e=VE0e):
                    for ntg in range(NTG):
                        sl = ntg % 2
                        C.wait(s, "s", "te", TE0e + 32 * (ntg + 1))
                        if ntg >= 1:
                            C.wait(s, "s", "ve", VE0e + 2 * ntg)
                        C.inc(s.activation(out=geb[:, 0:T2], in_=pgs[sl][0][:, 0:T2],
                                           func=AF.Gelu_apprx_tanh), "se")
                on("scalar", fSg)

                for ntg in range(NTG):
                    sl = ntg % 3
                    ps_ = ntg % 2

                    def f(s, ntg=ntg, sl=sl):
                        C.wait(s, "sy", "c2", 1)
                        C.wait(s, "sy", "te", wgu_free[sl])
                        rg = rows_of(ntg, 0)
                        wgu_tok[sl] = C.dma(s, "sy", f"wg{sl}",
                            [(wgu[:, sl * 2 * H:sl * 2 * H + H],
                              d_wag_out.ap()[rg:rg + 128, :])])
                    on("sync", f)

                    def f(g, ntg=ntg, sl=sl):
                        C.wait(g, "g", "c2", 1)
                        C.wait(g, "g", "te", wgu_free[sl])
                        ru = rows_of(ntg, 1)
                        wgu_tok2[sl] = C.dma(g, "g", f"wu{sl}",
                            [(wgu[:, sl * 2 * H + H:(sl + 1) * 2 * H],
                              d_wag_out.ap()[ru:ru + 128, :])])
                    on("gpsimd", f)

                    def f(t, ntg=ntg, sl=sl, ps_=ps_, t0=t0):
                        C.wait(t, "t", f"wg{sl}", wgu_tok[sl])
                        C.wait(t, "t", f"wu{sl}", wgu_tok2[sl])
                        C.wait(t, "t", "ve", pp_free[ps_])
                        for j in range(16):
                            rhs = xlt2[:, j * T + t0:j * T + t0 + T2]
                            C.inc(t.matmul(
                                pgs[ps_][0][:, 0:T2],
                                wgu[:, sl * 2 * H + j * 128:sl * 2 * H + (j + 1) * 128],
                                rhs, start=(j == 0), stop=(j == 15)), "te")
                            C.inc(t.matmul(
                                pgs[ps_][1][:, 0:T2],
                                wgu[:, sl * 2 * H + H + j * 128:sl * 2 * H + H + (j + 1) * 128],
                                rhs, start=(j == 0), stop=(j == 15)), "te")
                        wgu_free[sl] = C.counts["te"]
                    on("tensor", f)
                    mm_tok = C.counts["te"]

                    def f(v, ntg=ntg, ps_=ps_, mm_tok=mm_tok, SE0e=SE0e):
                        C.wait(v, "v", "te", mm_tok)
                        C.wait(v, "v", "ve")
                        C.inc(v.tensor_copy(out=upb[:, 0:T2], in_=pgs[ps_][1][:, 0:T2]), "ve")
                        C.wait(v, "v", "se", SE0e + ntg + 1)
                        C.wait(v, "v", "ve")
                        C.inc(v.tensor_tensor(out=it[:, ntg * T2:(ntg + 1) * T2],
                                              in0=geb[:, 0:T2], in1=upb[:, 0:T2],
                                              op=ALU.mult), "ve")
                        pp_free[ps_] = C.counts["ve"]
                    on("vector", f)

            it_tok = C.counts["ve"]

            with (nc.psum_tensor(f"pd{th}", [128, TL2 * 512], F32) as pd_,
                  nc.psum_tensor(f"pe{th}", [128, TL2 * 512], F32) as pe_):
                pd_pair = [pd_, pe_]
                pd_free = [C.counts["ve"], C.counts["ve"]]
                se_snap = C.counts["se"]
                for hc in range(4):
                    pd = pd_pair[hc % 2]

                    def f(t, hc=hc, se_snap=se_snap):
                        C.wait(t, "t", "se", se_snap)
                        C.wait(t, "t", "ve", pd_free[hc % 2])
                    on("tensor", f)

                    for tt2 in range(TL2):
                        def f(s, th=th, hc=hc, tt2=tt2):
                            C.wait(s, "sc", "ve", xr_free[tt2])
                            C.wait(s, "sc", "x1s", C.counts["x1s"])
                            row = th * T2 + tt2 * 128
                            xr_tok[tt2] = C.dma(s, "sc", f"xr{tt2}",
                                [(xr[:, tt2 * 512:(tt2 + 1) * 512],
                                  d_x1.ap()[row:row + 128, hc * 512:(hc + 1) * 512])])
                        on("scalar", f)

                    for ntg in range(NTG):
                        sl = ntg % 4
                        qeng, qkey = (("sync", "sy") if sl % 2 == 0
                                      else ("scalar", "sc"))

                        def f(s, ntg=ntg, sl=sl, hc=hc, qkey=qkey):
                            C.wait(s, qkey, "te", wd_free[sl])
                            rd = rows_of(ntg, 2)
                            wd_tok[sl] = C.dma(s, qkey, f"wd{sl}",
                                [(wdb[:, sl * 512:(sl + 1) * 512],
                                  d_wag_out.ap()[rd:rd + 128, hc * 512:(hc + 1) * 512])])
                        on(qeng, f)

                        def f(t, ntg=ntg, sl=sl, it_tok=it_tok):
                            C.wait(t, "t", f"wd{sl}", wd_tok[sl])
                            C.wait(t, "t", "ve", it_tok)
                            for tt2 in range(TL2):
                                C.inc(t.matmul(
                                    pd[:, tt2 * 512:(tt2 + 1) * 512],
                                    it[:, ntg * T2 + tt2 * 128:ntg * T2 + (tt2 + 1) * 128],
                                    wdb[:, sl * 512:(sl + 1) * 512],
                                    start=(ntg == 0), stop=(ntg == NTG - 1)), "te")
                            wd_free[sl] = C.counts["te"]
                        on("tensor", f)

                    for tt2 in range(TL2):
                        def f(v, tt2=tt2, pd=pd):
                            C.wait(v, "v", "te")
                            C.wait(v, "v", f"xr{tt2}", xr_tok[tt2])
                            C.wait(v, "v", f"st{tt2}", C.counts[f"st{tt2}"])
                            C.inc(v.tensor_tensor(out=yo[:, tt2 * 512:(tt2 + 1) * 512],
                                                  in0=xr[:, tt2 * 512:(tt2 + 1) * 512],
                                                  in1=pd[:, tt2 * 512:(tt2 + 1) * 512],
                                                  op=ALU.add), "ve")
                            xr_free[tt2] = C.counts["ve"]
                        on("vector", f)
                        yo_tok = C.counts["ve"]

                        def f(s, th=th, hc=hc, tt2=tt2, yo_tok=yo_tok):
                            C.wait(s, "sy", "ve", yo_tok)
                            row = th * T2 + tt2 * 128
                            C.dma(s, "sy", f"st{tt2}",
                                  [(d_y[row:row + 128, hc * 512:(hc + 1) * 512],
                                    yo[:, tt2 * 512:(tt2 + 1) * 512])])
                        on("sync", f)
                    pd_free[hc % 2] = C.counts["ve"]



def _host_layouts(inputs, cfg: Cfg):
    x = np.ascontiguousarray(np.asarray(inputs["L0_attn_hidden_states"],
                                        dtype=np.float32).reshape(cfg.S, H))
    wqT = np.ascontiguousarray(np.asarray(inputs["L0_attn_Wq"], dtype=np.float32).T)
    wkT = np.ascontiguousarray(np.asarray(inputs["L0_attn_Wk"], dtype=np.float32).T)
    wvT = np.ascontiguousarray(np.asarray(inputs["L0_attn_Wv"], dtype=np.float32).T)
    wo = np.asarray(inputs["L0_attn_Wo"], dtype=np.float32)
    gate = np.asarray(inputs["L0_attn_gate"], dtype=np.float32).reshape(NH)

    woT = np.ascontiguousarray(wo.T)

    inv_freq = 1.0 / 10000.0 ** (np.arange(0, H, 2, dtype=np.float64) / H)
    t = np.arange(cfg.S, dtype=np.float64)
    ang = t[None, :] * inv_freq[:, None]
    cosT = np.cos(ang).astype(np.float32)
    sinT = np.sin(ang).astype(np.float32)

    def codes_u8(a):
        a = np.ascontiguousarray(np.asarray(a, dtype=np.int32))
        return np.ascontiguousarray(a.view(np.uint8)[:, ::4])

    gcod = codes_u8(inputs["L0_ffn_gate_W_packed"])
    ucod = codes_u8(inputs["L0_ffn_up_W_packed"])
    dcod = codes_u8(inputs["L0_ffn_down_W_packed"])
    gsc = np.ascontiguousarray(np.asarray(inputs["L0_ffn_gate_scales"], dtype=np.float32))
    usc = np.ascontiguousarray(np.asarray(inputs["L0_ffn_up_scales"], dtype=np.float32))
    dsc = np.ascontiguousarray(np.asarray(inputs["L0_ffn_down_scales"], dtype=np.float32))

    NPC = cfg.MLP // NCORES
    in_maps = []
    for c in range(NCORES):
        dsl = slice(c * DH, (c + 1) * DH)
        dsl2 = slice(H // 2 + c * DH, H // 2 + (c + 1) * DH)
        in_maps.append({
            "x": np.ascontiguousarray(x[c * cfg.T:(c + 1) * cfg.T]),
            "wqT": np.ascontiguousarray(np.concatenate([wqT[:, dsl], wqT[:, dsl2]], axis=1)),
            "wkT": np.ascontiguousarray(np.concatenate([wkT[:, dsl], wkT[:, dsl2]], axis=1)),
            "wvT": np.ascontiguousarray(np.concatenate([wvT[:, dsl], wvT[:, dsl2]], axis=1)),
            "woT": woT,
            "cosT": np.ascontiguousarray(cosT[c * DH:(c + 1) * DH]),
            "sinT": np.ascontiguousarray(sinT[c * DH:(c + 1) * DH]),
            "gate2": np.ascontiguousarray(np.broadcast_to(gate[[c, c + 8]].reshape(1, 2), (128, 2))),
            "gcod": np.ascontiguousarray(gcod[c * NPC:(c + 1) * NPC]),
            "ucod": np.ascontiguousarray(ucod[c * NPC:(c + 1) * NPC]),
            "dcod": np.ascontiguousarray(dcod[:, c * (NPC // 2):(c + 1) * (NPC // 2)]),
            "gsc": np.ascontiguousarray(gsc[c * NPC:(c + 1) * NPC]),
            "usc": np.ascontiguousarray(usc[c * NPC:(c + 1) * NPC]),
            "dsc": np.ascontiguousarray(dsc[:, c * (NPC // 32):(c + 1) * (NPC // 32)]),
        })
    return in_maps


def kernel(**inputs):
    cfg = Cfg()
    in_maps = _host_layouts(inputs, cfg)
    nc = build_program(cfg)
    res = run_bass_kernel_spmd(nc, in_maps, core_ids=list(range(NCORES)))
    y = np.concatenate([np.asarray(r["y"]) for r in res.results], axis=0)
    return np.ascontiguousarray(y.reshape(1, cfg.S, H), dtype=np.float32)


# version 26
# speedup vs baseline: 1.0081x; 1.0081x over previous
import numpy as np
from contextlib import ExitStack

import concourse.bass as bass
import concourse.mybir as mybir
from concourse.bass_utils import run_bass_kernel_spmd

F32 = mybir.dt.float32
F16 = mybir.dt.float16
BF16 = mybir.dt.bfloat16
U8 = mybir.dt.uint8
AF = mybir.ActivationFunctionType
ALU = mybir.AluOpType
AXX = mybir.AxisListType.X

NCORES = 8
NEGM_TOK = [0]
NH = 16
DH = 128
H = NH * DH
INV_SQRT_DH = 1.0 / float(np.sqrt(DH))


class Cfg:
    def __init__(self, S=8192, MLP=16384, ffn_tsplit=2):
        self.S, self.MLP = S, MLP
        self.T = S // NCORES
        self.TT = self.T // 128
        self.SC = S // 512
        self.KT = S // 128
        self.BCH = min(512, self.T)
        self.SCB = S // self.BCH
        self.NTL = MLP // 128 // NCORES
        self.NTG = MLP // 128
        self.ffn_tsplit = ffn_tsplit
        self.T2 = self.T // ffn_tsplit
        assert self.T % 128 == 0 and S % 512 == 0 and self.T2 % 128 == 0
        assert self.T2 <= 512 and MLP % (128 * NCORES) == 0
        assert (2 * self.KT) % 4 == 0


def build_program(cfg: Cfg):
    S, T, MLP, TT, SC, KT = cfg.S, cfg.T, cfg.MLP, cfg.TT, cfg.SC, cfg.KT
    NTL, NTG, T2, BCH, SCB = cfg.NTL, cfg.NTG, cfg.T2, cfg.BCH, cfg.SCB
    nc = bass.Bass("TRN2", target_bir_lowering=False)

    d_x = nc.dram_tensor("x", [T, H], F32, kind="ExternalInput").ap()
    d_wqT = nc.dram_tensor("wqT", [H, 256], F32, kind="ExternalInput").ap()
    d_wkT = nc.dram_tensor("wkT", [H, 256], F32, kind="ExternalInput").ap()
    d_wvT = nc.dram_tensor("wvT", [H, 256], F32, kind="ExternalInput").ap()
    d_woT = nc.dram_tensor("woT", [H, H], F32, kind="ExternalInput").ap()
    d_cos = nc.dram_tensor("cosT", [DH, S], F32, kind="ExternalInput").ap()
    d_sin = nc.dram_tensor("sinT", [DH, S], F32, kind="ExternalInput").ap()
    d_gate = nc.dram_tensor("gate2", [128, 2], F32, kind="ExternalInput").ap()
    d_gcod = nc.dram_tensor("gcod", [MLP // NCORES, H // 2], U8, kind="ExternalInput").ap()
    d_ucod = nc.dram_tensor("ucod", [MLP // NCORES, H // 2], U8, kind="ExternalInput").ap()
    d_dcod = nc.dram_tensor("dcod", [H, MLP // 2 // NCORES], U8, kind="ExternalInput").ap()
    d_gsc = nc.dram_tensor("gsc", [MLP // NCORES, H // 32], F32, kind="ExternalInput").ap()
    d_usc = nc.dram_tensor("usc", [MLP // NCORES, H // 32], F32, kind="ExternalInput").ap()
    d_dsc = nc.dram_tensor("dsc", [H, MLP // 32 // NCORES], F32, kind="ExternalInput").ap()
    d_y = nc.dram_tensor("y", [T, H], F32, kind="ExternalOutput").ap()

    AGH = 2 if T // 2 >= BCH else 1
    d_agx_in = [nc.dram_tensor(f"agx_in{h}", [H, T // AGH], BF16) for h in range(AGH)]
    d_agx_out = [nc.dram_tensor(f"agx_out{h}", [NCORES * H, T // AGH], BF16, addr_space="Shared")
                 for h in range(AGH)]
    nblk = 3 * NTL
    d_wag_in = nc.dram_tensor("wag_in", [nblk * 128, H], BF16)
    d_wag_out = nc.dram_tensor("wag_out", [NCORES * nblk * 128, H], BF16, addr_space="Shared")
    d_cmb_in = [nc.dram_tensor(f"cmbi{h}", [128, S], BF16) for h in range(2)]
    d_cmb_out = [nc.dram_tensor(f"cmbo{h}", [NCORES * 128, S], BF16, addr_space="Shared") for h in range(2)]
    d_x1 = nc.dram_tensor("x1d", [T, H], F32)

    ctx = ExitStack()

    class Sems:
        def __init__(self):
            self.counts, self.marks, self.sems = {}, {}, {}

        def new(self, name):
            self.sems[name] = ctx.enter_context(nc.semaphore(name))
            self.counts[name] = 0

        def inc(self, instr, sname, n=1):
            instr.then_inc(self.sems[sname], n)
            self.counts[sname] += n
            return self.counts[sname]

        def dma(self, eng, ename, sname, pairs):
            self.wait(eng, ename, sname, self.counts[sname])
            for (o, i_) in pairs:
                self.inc(eng.dma_start(out=o, in_=i_), sname, 16)
            return self.counts[sname]

        def wait(self, eng, ename, sname, val=None):
            if val is None:
                val = self.counts[sname]
            if val <= 0:
                return
            key = (ename, sname)
            if self.marks.get(key, 0) >= val:
                return
            eng.wait_ge(self.sems[sname], val)
            self.marks[key] = val

    C = Sems()

    def SB(name, shape, dt=F32):
        return ctx.enter_context(nc.sbuf_tensor(name, shape, dt))

    with ctx:
        for sn in ["te", "ve", "se", "ge", "c1", "c2", "rca", "rcl",
                   "sg", "swl", "agx", "cod", "wao", "xa0", "xa1", "xs0", "xs1",
                   "cs0", "cs1", "wg0", "wg1", "wd0", "wd1", "xr0", "xr1",
                   "cst", "swo", "x1s", "st0", "st1", "cmi", "c3", "wdo", "wdo2", "cod2", "wg2",
                   "wd2", "wd3", "cst2", "xb0", "xb1", "xt0", "xt1",
                   "wu0", "wu1", "wu2", "xr2", "xr3", "st2", "st3", "cst3"]:
            C.new(sn)

        block = ctx.enter_context(nc.Block())

        def on(eng, fn):
            getattr(block, eng)(fn)

        _BARRIER_SKIP = {"c1", "c2", "c3", "rca", "rcl"}

        def phase_barrier():
            snap = {k: v for k, v in C.counts.items() if k not in _BARRIER_SKIP}
            for eng, key in [("sync", "sy"), ("scalar", "s"),
                             ("vector", "v"), ("tensor", "t"), ("gpsimd", "g")]:
                def f(e, key=key, snap=snap):
                    for sn, val in snap.items():
                        C.wait(e, key, sn, val)
                on(eng, f)

        consts = SB("consts", [128, 4], F32)
        ident = SB("ident", [128, 128], BF16)
        iot = SB("iot", [128, 128], F32)
        pidx = SB("pidx", [128, 1], F32)
        ones128 = SB("ones128", [128, 128], BF16)
        s_gate = SB("s_gate", [128, 2], F32)
        gmix = SB("gmix", [128, 4], F32)
        negM = SB("negM", [128, 2], F32)
        nsc = SB("nsc", [128, 4 * SC + 6], F32)

        def f(s):
            C.dma(s, "sy", "sg", [(s_gate[:, :], d_gate[:, :])])
        on("sync", f)

        def f(g):
            C.inc(g.iota(iot[:, :], [[1, 128]], channel_multiplier=0,
                         allow_small_or_imprecise_dtypes=True), "ge")
            C.inc(g.iota(pidx[:, :], [[0, 1]], channel_multiplier=1,
                         allow_small_or_imprecise_dtypes=True), "ge")
        on("gpsimd", f)

        def f(v):
            C.inc(v.memset(consts[:, 0:1], -4.0), "ve")
            C.inc(v.memset(consts[:, 1:2], -6.0), "ve")
            C.inc(v.memset(consts[:, 2:3], 1e-5), "ve")
            C.inc(v.memset(ones128[:, :], 1.0), "ve")
            C.wait(v, "v", "ge", 2)
            C.inc(v.tensor_scalar(out=ident[:, :], in0=iot[:, :], scalar1=pidx[:, 0:1],
                                  scalar2=None, op0=ALU.is_equal), "ve")
        on("vector", f)

        def f(s):
            C.wait(s, "s", "sg", 16)
            C.inc(s.activation(out=gmix[:, 0:1], in_=s_gate[:, 0:1], func=AF.Sigmoid), "se")
            C.inc(s.activation(out=gmix[:, 2:3], in_=s_gate[:, 1:2], func=AF.Sigmoid), "se")
        on("scalar", f)

        def f(v):
            C.wait(v, "v", "se")
            C.inc(v.tensor_scalar(out=gmix[:, 1:2], in0=gmix[:, 0:1], scalar1=-1.0,
                                  scalar2=1.0, op0=ALU.mult, op1=ALU.add), "ve")
            C.inc(v.tensor_scalar(out=gmix[:, 3:4], in0=gmix[:, 2:3], scalar1=-1.0,
                                  scalar2=1.0, op0=ALU.mult, op1=ALU.add), "ve")
        on("vector", f)

        phase_barrier()
        with (nc.sbuf_tensor("qrT", [128, 2 * S], BF16) as qrT,
              nc.sbuf_tensor("krT", [128, 2 * S], BF16) as krT,
              nc.sbuf_tensor("vT", [128, 2 * S], BF16) as vT):

            with ExitStack() as bctx:
                _e = bctx.enter_context
                cod = _e(nc.sbuf_tensor("cod", [128, 2 * 1024], U8))
                nib = _e(nc.sbuf_tensor("nib", [128, 2048], U8))
                fub = _e(nc.sbuf_tensor("fub", [128, 2048], BF16))
                g8b = _e(nc.sbuf_tensor("g8b", [128, 2048], BF16))
                mtb = _e(nc.sbuf_tensor("mtb", [128, 2048], BF16))
                r1b = _e(nc.sbuf_tensor("r1b", [128, 2048], BF16))
                scl = _e(nc.sbuf_tensor("scl", [128, 2 * 64], F32))
                wn = _e(nc.sbuf_tensor("wn", [128, 2048], BF16))
                wasm = _e(nc.sbuf_tensor("wasm", [128, H], BF16))
                wstg = _e(nc.sbuf_tensor("wstg", [128, 2 * 1024], BF16))
                ptC = _e(nc.psum_tensor("ptC", [128, 1024], BF16))
                ptD = _e(nc.psum_tensor("ptD", [128, 1024], BF16))

                ptCD = [ptC, ptD]
                ptw_busy = [0, 0]
                ptw_free = [0, 0]
                cod_sem = ["cod", "cod2"]
                cod_free = [0, 0]
                cod_tok = [0, 0]
                stg_free = [0, 0]
                wd_sem = ["wdo", "wdo2"]

                def cod_load(idx, src_cod, src_scl, kb):
                    sl2 = idx % 2

                    def f(s, sl2=sl2, src_cod=src_cod, src_scl=src_scl, kb=kb):
                        C.wait(s, "sc", "ve", cod_free[sl2])
                        C.wait(s, "sc", "ve", gcod_free[sl2])
                        C.dma(s, "sc", cod_sem[sl2],
                              [(cod[:, sl2 * 1024:sl2 * 1024 + kb], src_cod),
                               (scl[:, sl2 * 64:sl2 * 64 + kb // 16], src_scl)])
                        cod_tok[sl2] = C.counts[cod_sem[sl2]]
                    on("scalar", f)

                gcod_free = [0, 0]

                def dq_block(idx, kb, wn_war=0):
                    sl2 = idx % 2
                    K2 = 2 * kb

                    def f(v, sl2=sl2, kb=kb, K2=K2, wn_war=wn_war):
                        C.wait(v, "v", cod_sem[sl2], cod_tok[sl2])
                        C.wait(v, "v", "ve")
                        cslot = cod[:, sl2 * 1024:sl2 * 1024 + kb]
                        C.inc(v.tensor_scalar(out=nib[:, 0:kb], in0=cslot,
                                              scalar1=15, scalar2=None,
                                              op0=ALU.bitwise_and), "ve")
                        C.inc(v.tensor_scalar(out=nib[:, kb:K2], in0=cslot,
                                              scalar1=4, scalar2=None,
                                              op0=ALU.logical_shift_right), "ve")
                        gcod_free[sl2] = C.counts["ve"]
                        C.wait(v, "v", "ve")
                        C.inc(v.tensor_scalar(out=g8b[:, 0:K2], in0=nib[:, 0:K2],
                                              scalar1=8.0,
                                              scalar2=None, op0=ALU.is_ge), "ve")
                        C.wait(v, "v", "ve")
                        C.inc(v.scalar_tensor_tensor(
                            out=mtb[:, 0:K2], in0=g8b[:, 0:K2],
                            scalar=-8.0, in1=nib[:, 0:K2],
                            op0=ALU.mult, op1=ALU.add), "ve")
                        C.wait(v, "v", "ve")
                        C.inc(v.tensor_scalar(out=g8b[:, 0:K2], in0=g8b[:, 0:K2],
                                              scalar1=-2.0, scalar2=1.0,
                                              op0=ALU.mult, op1=ALU.add), "ve")
                        C.inc(v.tensor_scalar(out=r1b[:, 0:K2], in0=mtb[:, 0:K2],
                                              scalar1=-4.0, scalar2=0.0,
                                              op0=ALU.add, op1=ALU.max), "ve")
                        C.wait(v, "v", "ve")
                        C.inc(v.tensor_tensor(out=fub[:, 0:K2], in0=mtb[:, 0:K2],
                                              in1=r1b[:, 0:K2], op=ALU.add), "ve")
                        C.wait(v, "v", "ve")
                        C.inc(v.tensor_scalar(out=r1b[:, 0:K2], in0=mtb[:, 0:K2],
                                              scalar1=-6.0, scalar2=0.0,
                                              op0=ALU.add, op1=ALU.max), "ve")
                        C.wait(v, "v", "ve")
                        C.inc(v.scalar_tensor_tensor(out=fub[:, 0:K2], in0=fub[:, 0:K2],
                                                     scalar=0.5, in1=r1b[:, 0:K2],
                                                     op0=ALU.mult, op1=ALU.add), "ve")
                        C.wait(v, "v", "ve")
                        C.inc(v.tensor_tensor(out=fub[:, 0:K2], in0=fub[:, 0:K2],
                                              in1=g8b[:, 0:K2], op=ALU.mult), "ve")
                        C.wait(v, "v", "ve")
                        C.wait(v, "v", "te", wn_war)
                        sb = scl[:, sl2 * 64:sl2 * 64 + kb // 16].broadcast_to((128, kb // 16, 16))
                        C.inc(v.scalar_tensor_tensor(
                            out=wn[:, 0:K2:2].rearrange("p (b r) -> p b r", r=16),
                            in0=fub[:, 0:kb].rearrange("p (b r) -> p b r", r=16),
                            scalar=1.0, in1=sb, op0=ALU.mult, op1=ALU.mult), "ve")
                        C.inc(v.scalar_tensor_tensor(
                            out=wn[:, 1:K2:2].rearrange("p (b r) -> p b r", r=16),
                            in0=fub[:, kb:K2].rearrange("p (b r) -> p b r", r=16),
                            scalar=1.0, in1=sb, op0=ALU.mult, op1=ALU.mult), "ve")
                        cod_free[sl2] = C.counts["ve"]
                    on("vector", f)
                    return C.counts["ve"]

                def tr_group(srcs_, dst_ap, wn_tok, war=None):
                    pj = tr_group.n % 2
                    tr_group.n += 1
                    n = len(srcs_)

                    def f(t, pj=pj, wn_tok=wn_tok, srcs_=srcs_):
                        C.wait(t, "t", "ve", wn_tok)
                        C.wait(t, "t", "ve", ptw_free[pj])
                        for j, sap in enumerate(srcs_):
                            C.inc(t.transpose(ptCD[pj][:, j * 128:(j + 1) * 128],
                                              sap, ident[:, :]), "te")
                        ptw_busy[pj] = C.counts["te"]
                    on("tensor", f)

                    def f(v, pj=pj, dst_ap=dst_ap, war=war, n=n):
                        if war:
                            for sn, val in war:
                                C.wait(v, "v", sn, val)
                        C.wait(v, "v", "te", ptw_busy[pj])
                        C.inc(v.tensor_copy(out=dst_ap, in_=ptCD[pj][:, 0:n * 128]), "ve")
                        ptw_free[pj] = C.counts["ve"]
                    on("vector", f)
                tr_group.n = 0

                WDB = MLP // 2 // NCORES

                def w_sources():
                    for mi, (dc, ds) in enumerate([(d_gcod, d_gsc), (d_ucod, d_usc)]):
                        for nl in range(NTL):
                            yield ("gu", mi, nl, 1024,
                                   dc[nl * 128:(nl + 1) * 128, :],
                                   ds[nl * 128:(nl + 1) * 128, :])
                    for ht in range(16):
                        yield ("wd", 0, ht, WDB,
                               d_dcod[ht * 128:(ht + 1) * 128, :],
                               d_dsc[ht * 128:(ht + 1) * 128, :])

                wao_last = [0]
                wd_last = [0, 0]

                def emit_out(kind, mi, which, kb, tok):
                    if kind == "gu":
                        for g8 in range(2):
                            tr_group([wn[:, (8 * g8 + j) * 128:(8 * g8 + j + 1) * 128]
                                      for j in range(8)],
                                     wasm[:, g8 * 1024:(g8 + 1) * 1024], tok,
                                     war=[("wao", wao_last[0])] if g8 == 0 else None)

                        def f(s, mi=mi, which=which):
                            C.wait(s, "sc", "ve")
                            bb = mi * NTL + which
                            C.dma(s, "sc", "wao",
                                  [(d_wag_in.ap()[bb * 128:(bb + 1) * 128, :], wasm[:, :])])
                        on("scalar", f)
                        wao_last[0] = C.counts["wao"]
                    else:
                        ht = which
                        for nl0 in range(0, NTL, 8):
                            nls = list(range(nl0, min(nl0 + 8, NTL)))
                            sg = tr_group.n % 2
                            tr_group([wn[:, nl * 128:(nl + 1) * 128] for nl in nls],
                                     wstg[:, sg * 1024:sg * 1024 + len(nls) * 128], tok,
                                     war=[(wd_sem[sg], wd_last[sg])])

                            def f(s, ht=ht, nls=nls, sg=sg):
                                C.wait(s, "sc", "ve")
                                C.dma(s, "sc", wd_sem[sg],
                                      [(d_wag_in.ap()[(2 * NTL + nl) * 128:
                                                      (2 * NTL + nl + 1) * 128,
                                                      ht * 128:(ht + 1) * 128],
                                        wstg[:, sg * 1024 + jj * 128:
                                             sg * 1024 + (jj + 1) * 128])
                                       for jj, nl in enumerate(nls)])
                            on("scalar", f)
                            wd_last[sg] = C.counts[wd_sem[sg]]

                def wsteps():
                    srcs = list(w_sources())
                    cod_load(0, srcs[0][4], srcs[0][5], srcs[0][3])
                    pend = None
                    for idx, (kind, mi, which, kb, sc_, ss_) in enumerate(srcs):
                        if idx + 1 < len(srcs):
                            cod_load(idx + 1, srcs[idx + 1][4], srcs[idx + 1][5],
                                     srcs[idx + 1][3])
                        if pend is not None:
                            emit_out(*pend)
                        wn_war = C.counts["te"]
                        tok = dq_block(idx, kb, wn_war)
                        pend = (kind, mi, which, kb, tok)
                        yield
                    emit_out(*pend)
                    yield

                wgen = wsteps()

                with (nc.sbuf_tensor("xa", [128, 2 * H], F32) as xa,
                      nc.sbuf_tensor("xln", [128, H], BF16) as xln,
                      nc.sbuf_tensor("xlt", [128, 16 * T], BF16) as xlt,
                      nc.sbuf_tensor("st6", [128, 24], F32) as st6,
                      nc.sbuf_tensor("mrs", [128, 4], F32) as mrs,
                      nc.psum_tensor("ptA", [128, 1024], BF16) as ptA,
                      nc.psum_tensor("ptB", [128, 1024], BF16) as ptB):
                    ptAB = [ptA, ptB]
                    xa_tok = [0, 0]
                    xb_tok = [0, 0]
                    xa_free = [0, 0]
                    pt_busy = [0, 0]
                    pt_free = [0, 0]
                    H2 = H // 2
                    TAH = T // AGH

                    def emit_ag(half):
                        def f(g, half=half):
                            C.wait(g, "g", "ve")
                            C.dma(g, "g", "agx",
                                  [(d_agx_in[half].ap()[j * 128:(j + 1) * 128, :],
                                    xlt[:, j * T + half * TAH:j * T + (half + 1) * TAH])
                                   for j in range(16)])
                        on("gpsimd", f)
                        agx_tok = C.counts["agx"]

                        def f(g, half=half, agx_tok=agx_tok):
                            C.wait(g, "g", "agx", agx_tok)
                            C.inc(g.collective_compute(
                                "AllGather", ALU.bypass, replica_groups=[list(range(NCORES))],
                                ins=[d_agx_in[half].ap().opt()],
                                outs=[d_agx_out[half].ap().opt()]), "c1")
                        on("gpsimd", f)

                    for i in range(TT):
                        sl = i % 2

                        def f(s, i=i, sl=sl):
                            C.wait(s, "sy", "ve", xa_free[sl])
                            xa_tok[sl] = C.dma(s, "sy", f"xa{sl}",
                                               [(xa[:, sl * H:sl * H + H2],
                                                 d_x[i * 128:(i + 1) * 128, 0:H2])])
                        on("sync", f)

                        def f(s, i=i, sl=sl):
                            C.wait(s, "sc", "ve", xa_free[sl])
                            xb_tok[sl] = C.dma(s, "sc", f"xb{sl}",
                                               [(xa[:, sl * H + H2:(sl + 1) * H],
                                                 d_x[i * 128:(i + 1) * 128, H2:H])])
                        on("scalar", f)

                        def f(v, sl=sl):
                            C.wait(v, "v", f"xa{sl}", xa_tok[sl])
                            C.wait(v, "v", f"xb{sl}", xb_tok[sl])
                            C.wait(v, "v", "se")
                            for c4 in range(4):
                                C.inc(v.bn_stats(out=st6[:, 6 * c4:6 * c4 + 6],
                                                 in_=xa[:, sl * H + 512 * c4:sl * H + 512 * (c4 + 1)]), "ve")
                            C.wait(v, "v", "ve")
                            C.inc(v.bn_aggr(out=mrs[:, 0:2], in_=st6[:, :]), "ve")
                        on("vector", f)

                        def f(s):
                            C.wait(s, "s", "ve")
                            C.inc(s.activation(out=mrs[:, 2:3], in_=mrs[:, 1:2], func=AF.Sqrt,
                                               bias=consts[:, 2:3], scale=1.0), "se")
                        on("scalar", f)

                        def f(v, sl=sl):
                            C.wait(v, "v", "se")
                            C.wait(v, "v", "te")
                            C.inc(v.reciprocal(out=mrs[:, 3:4], in_=mrs[:, 2:3]), "ve")
                            C.wait(v, "v", "ve")
                            C.inc(v.tensor_scalar(out=xln[:, :], in0=xa[:, sl * H:(sl + 1) * H],
                                                  scalar1=mrs[:, 0:1], scalar2=mrs[:, 3:4],
                                                  op0=ALU.subtract, op1=ALU.mult), "ve")
                            xa_free[sl] = C.counts["ve"]
                        on("vector", f)
                        ln_tok = C.counts["ve"]

                        for g8 in range(2):
                            def f(t, g8=g8, ln_tok=ln_tok):
                                C.wait(t, "t", "ve", ln_tok)
                                C.wait(t, "t", "ve", pt_free[g8])
                                for j8 in range(8):
                                    j = g8 * 8 + j8
                                    C.inc(t.transpose(ptAB[g8][:, j8 * 128:(j8 + 1) * 128],
                                                      xln[:, j * 128:(j + 1) * 128],
                                                      ident[:, :]), "te")
                                pt_busy[g8] = C.counts["te"]
                            on("tensor", f)

                            def f(v, i=i, g8=g8):
                                C.wait(v, "v", "te", pt_busy[g8])
                                for j8 in range(8):
                                    j = g8 * 8 + j8
                                    C.inc(v.tensor_copy(
                                        out=xlt[:, j * T + i * 128:j * T + (i + 1) * 128],
                                        in_=ptAB[g8][:, j8 * 128:(j8 + 1) * 128]), "ve")
                                pt_free[g8] = C.counts["ve"]
                            on("vector", f)

                        if (i + 1) % (TT // AGH) == 0:
                            emit_ag((i + 1) // (TT // AGH) - 1)



                swq = _e(nc.sbuf_tensor("swq", [128, 16 * 256], BF16))
                swk = _e(nc.sbuf_tensor("swk", [128, 16 * 256], BF16))
                swv = _e(nc.sbuf_tensor("swv", [128, 16 * 256], BF16))
                xs = _e(nc.sbuf_tensor("xs", [128, 2 * 16 * BCH], BF16))
                cs = _e(nc.sbuf_tensor("cs", [128, 2 * 2 * BCH], F16))
                rsc = _e(nc.sbuf_tensor("rsc", [128, 2 * BCH], F16))
                rp16 = _e(nc.sbuf_tensor("rp16", [128, 2 * BCH], F16))
                pp1 = _e(nc.psum_tensor("pp1", [128, 3 * 512], F32))
                pp2 = _e(nc.psum_tensor("pp2", [128, 3 * 512], F32))

                def f(g):
                    C.dma(g, "g", "swl",
                          [(dst[:, :].rearrange("p (a b) -> p a b", a=16),
                            src.rearrange("(a p) b -> p a b", p=128))
                           for dst, src in [(swq, d_wqT), (swk, d_wkT), (swv, d_wvT)]])
                on("gpsimd", f)
                sw_tok = C.counts["swl"]

                xs_tok = [0, 0]
                xs_free = [0, 0]
                cs_tok = [0, 0]
                cs_free = [0, 0]
                pp_busy = [0, 0]
                pp_free = [0, 0]
                pps = [pp1, pp2]
                for _ in range(8):
                    next(wgen, None)
                NWSTEP = (40 + SCB - 1) // SCB

                TAH = T // AGH
                H2 = H // 2
                xt_tok = [0, 0]
                tb_order = sorted(range(SCB), key=lambda tb: ((tb * BCH) % T) // TAH)

                def tb_coords(tb):
                    r, off = (tb * BCH) // T, (tb * BCH) % T
                    half = off // TAH
                    return r, half, off - half * TAH

                for ti_, tb in enumerate(tb_order):
                    sl = ti_ % 2
                    r, half, off2 = tb_coords(tb)
                    nj0 = 16 if ti_ == 0 else 8

                    def f(s, sl=sl, r=r, off2=off2, half=half, nj0=nj0):
                        C.wait(s, "sy", "c1", 1 + half)
                        C.wait(s, "sy", "te", xs_free[sl])
                        xs_tok[sl] = C.dma(s, "sy", f"xs{sl}",
                            [(xs[:, sl * 16 * BCH:sl * 16 * BCH + nj0 * BCH]
                              .rearrange("p (j c) -> p j c", j=nj0),
                              d_agx_out[half].ap()[r * H:r * H + nj0 * 128, off2:off2 + BCH]
                              .rearrange("(j p) c -> p j c", p=128))])
                    on("sync", f)

                    if ti_ + 1 < SCB:
                        sl2 = (ti_ + 1) % 2
                        r2, half2, of2 = tb_coords(tb_order[ti_ + 1])

                        def f(s, sl2=sl2, r2=r2, of2=of2, half2=half2):
                            C.wait(s, "sc", "c1", 1 + half2)
                            C.wait(s, "sc", "te", xs_free[sl2])
                            xt_tok[sl2] = C.dma(s, "sc", f"xt{sl2}",
                                [(xs[:, sl2 * 16 * BCH + 8 * BCH:(sl2 + 1) * 16 * BCH]
                                  .rearrange("p (j c) -> p j c", j=8),
                                  d_agx_out[half2].ap()[r2 * H + H2:(r2 + 1) * H, of2:of2 + BCH]
                                  .rearrange("(j p) c -> p j c", p=128))])
                        on("scalar", f)

                    def f(g, sl=sl, tb=tb):
                        C.wait(g, "g", "ve", cs_free[sl])
                        cs_tok[sl] = C.dma(g, "g", f"cs{sl}",
                            [(cs[:, sl * 2 * BCH:sl * 2 * BCH + BCH],
                              d_cos[:, tb * BCH:(tb + 1) * BCH]),
                             (cs[:, sl * 2 * BCH + BCH:(sl + 1) * 2 * BCH],
                              d_sin[:, tb * BCH:(tb + 1) * BCH])])
                    on("gpsimd", f)

                    for pi in range(2):
                        plist = ([("q", 0), ("q", 1), ("v", 0)] if pi == 0
                                 else [("k", 0), ("k", 1), ("v", 1)])

                        def f(t, sl=sl, pi=pi, plist=plist):
                            C.wait(t, "t", "swl", sw_tok)
                            C.wait(t, "t", f"xs{sl}", xs_tok[sl])
                            C.wait(t, "t", f"xt{sl}", xt_tok[sl])
                            C.wait(t, "t", "ve", pp_free[pi])
                            for j in range(16):
                                rhs = xs[:, (sl * 16 + j) * BCH:(sl * 16 + j + 1) * BCH]
                                for oi, (pr, dt) in enumerate(plist):
                                    w = {"q": swq, "k": swk, "v": swv}[pr]
                                    C.inc(t.matmul(
                                        pps[pi][:, oi * 512:oi * 512 + BCH],
                                        w[:, j * 256 + dt * 128:j * 256 + (dt + 1) * 128],
                                        rhs, start=(j == 0), stop=(j == 15)), "te")
                            pp_busy[pi] = C.counts["te"]
                            if pi == 1:
                                xs_free[sl] = C.counts["te"]
                        on("tensor", f)

                        def f(v, sl=sl, pi=pi, tb=tb):
                            C.wait(v, "v", "te", pp_busy[pi])
                            C.wait(v, "v", f"cs{sl}", cs_tok[sl])
                            C.wait(v, "v", "ve")
                            pp = pps[pi]
                            co = cs[:, sl * 2 * BCH:sl * 2 * BCH + BCH]
                            sn = cs[:, sl * 2 * BCH + BCH:(sl + 1) * 2 * BCH]
                            dst = qrT if pi == 0 else krT
                            a, b = tb * BCH, (tb + 1) * BCH
                            C.inc(v.tensor_copy(out=rp16[:, 0:BCH], in_=pp[:, 0:BCH]), "ve")
                            C.inc(v.tensor_copy(out=rp16[:, BCH:2 * BCH],
                                                in_=pp[:, 512:512 + BCH]), "ve")
                            C.inc(v.tensor_copy(out=vT[:, (0 if pi == 0 else 1) * S + a:
                                                       (0 if pi == 0 else 1) * S + b],
                                                in_=pp[:, 1024:1024 + BCH]), "ve")
                            pp_free[pi] = C.counts["ve"]
                            C.wait(v, "v", "ve")
                            C.inc(v.tensor_tensor(out=rsc[:, 0:BCH], in0=rp16[:, 0:BCH],
                                                  in1=co, op=ALU.mult), "ve")
                            C.inc(v.tensor_tensor(out=rsc[:, BCH:2 * BCH],
                                                  in0=rp16[:, BCH:2 * BCH],
                                                  in1=sn, op=ALU.mult), "ve")
                            C.wait(v, "v", "ve")
                            C.inc(v.tensor_tensor(out=dst[:, a:b], in0=rsc[:, 0:BCH],
                                                  in1=rsc[:, BCH:2 * BCH],
                                                  op=ALU.subtract), "ve")
                            C.wait(v, "v", "ve")
                            C.inc(v.tensor_tensor(out=rsc[:, 0:BCH],
                                                  in0=rp16[:, BCH:2 * BCH],
                                                  in1=co, op=ALU.mult), "ve")
                            C.wait(v, "v", "ve")
                            C.inc(v.tensor_tensor(out=rsc[:, BCH:2 * BCH],
                                                  in0=rp16[:, 0:BCH],
                                                  in1=sn, op=ALU.mult), "ve")
                            C.wait(v, "v", "ve")
                            C.inc(v.tensor_tensor(out=dst[:, S + a:S + b], in0=rsc[:, 0:BCH],
                                                  in1=rsc[:, BCH:2 * BCH], op=ALU.add), "ve")
                            if pi == 1:
                                cs_free[sl] = C.counts["ve"]
                        on("vector", f)

                        if pi == 0:
                            for _ in range(NWSTEP // 2):
                                next(wgen, None)

                    for _ in range(NWSTEP - NWSTEP // 2):
                        next(wgen, None)

                for _ in wgen:
                    pass

                for eng_, key_ in [("vector", "v"), ("scalar", "s"), ("tensor", "t")]:
                    def f(e, key_=key_):
                        C.wait(e, key_, "wao", C.counts["wao"])
                        C.wait(e, key_, "wdo", C.counts["wdo"])
                        C.wait(e, key_, "wdo2", C.counts["wdo2"])
                    on(eng_, f)

                def f(g):
                    C.wait(g, "g", "wao", C.counts["wao"])
                    C.wait(g, "g", "wdo", C.counts["wdo"])
                    C.wait(g, "g", "wdo2", C.counts["wdo2"])
                    C.inc(g.collective_compute(
                        "AllGather", ALU.bypass, replica_groups=[list(range(NCORES))],
                        ins=[d_wag_in.ap().opt()], outs=[d_wag_out.ap().opt()]), "c2")
                on("gpsimd", f)

            with (nc.sbuf_tensor("sqs", [128, 512], BF16) as sqs,
                  nc.psum_tensor("pn", [128, 512], F32) as pn):
                for hi in range(2):
                    for ti, src_ in enumerate([qrT, krT]):
                        for ch in range(SC):
                            def f(s, hi=hi, src_=src_, ch=ch):
                                C.wait(s, "s", "te")
                                C.wait(s, "s", "ve")
                                C.inc(s.activation(
                                    out=sqs[:, :],
                                    in_=src_[:, hi * S + ch * 512:hi * S + (ch + 1) * 512],
                                    func=AF.Square), "se")
                            on("scalar", f)

                            def f(t):
                                C.wait(t, "t", "se")
                                C.wait(t, "t", "ve")
                                C.inc(t.matmul(pn[:, :], ones128[:, :], sqs[:, :],
                                               start=True, stop=True), "te")
                            on("tensor", f)

                            def f(v, hi=hi, ti=ti, ch=ch):
                                C.wait(v, "v", "te")
                                o = (2 * hi + ti) * SC + ch
                                C.inc(v.tensor_reduce(out=nsc[:, o:o + 1], in_=pn[:, :],
                                                      axis=AXX, op=ALU.max), "ve")
                            on("vector", f)

                b = 4 * SC
                for hi in range(2):
                    def f(v, hi=hi, b=b):
                        C.wait(v, "v", "ve")
                        C.inc(v.tensor_reduce(out=nsc[:, b:b + 1],
                                              in_=nsc[:, 2 * hi * SC:(2 * hi + 1) * SC],
                                              axis=AXX, op=ALU.max), "ve")
                        C.inc(v.tensor_reduce(out=nsc[:, b + 1:b + 2],
                                              in_=nsc[:, (2 * hi + 1) * SC:(2 * hi + 2) * SC],
                                              axis=AXX, op=ALU.max), "ve")
                        C.wait(v, "v", "ve")
                        C.inc(v.tensor_tensor(out=nsc[:, b + 2 + hi:b + 3 + hi],
                                              in0=nsc[:, b:b + 1],
                                              in1=nsc[:, b + 1:b + 2],
                                              op=ALU.mult), "ve")
                    on("vector", f)

                def f(s, b=b):
                    C.wait(s, "s", "ve")
                    C.inc(s.activation(out=nsc[:, b + 4:b + 6],
                                       in_=nsc[:, b + 2:b + 4], func=AF.Sqrt), "se")
                on("scalar", f)

                def f(v, b=b):
                    C.wait(v, "v", "se")
                    C.inc(v.tensor_scalar(out=negM[:, 0:2],
                                          in0=nsc[:, b + 4:b + 6],
                                          scalar1=-INV_SQRT_DH, scalar2=None,
                                          op0=ALU.mult), "ve")
                on("vector", f)

            NEGM_TOK[0] = C.counts["ve"]

            with nc.sbuf_tensor("cbT", [128, 2 * S], BF16) as cbT:
                for hi in range(2):
                    attn_head(nc, C, on, cfg, hi, qrT, krT, vT, ident, gmix, negM, cbT)

                    def f(s, hi=hi):
                        C.wait(s, "sy", "ve")
                        C.dma(s, "sy", "cmi",
                              [(d_cmb_in[hi].ap()[:, :], cbT[:, hi * S:(hi + 1) * S])])
                    on("sync", f)
                    cmi_tok = C.counts["cmi"]

                    def f(g, hi=hi, cmi_tok=cmi_tok):
                        C.wait(g, "g", "cmi", cmi_tok)
                        C.inc(g.collective_compute(
                            "AllGather", ALU.bypass, replica_groups=[list(range(NCORES))],
                            ins=[d_cmb_in[hi].ap().opt()], outs=[d_cmb_out[hi].ap().opt()]), "c3")
                    on("gpsimd", f)

        with nc.sbuf_tensor("xlt2", [128, 16 * T], BF16) as xlt2:
            phase_barrier()
            phase_D(nc, C, on, cfg, ident, consts, d_cmb_out, d_woT, d_x, d_x1, xlt2)

            phase_barrier()
            phase_E(nc, C, on, cfg, d_wag_out, d_x1, d_y, xlt2)

        def f(s):
            for sn in ("st0", "st1", "st2", "st3"):
                C.wait(s, "sy", sn, C.counts[sn])
        on("sync", f)

    return nc


def attn_head(nc, C, on, cfg, hi, qrT, krT, vT, ident, gmix, negM, cbT):
    S, T, KT, SC = cfg.S, cfg.T, cfg.KT, cfg.SC
    base = hi * S
    with (nc.sbuf_tensor(f"va{hi}", [128, KT * 132], BF16) as vaug,
          nc.sbuf_tensor(f"mo{hi}", [128, KT * 130], BF16) as mo,
          nc.sbuf_tensor(f"sc{hi}", [128, 1024], F32) as sc,
          nc.sbuf_tensor(f"mem{hi}", [128, 132], BF16) as mem,
          nc.sbuf_tensor(f"pb{hi}", [128, 2 * 1024], BF16) as pbuf,
          nc.sbuf_tensor(f"pvs{hi}", [128, 4 * 132], F32) as pvs,
          nc.sbuf_tensor(f"rc{hi}", [128, 2], F32) as rcp):

        with (nc.sbuf_tensor(f"qf{hi}", [128, S], BF16) as qf,
              nc.sbuf_tensor(f"kfT{hi}", [128, S], BF16) as kfT,
              nc.sbuf_tensor(f"kn4{hi}", [128, 512], BF16) as kfn4,
              nc.psum_tensor(f"pm{hi}", [128, 132], F32) as pmm,
              nc.psum_tensor(f"pm2{hi}", [128, 132], F32) as pmm2,
              nc.psum_tensor(f"pt{hi}", [128, 512], BF16) as ptr,
              nc.psum_tensor(f"pt3{hi}", [128, 512], BF16) as ptrB,
              nc.psum_tensor(f"pt5{hi}", [128, 512], BF16) as ptrC,
              nc.psum_tensor(f"pt6{hi}", [128, 512], BF16) as ptrD):
            pairs = [(ptr, ptrB), (ptrC, ptrD)]
            pair_busy = [0, 0]
            pair_free = [0, 0]

            def f(v):
                C.wait(v, "v", "te")
                C.wait(v, "v", "se")
                C.inc(v.memset(vaug[:, 128:KT * 132:132], 1.0), "ve")
                C.inc(v.tensor_scalar(out=qf[:, :], in0=qrT[:, base:base + S],
                                      scalar1=0.0, scalar2=None, op0=ALU.min), "ve")
                C.inc(v.tensor_scalar(out=kfT[:, :], in0=krT[:, base:base + S],
                                      scalar1=0.0, scalar2=None, op0=ALU.min), "ve")
            on("vector", f)

            def f(s):
                C.wait(s, "s", "ve")
                C.inc(s.activation(out=qf[:, :], in_=qf[:, :], func=AF.Exp), "se")
                C.inc(s.activation(out=kfT[:, :], in_=kfT[:, :], func=AF.Exp), "se")
            on("scalar", f)

            def f(v):
                C.wait(v, "v", "se")
                C.inc(v.scalar_tensor_tensor(out=qf[:, :], in0=qrT[:, base:base + S],
                                             scalar=0.0, in1=qf[:, :],
                                             op0=ALU.max, op1=ALU.add), "ve")
                C.inc(v.scalar_tensor_tensor(out=kfT[:, :], in0=krT[:, base:base + S],
                                             scalar=0.0, in1=kfT[:, :],
                                             op0=ALU.max, op1=ALU.add), "ve")
            on("vector", f)
            kf_tok = C.counts["ve"]
            kn4_free = [0]

            for ch in range(SC):
                def f(t, ch=ch, kf_tok=kf_tok):
                    pA, pB = pairs[ch % 2]
                    C.wait(t, "t", "ve", kf_tok)
                    C.wait(t, "t", "ve", pair_free[ch % 2])
                    for q in range(4):
                        kt = ch * 4 + q
                        C.inc(t.transpose(pA[:, q * 128:(q + 1) * 128],
                                          vT[:, base + kt * 128:base + (kt + 1) * 128],
                                          ident[:, :]), "te")
                        C.inc(t.transpose(pB[:, q * 128:(q + 1) * 128],
                                          kfT[:, kt * 128:(kt + 1) * 128],
                                          ident[:, :]), "te")
                    pair_busy[ch % 2] = C.counts["te"]
                on("tensor", f)

                def f(v, ch=ch):
                    pA, pB = pairs[ch % 2]
                    C.wait(v, "v", "te", pair_busy[ch % 2])
                    C.wait(v, "v", "te", kn4_free[0])
                    for q in range(4):
                        kt = ch * 4 + q
                        C.inc(v.tensor_copy(out=vaug[:, kt * 132:kt * 132 + 128],
                                            in_=pA[:, q * 128:(q + 1) * 128]), "ve")
                    C.inc(v.tensor_copy(out=kfn4[:, :], in_=pB[:, :]), "ve")
                    pair_free[ch % 2] = C.counts["ve"]
                on("vector", f)

                def f(t, ch=ch):
                    C.wait(t, "t", "ve")
                    for q in range(4):
                        kt = ch * 4 + q
                        C.inc(t.matmul(pmm[:, 0:129], kfn4[:, q * 128:(q + 1) * 128],
                                       vaug[:, kt * 132:kt * 132 + 129],
                                       start=(kt == 0), stop=(kt == KT - 1)), "te")
                    kn4_free[0] = C.counts["te"]
                on("tensor", f)

            def f(v):
                C.wait(v, "v", "te")
                C.inc(v.tensor_copy(out=mem[:, 0:129], in_=pmm[:, 0:129]), "ve")
            on("vector", f)

            mo_busy = [0, 0]
            mo_free = [0, 0]
            with (nc.psum_tensor(f"po{hi}", [128, 396], F32) as pmo1,
                  nc.psum_tensor(f"po2{hi}", [128, 396], F32) as pmo2):
                pmos = [pmo1, pmo2]
                for g0 in range(0, KT, 3):
                    g2 = (g0 // 3) % 2
                    kts = tuple(range(g0, min(g0 + 3, KT)))

                    def f(t, g0=g0, g2=g2, kts=kts):
                        if g0 == 0:
                            C.wait(t, "t", "ve")
                        C.wait(t, "t", "ve", mo_free[g2])
                        for j, kt in enumerate(kts):
                            C.inc(t.matmul(pmos[g2][:, j * 132:j * 132 + 129],
                                           qf[:, kt * 128:(kt + 1) * 128],
                                           mem[:, 0:129], start=True, stop=True), "te")
                        mo_busy[g2] = C.counts["te"]
                    on("tensor", f)

                    def f(v, g2=g2, kts=kts):
                        C.wait(v, "v", "te", mo_busy[g2])
                        for j, kt in enumerate(kts):
                            C.inc(v.tensor_copy(out=mo[:, kt * 130:kt * 130 + 129],
                                                in_=pmos[g2][:, j * 132:j * 132 + 129]), "ve")
                        mo_free[g2] = C.counts["ve"]
                    on("vector", f)

        with nc.sbuf_tensor(f"cn{hi}", [128, S], BF16) as cbN:
            attn_softmax(nc, C, on, cfg, hi, qrT, krT, vT, ident, gmix, negM,
                         cbT, vaug, mo, sc, pbuf, rcp, cbN, pvs)


def attn_softmax(nc, C, on, cfg, hi, qrT, krT, vT, ident, gmix, negM,
                 cbT, vaug, mo, sc, pbuf, rcp, cbN, pvs):
    S, T, KT, SC = cfg.S, cfg.T, cfg.KT, cfg.SC
    base = hi * S
    if True:
        with (nc.psum_tensor(f"pSa{hi}", [128, 1024], F32) as pSa,
              nc.psum_tensor(f"pSb{hi}", [128, 1024], F32) as pSb,
              nc.psum_tensor(f"pv0{hi}", [128, 129], F32) as pv0,
              nc.psum_tensor(f"pv1{hi}", [128, 129], F32) as pv1,
              nc.psum_tensor(f"pv2{hi}", [128, 129], F32) as pv2,
              nc.psum_tensor(f"pv3{hi}", [128, 129], F32) as pv3):
            pS = [pSa, pSb]
            ppv = [pv0, pv1, pv2, pv3]
            ppv_free = [0]
            NG = KT // 2

            for qc in range(SC):
                TE0 = C.counts["te"]
                SE0 = C.counts["se"]

                def fT(t, qc=qc, TE0=TE0, SE0=SE0):
                    C.wait(t, "t", "ve", ppv_free[0])
                    if qc == 0:
                        C.wait(t, "t", "ve")
                    C.wait(t, "t", "se", SE0)

                    def smm(kg):
                        gs = kg % 2
                        if kg >= 2:
                            C.wait(t, "t", "se", SE0 + kg - 1)
                        for u in range(2):
                            kt = kg * 2 + u
                            C.inc(t.matmul(pS[gs][:, u * 512:(u + 1) * 512],
                                           krT[:, base + kt * 128:base + (kt + 1) * 128],
                                           qrT[:, base + qc * 512:base + (qc + 1) * 512],
                                           start=True, stop=True), "te")

                    def pv(kg):
                        gs = kg % 2
                        C.wait(t, "t", "se", SE0 + kg + 1)
                        for u in range(2):
                            kt = kg * 2 + u
                            for qs in range(4):
                                C.inc(t.matmul(
                                    ppv[qs][:, 0:129],
                                    pbuf[:, gs * 1024 + u * 512 + qs * 128:
                                         gs * 1024 + u * 512 + (qs + 1) * 128],
                                    vaug[:, kt * 132:kt * 132 + 129],
                                    start=(kt == 0), stop=(kt == KT - 1)), "te")

                    smm(0)
                    for kg in range(1, NG):
                        smm(kg)
                        pv(kg - 1)
                    pv(NG - 1)
                on("tensor", fT)

                def fS(s, hi=hi, TE0=TE0):
                    C.wait(s, "s", "ve", NEGM_TOK[0])
                    for kg in range(NG):
                        gs = kg % 2
                        C.wait(s, "s", "te", TE0 + (2 if kg == 0 else 10 * kg - 6))
                        C.inc(s.activation(out=pbuf[:, gs * 1024:(gs + 1) * 1024],
                                           in_=pS[gs][:, :], func=AF.Exp,
                                           bias=negM[:, hi:hi + 1], scale=INV_SQRT_DH), "se")
                on("scalar", fS)

                def f(v):
                    C.wait(v, "v", "te")
                    C.wait(v, "v", "ve")
                    for qs in range(4):
                        C.inc(v.tensor_copy(out=pvs[:, qs * 132:qs * 132 + 129],
                                            in_=ppv[qs][:, 0:129]), "ve")
                    ppv_free[0] = C.counts["ve"]
                on("vector", f)

                for qs in range(4):
                    tt = qc * 4 + qs

                    def f(v, hi=hi, qs=qs, tt=tt):
                        C.wait(v, "v", "ve")
                        C.inc(v.reciprocal(out=rcp[:, 0:1],
                                           in_=pvs[:, qs * 132 + 128:qs * 132 + 129]), "ve")
                        C.inc(v.reciprocal(out=rcp[:, 1:2],
                                           in_=mo[:, tt * 130 + 128:tt * 130 + 129]), "ve")
                        C.wait(v, "v", "ve")
                        C.inc(v.tensor_scalar(out=sc[:, 0:128],
                                              in0=pvs[:, qs * 132:qs * 132 + 128],
                                              scalar1=rcp[:, 0:1],
                                              scalar2=gmix[:, 2 * hi + 1:2 * hi + 2],
                                              op0=ALU.mult, op1=ALU.mult), "ve")
                        C.inc(v.tensor_scalar(out=sc[:, 128:256],
                                              in0=mo[:, tt * 130:tt * 130 + 128],
                                              scalar1=rcp[:, 1:2],
                                              scalar2=gmix[:, 2 * hi:2 * hi + 1],
                                              op0=ALU.mult, op1=ALU.mult), "ve")
                        C.wait(v, "v", "ve")
                        C.inc(v.tensor_tensor(out=cbN[:, tt * 128:(tt + 1) * 128],
                                              in0=sc[:, 0:128],
                                              in1=sc[:, 128:256], op=ALU.add), "ve")
                    on("vector", f)

        with (nc.psum_tensor(f"pt2{hi}", [128, 1024], BF16) as ptr2,
              nc.psum_tensor(f"pt4{hi}", [128, 1024], BF16) as ptr2b):
            p3 = [ptr2, ptr2b]
            s3_busy = [0, 0]
            s3_free = [0, 0]
            for g in range(KT // 8):
                def f(t, g=g):
                    if g == 0:
                        C.wait(t, "t", "ve")
                    C.wait(t, "t", "ve", s3_free[g % 2])
                    for j in range(8):
                        tt = g * 8 + j
                        C.inc(t.transpose(p3[g % 2][:, j * 128:(j + 1) * 128],
                                          cbN[:, tt * 128:(tt + 1) * 128],
                                          ident[:, :]), "te")
                    s3_busy[g % 2] = C.counts["te"]
                on("tensor", f)

                def f(v, hi=hi, g=g):
                    C.wait(v, "v", "te", s3_busy[g % 2])
                    C.inc(v.tensor_copy(out=cbT[:, hi * S + g * 1024:hi * S + (g + 1) * 1024],
                                        in_=p3[g % 2][:, :]), "ve")
                    s3_free[g % 2] = C.counts["ve"]
                on("vector", f)


def phase_D(nc, C, on, cfg, ident, consts, d_cmb_out, d_woT, d_x, d_x1, xlt2):
    S, T, TT = cfg.S, cfg.T, cfg.TT
    with (nc.sbuf_tensor("cstT", [128, 16 * T], BF16) as cstT,
          nc.sbuf_tensor("swo", [128, 16 * H], BF16) as swo,
          nc.sbuf_tensor("xa2", [128, 2 * H], F32) as xa2,
          nc.sbuf_tensor("x1t", [128, H], F32) as x1t,
          nc.sbuf_tensor("xln2", [128, H], BF16) as xln2,
          nc.sbuf_tensor("st6b", [128, 24], F32) as st6,
          nc.sbuf_tensor("mrs2", [128, 4], F32) as mrs,
          nc.psum_tensor("pwo", [128, 2048], F32) as pwo,
          nc.psum_tensor("ptE", [128, 1024], BF16) as ptE,
          nc.psum_tensor("ptF", [128, 1024], BF16) as ptF):
        ptEF = [ptE, ptF]

        def f(g):
            C.dma(g, "g", "swo",
                  [(swo[:, :].rearrange("p (a b) -> p a b", a=16),
                    d_woT.rearrange("(a p) b -> p a b", p=128))])
            pid = nc.partition_id(engines=[g.engine])
            off = pid * T
            C.wait(g, "g", "c3", 1)
            C.dma(g, "g", "cst",
                  [(cstT[:, r * T:(r + 1) * T],
                    d_cmb_out[0].ap()[r * 128:(r + 1) * 128, bass.ds(off, T)])
                   for r in range(NCORES)])
            C.wait(g, "g", "c3", 2)
            C.dma(g, "g", "cst2",
                  [(cstT[:, (r + 8) * T:(r + 9) * T],
                    d_cmb_out[1].ap()[r * 128:(r + 1) * 128, bass.ds(off, T)])
                   for r in range(NCORES // 2)])
        on("gpsimd", f)

        def f(s):
            pid = nc.partition_id(engines=[s.engine])
            off = pid * T
            C.wait(s, "sc", "c3", 2)
            C.dma(s, "sc", "cst3",
                  [(cstT[:, (r + 8) * T:(r + 9) * T],
                    d_cmb_out[1].ap()[r * 128:(r + 1) * 128, bass.ds(off, T)])
                   for r in range(NCORES // 2, NCORES)])
        on("scalar", f)
        swo_tok = C.counts["swo"]
        cst_tok = C.counts["cst"]
        cst2_tok = C.counts["cst2"]
        cst3_tok = C.counts["cst3"]

        xa_tok = [0, 0]
        xa_free = [0, 0]
        pt_busy = [0, 0]
        pt_free = [0, 0]
        pwo_free = [0]
        x1sp_tok = [0]

        for i in range(TT):
            sl = i % 2

            def f(s, i=i, sl=sl):
                C.wait(s, "sy", "ve", xa_free[sl])
                xa_tok[sl] = C.dma(s, "sy", f"xa{sl}",
                                   [(xa2[:, sl * H:(sl + 1) * H],
                                     d_x[i * 128:(i + 1) * 128, :])])
            on("sync", f)

            def f(t, i=i):
                C.wait(t, "t", "swo", swo_tok)
                C.wait(t, "t", "cst", cst_tok)
                C.wait(t, "t", "ve", pwo_free[0])
                for dt in range(16):
                    if dt == 8:
                        C.wait(t, "t", "cst2", cst2_tok)
                        C.wait(t, "t", "cst3", cst3_tok)
                    lhs = cstT[:, dt * T + i * 128:dt * T + (i + 1) * 128]
                    for ec in range(4):
                        C.inc(t.matmul(pwo[:, ec * 512:(ec + 1) * 512], lhs,
                                       swo[:, dt * H + ec * 512:dt * H + (ec + 1) * 512],
                                       start=(dt == 0), stop=(dt == 15)), "te")
            on("tensor", f)

            def f(v, sl=sl):
                C.wait(v, "v", "te")
                C.wait(v, "v", f"xa{sl}", xa_tok[sl])
                C.wait(v, "v", "x1s", x1sp_tok[0])
                C.wait(v, "v", "ve")
                C.wait(v, "v", "se")
                C.inc(v.tensor_tensor(out=x1t[:, :], in0=xa2[:, sl * H:(sl + 1) * H],
                                      in1=pwo[:, :], op=ALU.add), "ve")
                xa_free[sl] = C.counts["ve"]
                pwo_free[0] = C.counts["ve"]
                C.wait(v, "v", "ve")
                for c4 in range(4):
                    C.inc(v.bn_stats(out=st6[:, 6 * c4:6 * c4 + 6],
                                     in_=x1t[:, 512 * c4:512 * (c4 + 1)]), "ve")
                C.wait(v, "v", "ve")
                C.inc(v.bn_aggr(out=mrs[:, 0:2], in_=st6[:, :]), "ve")
            on("vector", f)
            x1_tok = C.counts["ve"] - 2

            def f(s, i=i, x1_tok=x1_tok):
                C.wait(s, "sy", "ve", x1_tok)
                x1sp_tok[0] = C.dma(s, "sy", "x1s",
                                    [(d_x1.ap()[i * 128:(i + 1) * 128, :], x1t[:, :])])
            on("sync", f)

            def f(s):
                C.wait(s, "s", "ve")
                C.inc(s.activation(out=mrs[:, 2:3], in_=mrs[:, 1:2], func=AF.Sqrt,
                                   bias=consts[:, 2:3], scale=1.0), "se")
            on("scalar", f)

            def f(v):
                C.wait(v, "v", "se")
                C.wait(v, "v", "te")
                C.inc(v.reciprocal(out=mrs[:, 3:4], in_=mrs[:, 2:3]), "ve")
                C.wait(v, "v", "ve")
                C.inc(v.tensor_scalar(out=xln2[:, :], in0=x1t[:, :],
                                      scalar1=mrs[:, 0:1], scalar2=mrs[:, 3:4],
                                      op0=ALU.subtract, op1=ALU.mult), "ve")
            on("vector", f)
            ln_tok = C.counts["ve"]

            for g8 in range(2):
                def f(t, g8=g8, ln_tok=ln_tok):
                    C.wait(t, "t", "ve", ln_tok)
                    C.wait(t, "t", "ve", pt_free[g8])
                    for j8 in range(8):
                        j = g8 * 8 + j8
                        C.inc(t.transpose(ptEF[g8][:, j8 * 128:(j8 + 1) * 128],
                                          xln2[:, j * 128:(j + 1) * 128],
                                          ident[:, :]), "te")
                    pt_busy[g8] = C.counts["te"]
                on("tensor", f)

                def f(v, i=i, g8=g8):
                    C.wait(v, "v", "te", pt_busy[g8])
                    for j8 in range(8):
                        j = g8 * 8 + j8
                        C.inc(v.tensor_copy(out=xlt2[:, j * T + i * 128:j * T + (i + 1) * 128],
                                            in_=ptEF[g8][:, j8 * 128:(j8 + 1) * 128]), "ve")
                    pt_free[g8] = C.counts["ve"]
                on("vector", f)


def phase_E(nc, C, on, cfg, d_wag_out, d_x1, d_y, xlt2):
    S, T, MLP, NTL, NTG, T2 = cfg.S, cfg.T, cfg.MLP, cfg.NTL, cfg.NTG, cfg.T2
    nblk = 3 * NTL
    TL2 = T2 // 128
    with (nc.sbuf_tensor("wgu", [128, 6 * H], BF16) as wgu,
          nc.sbuf_tensor("geb", [128, 512], BF16) as geb,
          nc.sbuf_tensor("upb", [128, 512], BF16) as upb,
          nc.sbuf_tensor("it", [128, NTG * T2], BF16) as it,
          nc.sbuf_tensor("wdb", [128, 4 * 512], BF16) as wdb,
          nc.sbuf_tensor("xr", [128, 4 * 512], F32) as xr,
          nc.sbuf_tensor("yo", [128, 4 * 512], F32) as yo):

        def rows_of(ntg, m):
            r, b = ntg // NTL, ntg % NTL
            return 128 * (nblk * r + m * NTL + b)

        wd_tok = [0, 0, 0, 0]
        wd_free = [0, 0, 0, 0]
        xr_tok = [0, 0, 0, 0]
        xr_free = [0, 0, 0, 0]
        for th in range(cfg.ffn_tsplit):
            t0 = th * T2
            wgu_tok = [0, 0, 0]
            wgu_tok2 = [0, 0, 0]
            wgu_free = [C.counts["te"]] * 3
            with (nc.psum_tensor(f"pg{th}", [128, 512], F32) as pg,
                  nc.psum_tensor(f"pu{th}", [128, 512], F32) as pu,
                  nc.psum_tensor(f"pg2{th}", [128, 512], F32) as pg2,
                  nc.psum_tensor(f"pu2{th}", [128, 512], F32) as pu2):
                pgs = [(pg, pu), (pg2, pu2)]
                pp_free = [C.counts["ve"], C.counts["ve"]]
                TE0e = C.counts["te"]
                VE0e = C.counts["ve"]
                SE0e = C.counts["se"]

                def fSg(s, TE0e=TE0e, VE0e=VE0e):
                    for ntg in range(NTG):
                        sl = ntg % 2
                        C.wait(s, "s", "te", TE0e + 32 * (ntg + 1))
                        if ntg >= 1:
                            C.wait(s, "s", "ve", VE0e + 2 * ntg)
                        C.inc(s.activation(out=geb[:, 0:T2], in_=pgs[sl][0][:, 0:T2],
                                           func=AF.Gelu_apprx_tanh), "se")
                on("scalar", fSg)

                for ntg in range(NTG):
                    sl = ntg % 3
                    ps_ = ntg % 2

                    def f(s, ntg=ntg, sl=sl):
                        C.wait(s, "sy", "c2", 1)
                        C.wait(s, "sy", "te", wgu_free[sl])
                        rg = rows_of(ntg, 0)
                        wgu_tok[sl] = C.dma(s, "sy", f"wg{sl}",
                            [(wgu[:, sl * 2 * H:sl * 2 * H + H],
                              d_wag_out.ap()[rg:rg + 128, :])])
                    on("sync", f)

                    def f(g, ntg=ntg, sl=sl):
                        C.wait(g, "g", "c2", 1)
                        C.wait(g, "g", "te", wgu_free[sl])
                        ru = rows_of(ntg, 1)
                        wgu_tok2[sl] = C.dma(g, "g", f"wu{sl}",
                            [(wgu[:, sl * 2 * H + H:(sl + 1) * 2 * H],
                              d_wag_out.ap()[ru:ru + 128, :])])
                    on("gpsimd", f)

                    def f(t, ntg=ntg, sl=sl, ps_=ps_, t0=t0):
                        C.wait(t, "t", f"wg{sl}", wgu_tok[sl])
                        C.wait(t, "t", f"wu{sl}", wgu_tok2[sl])
                        C.wait(t, "t", "ve", pp_free[ps_])
                        for j in range(16):
                            rhs = xlt2[:, j * T + t0:j * T + t0 + T2]
                            C.inc(t.matmul(
                                pgs[ps_][0][:, 0:T2],
                                wgu[:, sl * 2 * H + j * 128:sl * 2 * H + (j + 1) * 128],
                                rhs, start=(j == 0), stop=(j == 15)), "te")
                            C.inc(t.matmul(
                                pgs[ps_][1][:, 0:T2],
                                wgu[:, sl * 2 * H + H + j * 128:sl * 2 * H + H + (j + 1) * 128],
                                rhs, start=(j == 0), stop=(j == 15)), "te")
                        wgu_free[sl] = C.counts["te"]
                    on("tensor", f)
                    mm_tok = C.counts["te"]

                    def f(v, ntg=ntg, ps_=ps_, mm_tok=mm_tok, SE0e=SE0e):
                        C.wait(v, "v", "te", mm_tok)
                        C.wait(v, "v", "ve")
                        C.inc(v.tensor_copy(out=upb[:, 0:T2], in_=pgs[ps_][1][:, 0:T2]), "ve")
                        C.wait(v, "v", "se", SE0e + ntg + 1)
                        C.wait(v, "v", "ve")
                        C.inc(v.tensor_tensor(out=it[:, ntg * T2:(ntg + 1) * T2],
                                              in0=geb[:, 0:T2], in1=upb[:, 0:T2],
                                              op=ALU.mult), "ve")
                        pp_free[ps_] = C.counts["ve"]
                    on("vector", f)

            it_tok = C.counts["ve"]

            with (nc.psum_tensor(f"pd{th}", [128, TL2 * 512], F32) as pd_,
                  nc.psum_tensor(f"pe{th}", [128, TL2 * 512], F32) as pe_):
                pd_pair = [pd_, pe_]
                pd_free = [C.counts["ve"], C.counts["ve"]]
                se_snap = C.counts["se"]
                for hc in range(4):
                    pd = pd_pair[hc % 2]

                    def f(t, hc=hc, se_snap=se_snap):
                        C.wait(t, "t", "se", se_snap)
                        C.wait(t, "t", "ve", pd_free[hc % 2])
                    on("tensor", f)

                    for tt2 in range(TL2):
                        def f(s, th=th, hc=hc, tt2=tt2):
                            C.wait(s, "sc", "ve", xr_free[tt2])
                            C.wait(s, "sc", "x1s", C.counts["x1s"])
                            row = th * T2 + tt2 * 128
                            xr_tok[tt2] = C.dma(s, "sc", f"xr{tt2}",
                                [(xr[:, tt2 * 512:(tt2 + 1) * 512],
                                  d_x1.ap()[row:row + 128, hc * 512:(hc + 1) * 512])])
                        on("scalar", f)

                    for ntg in range(NTG):
                        sl = ntg % 4
                        qeng, qkey = (("sync", "sy") if sl % 2 == 0
                                      else ("scalar", "sc"))

                        def f(s, ntg=ntg, sl=sl, hc=hc, qkey=qkey):
                            C.wait(s, qkey, "te", wd_free[sl])
                            rd = rows_of(ntg, 2)
                            wd_tok[sl] = C.dma(s, qkey, f"wd{sl}",
                                [(wdb[:, sl * 512:(sl + 1) * 512],
                                  d_wag_out.ap()[rd:rd + 128, hc * 512:(hc + 1) * 512])])
                        on(qeng, f)

                        def f(t, ntg=ntg, sl=sl, it_tok=it_tok):
                            C.wait(t, "t", f"wd{sl}", wd_tok[sl])
                            C.wait(t, "t", "ve", it_tok)
                            for tt2 in range(TL2):
                                C.inc(t.matmul(
                                    pd[:, tt2 * 512:(tt2 + 1) * 512],
                                    it[:, ntg * T2 + tt2 * 128:ntg * T2 + (tt2 + 1) * 128],
                                    wdb[:, sl * 512:(sl + 1) * 512],
                                    start=(ntg == 0), stop=(ntg == NTG - 1)), "te")
                            wd_free[sl] = C.counts["te"]
                        on("tensor", f)

                    for tt2 in range(TL2):
                        def f(v, tt2=tt2, pd=pd):
                            C.wait(v, "v", "te")
                            C.wait(v, "v", f"xr{tt2}", xr_tok[tt2])
                            C.wait(v, "v", f"st{tt2}", C.counts[f"st{tt2}"])
                            C.inc(v.tensor_tensor(out=yo[:, tt2 * 512:(tt2 + 1) * 512],
                                                  in0=xr[:, tt2 * 512:(tt2 + 1) * 512],
                                                  in1=pd[:, tt2 * 512:(tt2 + 1) * 512],
                                                  op=ALU.add), "ve")
                            xr_free[tt2] = C.counts["ve"]
                        on("vector", f)
                        yo_tok = C.counts["ve"]

                        def f(s, th=th, hc=hc, tt2=tt2, yo_tok=yo_tok):
                            C.wait(s, "sy", "ve", yo_tok)
                            row = th * T2 + tt2 * 128
                            C.dma(s, "sy", f"st{tt2}",
                                  [(d_y[row:row + 128, hc * 512:(hc + 1) * 512],
                                    yo[:, tt2 * 512:(tt2 + 1) * 512])])
                        on("sync", f)
                    pd_free[hc % 2] = C.counts["ve"]



def _host_layouts(inputs, cfg: Cfg):
    x = np.ascontiguousarray(np.asarray(inputs["L0_attn_hidden_states"],
                                        dtype=np.float32).reshape(cfg.S, H))
    wqT = np.ascontiguousarray(np.asarray(inputs["L0_attn_Wq"], dtype=np.float32).T)
    wkT = np.ascontiguousarray(np.asarray(inputs["L0_attn_Wk"], dtype=np.float32).T)
    wvT = np.ascontiguousarray(np.asarray(inputs["L0_attn_Wv"], dtype=np.float32).T)
    wo = np.asarray(inputs["L0_attn_Wo"], dtype=np.float32)
    gate = np.asarray(inputs["L0_attn_gate"], dtype=np.float32).reshape(NH)

    woT = np.ascontiguousarray(wo.T)

    inv_freq = 1.0 / 10000.0 ** (np.arange(0, H, 2, dtype=np.float64) / H)
    t = np.arange(cfg.S, dtype=np.float64)
    ang = t[None, :] * inv_freq[:, None]
    cosT = np.cos(ang).astype(np.float32)
    sinT = np.sin(ang).astype(np.float32)

    def codes_u8(a):
        a = np.ascontiguousarray(np.asarray(a, dtype=np.int32))
        return np.ascontiguousarray(a.view(np.uint8)[:, ::4])

    gcod = codes_u8(inputs["L0_ffn_gate_W_packed"])
    ucod = codes_u8(inputs["L0_ffn_up_W_packed"])
    dcod = codes_u8(inputs["L0_ffn_down_W_packed"])
    gsc = np.ascontiguousarray(np.asarray(inputs["L0_ffn_gate_scales"], dtype=np.float32))
    usc = np.ascontiguousarray(np.asarray(inputs["L0_ffn_up_scales"], dtype=np.float32))
    dsc = np.ascontiguousarray(np.asarray(inputs["L0_ffn_down_scales"], dtype=np.float32))

    NPC = cfg.MLP // NCORES
    in_maps = []
    for c in range(NCORES):
        dsl = slice(c * DH, (c + 1) * DH)
        dsl2 = slice(H // 2 + c * DH, H // 2 + (c + 1) * DH)
        in_maps.append({
            "x": np.ascontiguousarray(x[c * cfg.T:(c + 1) * cfg.T]),
            "wqT": np.ascontiguousarray(np.concatenate([wqT[:, dsl], wqT[:, dsl2]], axis=1)),
            "wkT": np.ascontiguousarray(np.concatenate([wkT[:, dsl], wkT[:, dsl2]], axis=1)),
            "wvT": np.ascontiguousarray(np.concatenate([wvT[:, dsl], wvT[:, dsl2]], axis=1)),
            "woT": woT,
            "cosT": np.ascontiguousarray(cosT[c * DH:(c + 1) * DH]),
            "sinT": np.ascontiguousarray(sinT[c * DH:(c + 1) * DH]),
            "gate2": np.ascontiguousarray(np.broadcast_to(gate[[c, c + 8]].reshape(1, 2), (128, 2))),
            "gcod": np.ascontiguousarray(gcod[c * NPC:(c + 1) * NPC]),
            "ucod": np.ascontiguousarray(ucod[c * NPC:(c + 1) * NPC]),
            "dcod": np.ascontiguousarray(dcod[:, c * (NPC // 2):(c + 1) * (NPC // 2)]),
            "gsc": np.ascontiguousarray(gsc[c * NPC:(c + 1) * NPC]),
            "usc": np.ascontiguousarray(usc[c * NPC:(c + 1) * NPC]),
            "dsc": np.ascontiguousarray(dsc[:, c * (NPC // 32):(c + 1) * (NPC // 32)]),
        })
    return in_maps


def kernel(**inputs):
    cfg = Cfg()
    in_maps = _host_layouts(inputs, cfg)
    nc = build_program(cfg)
    res = run_bass_kernel_spmd(nc, in_maps, core_ids=list(range(NCORES)))
    y = np.concatenate([np.asarray(r["y"]) for r in res.results], axis=0)
    return np.ascontiguousarray(y.reshape(1, cfg.S, H), dtype=np.float32)


# version 27
# speedup vs baseline: 1.0119x; 1.0038x over previous
import numpy as np
from contextlib import ExitStack

import concourse.bass as bass
import concourse.mybir as mybir
from concourse.bass_utils import run_bass_kernel_spmd

F32 = mybir.dt.float32
F16 = mybir.dt.float16
BF16 = mybir.dt.bfloat16
U8 = mybir.dt.uint8
AF = mybir.ActivationFunctionType
ALU = mybir.AluOpType
AXX = mybir.AxisListType.X

NCORES = 8
NEGM_TOK = [0]
NH = 16
DH = 128
H = NH * DH
INV_SQRT_DH = 1.0 / float(np.sqrt(DH))


class Cfg:
    def __init__(self, S=8192, MLP=16384, ffn_tsplit=2):
        self.S, self.MLP = S, MLP
        self.T = S // NCORES
        self.TT = self.T // 128
        self.SC = S // 512
        self.KT = S // 128
        self.BCH = min(512, self.T)
        self.SCB = S // self.BCH
        self.NTL = MLP // 128 // NCORES
        self.NTG = MLP // 128
        self.ffn_tsplit = ffn_tsplit
        self.T2 = self.T // ffn_tsplit
        assert self.T % 128 == 0 and S % 512 == 0 and self.T2 % 128 == 0
        assert self.T2 <= 512 and MLP % (128 * NCORES) == 0
        assert (2 * self.KT) % 4 == 0


def build_program(cfg: Cfg):
    S, T, MLP, TT, SC, KT = cfg.S, cfg.T, cfg.MLP, cfg.TT, cfg.SC, cfg.KT
    NTL, NTG, T2, BCH, SCB = cfg.NTL, cfg.NTG, cfg.T2, cfg.BCH, cfg.SCB
    nc = bass.Bass("TRN2", target_bir_lowering=False)

    d_x = nc.dram_tensor("x", [T, H], F32, kind="ExternalInput").ap()
    d_wqT = nc.dram_tensor("wqT", [H, 256], F32, kind="ExternalInput").ap()
    d_wkT = nc.dram_tensor("wkT", [H, 256], F32, kind="ExternalInput").ap()
    d_wvT = nc.dram_tensor("wvT", [H, 256], F32, kind="ExternalInput").ap()
    d_woT = nc.dram_tensor("woT", [H, H], F32, kind="ExternalInput").ap()
    d_cos = nc.dram_tensor("cosT", [DH, S], F32, kind="ExternalInput").ap()
    d_sin = nc.dram_tensor("sinT", [DH, S], F32, kind="ExternalInput").ap()
    d_gate = nc.dram_tensor("gate2", [128, 2], F32, kind="ExternalInput").ap()
    d_gcod = nc.dram_tensor("gcod", [MLP // NCORES, H // 2], U8, kind="ExternalInput").ap()
    d_ucod = nc.dram_tensor("ucod", [MLP // NCORES, H // 2], U8, kind="ExternalInput").ap()
    d_dcod = nc.dram_tensor("dcod", [H, MLP // 2 // NCORES], U8, kind="ExternalInput").ap()
    d_gsc = nc.dram_tensor("gsc", [MLP // NCORES, H // 32], F32, kind="ExternalInput").ap()
    d_usc = nc.dram_tensor("usc", [MLP // NCORES, H // 32], F32, kind="ExternalInput").ap()
    d_dsc = nc.dram_tensor("dsc", [H, MLP // 32 // NCORES], F32, kind="ExternalInput").ap()
    d_y = nc.dram_tensor("y", [T, H], F32, kind="ExternalOutput").ap()

    AGH = 2 if T // 2 >= BCH else 1
    d_agx_in = [nc.dram_tensor(f"agx_in{h}", [H, T // AGH], BF16) for h in range(AGH)]
    d_agx_out = [nc.dram_tensor(f"agx_out{h}", [NCORES * H, T // AGH], BF16, addr_space="Shared")
                 for h in range(AGH)]
    nblk = 3 * NTL
    d_wag_in = nc.dram_tensor("wag_in", [nblk * 128, H], BF16)
    d_wag_out = nc.dram_tensor("wag_out", [NCORES * nblk * 128, H], BF16, addr_space="Shared")
    d_cmb_in = [nc.dram_tensor(f"cmbi{h}", [128, S], BF16) for h in range(2)]
    d_cmb_out = [nc.dram_tensor(f"cmbo{h}", [NCORES * 128, S], BF16, addr_space="Shared") for h in range(2)]
    d_x1 = nc.dram_tensor("x1d", [T, H], F32)

    ctx = ExitStack()

    class Sems:
        def __init__(self):
            self.counts, self.marks, self.sems = {}, {}, {}

        def new(self, name):
            self.sems[name] = ctx.enter_context(nc.semaphore(name))
            self.counts[name] = 0

        def inc(self, instr, sname, n=1):
            instr.then_inc(self.sems[sname], n)
            self.counts[sname] += n
            return self.counts[sname]

        def dma(self, eng, ename, sname, pairs):
            self.wait(eng, ename, sname, self.counts[sname])
            for (o, i_) in pairs:
                self.inc(eng.dma_start(out=o, in_=i_), sname, 16)
            return self.counts[sname]

        def wait(self, eng, ename, sname, val=None):
            if val is None:
                val = self.counts[sname]
            if val <= 0:
                return
            key = (ename, sname)
            if self.marks.get(key, 0) >= val:
                return
            eng.wait_ge(self.sems[sname], val)
            self.marks[key] = val

    C = Sems()

    def SB(name, shape, dt=F32):
        return ctx.enter_context(nc.sbuf_tensor(name, shape, dt))

    with ctx:
        for sn in ["te", "ve", "se", "ge", "c1", "c2", "rca", "rcl",
                   "sg", "swl", "agx", "cod", "wao", "xa0", "xa1", "xs0", "xs1",
                   "cs0", "cs1", "wg0", "wg1", "wd0", "wd1", "xr0", "xr1",
                   "cst", "swo", "x1s", "st0", "st1", "cmi", "c3", "wdo", "wdo2", "cod2", "wg2",
                   "wd2", "wd3", "cst2", "xb0", "xb1", "xt0", "xt1",
                   "wu0", "wu1", "wu2", "xr2", "xr3", "st2", "st3"]:
            C.new(sn)

        block = ctx.enter_context(nc.Block())

        def on(eng, fn):
            getattr(block, eng)(fn)

        _BARRIER_SKIP = {"c1", "c2", "c3", "rca", "rcl"}

        def phase_barrier():
            snap = {k: v for k, v in C.counts.items() if k not in _BARRIER_SKIP}
            for eng, key in [("sync", "sy"), ("scalar", "s"),
                             ("vector", "v"), ("tensor", "t"), ("gpsimd", "g")]:
                def f(e, key=key, snap=snap):
                    for sn, val in snap.items():
                        C.wait(e, key, sn, val)
                on(eng, f)

        consts = SB("consts", [128, 4], F32)
        ident = SB("ident", [128, 128], BF16)
        iot = SB("iot", [128, 128], F32)
        pidx = SB("pidx", [128, 1], F32)
        ones128 = SB("ones128", [128, 128], BF16)
        s_gate = SB("s_gate", [128, 2], F32)
        gmix = SB("gmix", [128, 4], F32)
        negM = SB("negM", [128, 2], F32)
        nsc = SB("nsc", [128, 4 * SC + 6], F32)

        def f(s):
            C.dma(s, "sy", "sg", [(s_gate[:, :], d_gate[:, :])])
        on("sync", f)

        def f(g):
            C.inc(g.iota(iot[:, :], [[1, 128]], channel_multiplier=0,
                         allow_small_or_imprecise_dtypes=True), "ge")
            C.inc(g.iota(pidx[:, :], [[0, 1]], channel_multiplier=1,
                         allow_small_or_imprecise_dtypes=True), "ge")
        on("gpsimd", f)

        def f(v):
            C.inc(v.memset(consts[:, 0:1], -4.0), "ve")
            C.inc(v.memset(consts[:, 1:2], -6.0), "ve")
            C.inc(v.memset(consts[:, 2:3], 1e-5), "ve")
            C.inc(v.memset(ones128[:, :], 1.0), "ve")
            C.wait(v, "v", "ge", 2)
            C.inc(v.tensor_scalar(out=ident[:, :], in0=iot[:, :], scalar1=pidx[:, 0:1],
                                  scalar2=None, op0=ALU.is_equal), "ve")
        on("vector", f)

        def f(s):
            C.wait(s, "s", "sg", 16)
            C.inc(s.activation(out=gmix[:, 0:1], in_=s_gate[:, 0:1], func=AF.Sigmoid), "se")
            C.inc(s.activation(out=gmix[:, 2:3], in_=s_gate[:, 1:2], func=AF.Sigmoid), "se")
        on("scalar", f)

        def f(v):
            C.wait(v, "v", "se")
            C.inc(v.tensor_scalar(out=gmix[:, 1:2], in0=gmix[:, 0:1], scalar1=-1.0,
                                  scalar2=1.0, op0=ALU.mult, op1=ALU.add), "ve")
            C.inc(v.tensor_scalar(out=gmix[:, 3:4], in0=gmix[:, 2:3], scalar1=-1.0,
                                  scalar2=1.0, op0=ALU.mult, op1=ALU.add), "ve")
        on("vector", f)

        phase_barrier()
        with (nc.sbuf_tensor("qrT", [128, 2 * S], BF16) as qrT,
              nc.sbuf_tensor("krT", [128, 2 * S], BF16) as krT,
              nc.sbuf_tensor("vT", [128, 2 * S], BF16) as vT):

            with ExitStack() as bctx:
                _e = bctx.enter_context
                cod = _e(nc.sbuf_tensor("cod", [128, 2 * 1024], U8))
                nib = _e(nc.sbuf_tensor("nib", [128, 2048], U8))
                fub = _e(nc.sbuf_tensor("fub", [128, 2048], BF16))
                g8b = _e(nc.sbuf_tensor("g8b", [128, 2048], BF16))
                mtb = _e(nc.sbuf_tensor("mtb", [128, 2048], BF16))
                r1b = _e(nc.sbuf_tensor("r1b", [128, 2048], BF16))
                scl = _e(nc.sbuf_tensor("scl", [128, 2 * 64], F32))
                wn = _e(nc.sbuf_tensor("wn", [128, 2048], BF16))
                wasm = _e(nc.sbuf_tensor("wasm", [128, H], BF16))
                wstg = _e(nc.sbuf_tensor("wstg", [128, 2 * 1024], BF16))
                ptC = _e(nc.psum_tensor("ptC", [128, 1024], BF16))
                ptD = _e(nc.psum_tensor("ptD", [128, 1024], BF16))

                ptCD = [ptC, ptD]
                ptw_busy = [0, 0]
                ptw_free = [0, 0]
                cod_sem = ["cod", "cod2"]
                cod_free = [0, 0]
                cod_tok = [0, 0]
                stg_free = [0, 0]
                wd_sem = ["wdo", "wdo2"]

                def cod_load(idx, src_cod, src_scl, kb):
                    sl2 = idx % 2

                    def f(s, sl2=sl2, src_cod=src_cod, src_scl=src_scl, kb=kb):
                        C.wait(s, "sc", "ve", cod_free[sl2])
                        C.wait(s, "sc", "ve", gcod_free[sl2])
                        C.dma(s, "sc", cod_sem[sl2],
                              [(cod[:, sl2 * 1024:sl2 * 1024 + kb], src_cod),
                               (scl[:, sl2 * 64:sl2 * 64 + kb // 16], src_scl)])
                        cod_tok[sl2] = C.counts[cod_sem[sl2]]
                    on("scalar", f)

                gcod_free = [0, 0]

                def dq_block(idx, kb, wn_war=0):
                    sl2 = idx % 2
                    K2 = 2 * kb

                    def f(v, sl2=sl2, kb=kb, K2=K2, wn_war=wn_war):
                        C.wait(v, "v", cod_sem[sl2], cod_tok[sl2])
                        C.wait(v, "v", "ve")
                        cslot = cod[:, sl2 * 1024:sl2 * 1024 + kb]
                        C.inc(v.tensor_scalar(out=nib[:, 0:kb], in0=cslot,
                                              scalar1=15, scalar2=None,
                                              op0=ALU.bitwise_and), "ve")
                        C.inc(v.tensor_scalar(out=nib[:, kb:K2], in0=cslot,
                                              scalar1=4, scalar2=None,
                                              op0=ALU.logical_shift_right), "ve")
                        gcod_free[sl2] = C.counts["ve"]
                        C.wait(v, "v", "ve")
                        C.inc(v.tensor_scalar(out=g8b[:, 0:K2], in0=nib[:, 0:K2],
                                              scalar1=8.0,
                                              scalar2=None, op0=ALU.is_ge), "ve")
                        C.wait(v, "v", "ve")
                        C.inc(v.scalar_tensor_tensor(
                            out=mtb[:, 0:K2], in0=g8b[:, 0:K2],
                            scalar=-8.0, in1=nib[:, 0:K2],
                            op0=ALU.mult, op1=ALU.add), "ve")
                        C.wait(v, "v", "ve")
                        C.inc(v.tensor_scalar(out=g8b[:, 0:K2], in0=g8b[:, 0:K2],
                                              scalar1=-2.0, scalar2=1.0,
                                              op0=ALU.mult, op1=ALU.add), "ve")
                        C.inc(v.tensor_scalar(out=r1b[:, 0:K2], in0=mtb[:, 0:K2],
                                              scalar1=-4.0, scalar2=0.0,
                                              op0=ALU.add, op1=ALU.max), "ve")
                        C.wait(v, "v", "ve")
                        C.inc(v.tensor_tensor(out=fub[:, 0:K2], in0=mtb[:, 0:K2],
                                              in1=r1b[:, 0:K2], op=ALU.add), "ve")
                        C.wait(v, "v", "ve")
                        C.inc(v.tensor_scalar(out=r1b[:, 0:K2], in0=mtb[:, 0:K2],
                                              scalar1=-6.0, scalar2=0.0,
                                              op0=ALU.add, op1=ALU.max), "ve")
                        C.wait(v, "v", "ve")
                        C.inc(v.scalar_tensor_tensor(out=fub[:, 0:K2], in0=fub[:, 0:K2],
                                                     scalar=0.5, in1=r1b[:, 0:K2],
                                                     op0=ALU.mult, op1=ALU.add), "ve")
                        C.wait(v, "v", "ve")
                        C.inc(v.tensor_tensor(out=fub[:, 0:K2], in0=fub[:, 0:K2],
                                              in1=g8b[:, 0:K2], op=ALU.mult), "ve")
                        C.wait(v, "v", "ve")
                        C.wait(v, "v", "te", wn_war)
                        sb = scl[:, sl2 * 64:sl2 * 64 + kb // 16].broadcast_to((128, kb // 16, 16))
                        C.inc(v.scalar_tensor_tensor(
                            out=wn[:, 0:K2:2].rearrange("p (b r) -> p b r", r=16),
                            in0=fub[:, 0:kb].rearrange("p (b r) -> p b r", r=16),
                            scalar=1.0, in1=sb, op0=ALU.mult, op1=ALU.mult), "ve")
                        C.inc(v.scalar_tensor_tensor(
                            out=wn[:, 1:K2:2].rearrange("p (b r) -> p b r", r=16),
                            in0=fub[:, kb:K2].rearrange("p (b r) -> p b r", r=16),
                            scalar=1.0, in1=sb, op0=ALU.mult, op1=ALU.mult), "ve")
                        cod_free[sl2] = C.counts["ve"]
                    on("vector", f)
                    return C.counts["ve"]

                def tr_group(srcs_, dst_ap, wn_tok, war=None):
                    pj = tr_group.n % 2
                    tr_group.n += 1
                    n = len(srcs_)

                    def f(t, pj=pj, wn_tok=wn_tok, srcs_=srcs_):
                        C.wait(t, "t", "ve", wn_tok)
                        C.wait(t, "t", "ve", ptw_free[pj])
                        for j, sap in enumerate(srcs_):
                            C.inc(t.transpose(ptCD[pj][:, j * 128:(j + 1) * 128],
                                              sap, ident[:, :]), "te")
                        ptw_busy[pj] = C.counts["te"]
                    on("tensor", f)

                    def f(v, pj=pj, dst_ap=dst_ap, war=war, n=n):
                        if war:
                            for sn, val in war:
                                C.wait(v, "v", sn, val)
                        C.wait(v, "v", "te", ptw_busy[pj])
                        C.inc(v.tensor_copy(out=dst_ap, in_=ptCD[pj][:, 0:n * 128]), "ve")
                        ptw_free[pj] = C.counts["ve"]
                    on("vector", f)
                tr_group.n = 0

                WDB = MLP // 2 // NCORES

                def w_sources():
                    for mi, (dc, ds) in enumerate([(d_gcod, d_gsc), (d_ucod, d_usc)]):
                        for nl in range(NTL):
                            yield ("gu", mi, nl, 1024,
                                   dc[nl * 128:(nl + 1) * 128, :],
                                   ds[nl * 128:(nl + 1) * 128, :])
                    for ht in range(16):
                        yield ("wd", 0, ht, WDB,
                               d_dcod[ht * 128:(ht + 1) * 128, :],
                               d_dsc[ht * 128:(ht + 1) * 128, :])

                wao_last = [0]
                wd_last = [0, 0]

                def emit_out(kind, mi, which, kb, tok):
                    if kind == "gu":
                        for g8 in range(2):
                            tr_group([wn[:, (8 * g8 + j) * 128:(8 * g8 + j + 1) * 128]
                                      for j in range(8)],
                                     wasm[:, g8 * 1024:(g8 + 1) * 1024], tok,
                                     war=[("wao", wao_last[0])] if g8 == 0 else None)

                        def f(s, mi=mi, which=which):
                            C.wait(s, "sc", "ve")
                            bb = mi * NTL + which
                            C.dma(s, "sc", "wao",
                                  [(d_wag_in.ap()[bb * 128:(bb + 1) * 128, :], wasm[:, :])])
                        on("scalar", f)
                        wao_last[0] = C.counts["wao"]
                    else:
                        ht = which
                        for nl0 in range(0, NTL, 8):
                            nls = list(range(nl0, min(nl0 + 8, NTL)))
                            sg = tr_group.n % 2
                            tr_group([wn[:, nl * 128:(nl + 1) * 128] for nl in nls],
                                     wstg[:, sg * 1024:sg * 1024 + len(nls) * 128], tok,
                                     war=[(wd_sem[sg], wd_last[sg])])

                            def f(s, ht=ht, nls=nls, sg=sg):
                                C.wait(s, "sc", "ve")
                                C.dma(s, "sc", wd_sem[sg],
                                      [(d_wag_in.ap()[(2 * NTL + nl) * 128:
                                                      (2 * NTL + nl + 1) * 128,
                                                      ht * 128:(ht + 1) * 128],
                                        wstg[:, sg * 1024 + jj * 128:
                                             sg * 1024 + (jj + 1) * 128])
                                       for jj, nl in enumerate(nls)])
                            on("scalar", f)
                            wd_last[sg] = C.counts[wd_sem[sg]]

                def wsteps():
                    srcs = list(w_sources())
                    cod_load(0, srcs[0][4], srcs[0][5], srcs[0][3])
                    pend = None
                    for idx, (kind, mi, which, kb, sc_, ss_) in enumerate(srcs):
                        if idx + 1 < len(srcs):
                            cod_load(idx + 1, srcs[idx + 1][4], srcs[idx + 1][5],
                                     srcs[idx + 1][3])
                        if pend is not None:
                            emit_out(*pend)
                        wn_war = C.counts["te"]
                        tok = dq_block(idx, kb, wn_war)
                        pend = (kind, mi, which, kb, tok)
                        yield
                    emit_out(*pend)
                    yield

                wgen = wsteps()

                with (nc.sbuf_tensor("xa", [128, 2 * H], F32) as xa,
                      nc.sbuf_tensor("xln", [128, H], BF16) as xln,
                      nc.sbuf_tensor("xlt", [128, 16 * T], BF16) as xlt,
                      nc.sbuf_tensor("st6", [128, 24], F32) as st6,
                      nc.sbuf_tensor("mrs", [128, 4], F32) as mrs,
                      nc.psum_tensor("ptA", [128, 1024], BF16) as ptA,
                      nc.psum_tensor("ptB", [128, 1024], BF16) as ptB):
                    ptAB = [ptA, ptB]
                    xa_tok = [0, 0]
                    xb_tok = [0, 0]
                    xa_free = [0, 0]
                    pt_busy = [0, 0]
                    pt_free = [0, 0]
                    H2 = H // 2
                    TAH = T // AGH

                    def emit_ag(half):
                        def f(g, half=half):
                            C.wait(g, "g", "ve")
                            C.dma(g, "g", "agx",
                                  [(d_agx_in[half].ap()[j * 128:(j + 1) * 128, :],
                                    xlt[:, j * T + half * TAH:j * T + (half + 1) * TAH])
                                   for j in range(16)])
                        on("gpsimd", f)
                        agx_tok = C.counts["agx"]

                        def f(g, half=half, agx_tok=agx_tok):
                            C.wait(g, "g", "agx", agx_tok)
                            C.inc(g.collective_compute(
                                "AllGather", ALU.bypass, replica_groups=[list(range(NCORES))],
                                ins=[d_agx_in[half].ap().opt()],
                                outs=[d_agx_out[half].ap().opt()]), "c1")
                        on("gpsimd", f)

                    for i in range(TT):
                        sl = i % 2

                        def f(s, i=i, sl=sl):
                            C.wait(s, "sy", "ve", xa_free[sl])
                            xa_tok[sl] = C.dma(s, "sy", f"xa{sl}",
                                               [(xa[:, sl * H:sl * H + H2],
                                                 d_x[i * 128:(i + 1) * 128, 0:H2])])
                        on("sync", f)

                        def f(s, i=i, sl=sl):
                            C.wait(s, "sc", "ve", xa_free[sl])
                            xb_tok[sl] = C.dma(s, "sc", f"xb{sl}",
                                               [(xa[:, sl * H + H2:(sl + 1) * H],
                                                 d_x[i * 128:(i + 1) * 128, H2:H])])
                        on("scalar", f)

                        def f(v, sl=sl):
                            C.wait(v, "v", f"xa{sl}", xa_tok[sl])
                            C.wait(v, "v", f"xb{sl}", xb_tok[sl])
                            C.wait(v, "v", "se")
                            for c4 in range(4):
                                C.inc(v.bn_stats(out=st6[:, 6 * c4:6 * c4 + 6],
                                                 in_=xa[:, sl * H + 512 * c4:sl * H + 512 * (c4 + 1)]), "ve")
                            C.wait(v, "v", "ve")
                            C.inc(v.bn_aggr(out=mrs[:, 0:2], in_=st6[:, :]), "ve")
                        on("vector", f)

                        def f(s):
                            C.wait(s, "s", "ve")
                            C.inc(s.activation(out=mrs[:, 2:3], in_=mrs[:, 1:2], func=AF.Sqrt,
                                               bias=consts[:, 2:3], scale=1.0), "se")
                        on("scalar", f)

                        def f(v, sl=sl):
                            C.wait(v, "v", "se")
                            C.wait(v, "v", "te")
                            C.inc(v.reciprocal(out=mrs[:, 3:4], in_=mrs[:, 2:3]), "ve")
                            C.wait(v, "v", "ve")
                            C.inc(v.tensor_scalar(out=xln[:, :], in0=xa[:, sl * H:(sl + 1) * H],
                                                  scalar1=mrs[:, 0:1], scalar2=mrs[:, 3:4],
                                                  op0=ALU.subtract, op1=ALU.mult), "ve")
                            xa_free[sl] = C.counts["ve"]
                        on("vector", f)
                        ln_tok = C.counts["ve"]

                        for g8 in range(2):
                            def f(t, g8=g8, ln_tok=ln_tok):
                                C.wait(t, "t", "ve", ln_tok)
                                C.wait(t, "t", "ve", pt_free[g8])
                                for j8 in range(8):
                                    j = g8 * 8 + j8
                                    C.inc(t.transpose(ptAB[g8][:, j8 * 128:(j8 + 1) * 128],
                                                      xln[:, j * 128:(j + 1) * 128],
                                                      ident[:, :]), "te")
                                pt_busy[g8] = C.counts["te"]
                            on("tensor", f)

                            def f(v, i=i, g8=g8):
                                C.wait(v, "v", "te", pt_busy[g8])
                                for j8 in range(8):
                                    j = g8 * 8 + j8
                                    C.inc(v.tensor_copy(
                                        out=xlt[:, j * T + i * 128:j * T + (i + 1) * 128],
                                        in_=ptAB[g8][:, j8 * 128:(j8 + 1) * 128]), "ve")
                                pt_free[g8] = C.counts["ve"]
                            on("vector", f)

                        if (i + 1) % (TT // AGH) == 0:
                            emit_ag((i + 1) // (TT // AGH) - 1)



                swq = _e(nc.sbuf_tensor("swq", [128, 16 * 256], BF16))
                swk = _e(nc.sbuf_tensor("swk", [128, 16 * 256], BF16))
                swv = _e(nc.sbuf_tensor("swv", [128, 16 * 256], BF16))
                xs = _e(nc.sbuf_tensor("xs", [128, 2 * 16 * BCH], BF16))
                cs = _e(nc.sbuf_tensor("cs", [128, 2 * 2 * BCH], F16))
                rsc = _e(nc.sbuf_tensor("rsc", [128, 2 * BCH], F16))
                rp16 = _e(nc.sbuf_tensor("rp16", [128, 2 * BCH], F16))
                pp1 = _e(nc.psum_tensor("pp1", [128, 3 * 512], F32))
                pp2 = _e(nc.psum_tensor("pp2", [128, 3 * 512], F32))

                def f(g):
                    C.dma(g, "g", "swl",
                          [(dst[:, :].rearrange("p (a b) -> p a b", a=16),
                            src.rearrange("(a p) b -> p a b", p=128))
                           for dst, src in [(swq, d_wqT), (swk, d_wkT), (swv, d_wvT)]])
                on("gpsimd", f)
                sw_tok = C.counts["swl"]

                xs_tok = [0, 0]
                xs_free = [0, 0]
                cs_tok = [0, 0]
                cs_free = [0, 0]
                pp_busy = [0, 0]
                pp_free = [0, 0]
                pps = [pp1, pp2]
                for _ in range(8):
                    next(wgen, None)
                NWSTEP = (40 + SCB - 1) // SCB

                TAH = T // AGH
                H2 = H // 2
                xt_tok = [0, 0]
                tb_order = sorted(range(SCB), key=lambda tb: ((tb * BCH) % T) // TAH)

                def tb_coords(tb):
                    r, off = (tb * BCH) // T, (tb * BCH) % T
                    half = off // TAH
                    return r, half, off - half * TAH

                for ti_, tb in enumerate(tb_order):
                    sl = ti_ % 2
                    r, half, off2 = tb_coords(tb)
                    nj0 = 16 if ti_ == 0 else 8

                    def f(s, sl=sl, r=r, off2=off2, half=half, nj0=nj0):
                        C.wait(s, "sy", "c1", 1 + half)
                        C.wait(s, "sy", "te", xs_free[sl])
                        xs_tok[sl] = C.dma(s, "sy", f"xs{sl}",
                            [(xs[:, sl * 16 * BCH:sl * 16 * BCH + nj0 * BCH]
                              .rearrange("p (j c) -> p j c", j=nj0),
                              d_agx_out[half].ap()[r * H:r * H + nj0 * 128, off2:off2 + BCH]
                              .rearrange("(j p) c -> p j c", p=128))])
                    on("sync", f)

                    if ti_ + 1 < SCB:
                        sl2 = (ti_ + 1) % 2
                        r2, half2, of2 = tb_coords(tb_order[ti_ + 1])

                        def f(s, sl2=sl2, r2=r2, of2=of2, half2=half2):
                            C.wait(s, "sc", "c1", 1 + half2)
                            C.wait(s, "sc", "te", xs_free[sl2])
                            xt_tok[sl2] = C.dma(s, "sc", f"xt{sl2}",
                                [(xs[:, sl2 * 16 * BCH + 8 * BCH:(sl2 + 1) * 16 * BCH]
                                  .rearrange("p (j c) -> p j c", j=8),
                                  d_agx_out[half2].ap()[r2 * H + H2:(r2 + 1) * H, of2:of2 + BCH]
                                  .rearrange("(j p) c -> p j c", p=128))])
                        on("scalar", f)

                    def f(g, sl=sl, tb=tb):
                        C.wait(g, "g", "ve", cs_free[sl])
                        cs_tok[sl] = C.dma(g, "g", f"cs{sl}",
                            [(cs[:, sl * 2 * BCH:sl * 2 * BCH + BCH],
                              d_cos[:, tb * BCH:(tb + 1) * BCH]),
                             (cs[:, sl * 2 * BCH + BCH:(sl + 1) * 2 * BCH],
                              d_sin[:, tb * BCH:(tb + 1) * BCH])])
                    on("gpsimd", f)

                    for pi in range(2):
                        plist = ([("q", 0), ("q", 1), ("v", 0)] if pi == 0
                                 else [("k", 0), ("k", 1), ("v", 1)])

                        def f(t, sl=sl, pi=pi, plist=plist):
                            C.wait(t, "t", "swl", sw_tok)
                            C.wait(t, "t", f"xs{sl}", xs_tok[sl])
                            C.wait(t, "t", f"xt{sl}", xt_tok[sl])
                            C.wait(t, "t", "ve", pp_free[pi])
                            for j in range(16):
                                rhs = xs[:, (sl * 16 + j) * BCH:(sl * 16 + j + 1) * BCH]
                                for oi, (pr, dt) in enumerate(plist):
                                    w = {"q": swq, "k": swk, "v": swv}[pr]
                                    C.inc(t.matmul(
                                        pps[pi][:, oi * 512:oi * 512 + BCH],
                                        w[:, j * 256 + dt * 128:j * 256 + (dt + 1) * 128],
                                        rhs, start=(j == 0), stop=(j == 15)), "te")
                            pp_busy[pi] = C.counts["te"]
                            if pi == 1:
                                xs_free[sl] = C.counts["te"]
                        on("tensor", f)

                        def f(v, sl=sl, pi=pi, tb=tb):
                            C.wait(v, "v", "te", pp_busy[pi])
                            C.wait(v, "v", f"cs{sl}", cs_tok[sl])
                            C.wait(v, "v", "ve")
                            pp = pps[pi]
                            co = cs[:, sl * 2 * BCH:sl * 2 * BCH + BCH]
                            sn = cs[:, sl * 2 * BCH + BCH:(sl + 1) * 2 * BCH]
                            dst = qrT if pi == 0 else krT
                            a, b = tb * BCH, (tb + 1) * BCH
                            C.inc(v.tensor_copy(out=vT[:, (0 if pi == 0 else 1) * S + a:
                                                       (0 if pi == 0 else 1) * S + b],
                                                in_=pp[:, 1024:1024 + BCH]), "ve")
                            C.inc(v.tensor_tensor(out=rsc[:, 0:BCH], in0=pp[:, 0:BCH],
                                                  in1=co, op=ALU.mult), "ve")
                            C.inc(v.tensor_tensor(out=rsc[:, BCH:2 * BCH],
                                                  in0=pp[:, 512:512 + BCH],
                                                  in1=sn, op=ALU.mult), "ve")
                            C.wait(v, "v", "ve")
                            C.inc(v.tensor_tensor(out=dst[:, a:b], in0=rsc[:, 0:BCH],
                                                  in1=rsc[:, BCH:2 * BCH],
                                                  op=ALU.subtract), "ve")
                            C.wait(v, "v", "ve")
                            C.inc(v.tensor_tensor(out=rsc[:, 0:BCH],
                                                  in0=pp[:, 512:512 + BCH],
                                                  in1=co, op=ALU.mult), "ve")
                            C.inc(v.tensor_tensor(out=rsc[:, BCH:2 * BCH],
                                                  in0=pp[:, 0:BCH],
                                                  in1=sn, op=ALU.mult), "ve")
                            pp_free[pi] = C.counts["ve"]
                            C.wait(v, "v", "ve")
                            C.inc(v.tensor_tensor(out=dst[:, S + a:S + b], in0=rsc[:, 0:BCH],
                                                  in1=rsc[:, BCH:2 * BCH], op=ALU.add), "ve")
                            if pi == 1:
                                cs_free[sl] = C.counts["ve"]
                        on("vector", f)

                        if pi == 0:
                            for _ in range(NWSTEP // 2):
                                next(wgen, None)

                    for _ in range(NWSTEP - NWSTEP // 2):
                        next(wgen, None)

                for _ in wgen:
                    pass

                for eng_, key_ in [("vector", "v"), ("scalar", "s"), ("tensor", "t")]:
                    def f(e, key_=key_):
                        C.wait(e, key_, "wao", C.counts["wao"])
                        C.wait(e, key_, "wdo", C.counts["wdo"])
                        C.wait(e, key_, "wdo2", C.counts["wdo2"])
                    on(eng_, f)

                def f(g):
                    C.wait(g, "g", "wao", C.counts["wao"])
                    C.wait(g, "g", "wdo", C.counts["wdo"])
                    C.wait(g, "g", "wdo2", C.counts["wdo2"])
                    C.inc(g.collective_compute(
                        "AllGather", ALU.bypass, replica_groups=[list(range(NCORES))],
                        ins=[d_wag_in.ap().opt()], outs=[d_wag_out.ap().opt()]), "c2")
                on("gpsimd", f)

            with (nc.sbuf_tensor("sqs", [128, 512], BF16) as sqs,
                  nc.psum_tensor("pn", [128, 512], F32) as pn):
                for hi in range(2):
                    for ti, src_ in enumerate([qrT, krT]):
                        for ch in range(SC):
                            def f(s, hi=hi, src_=src_, ch=ch):
                                C.wait(s, "s", "te")
                                C.wait(s, "s", "ve")
                                C.inc(s.activation(
                                    out=sqs[:, :],
                                    in_=src_[:, hi * S + ch * 512:hi * S + (ch + 1) * 512],
                                    func=AF.Square), "se")
                            on("scalar", f)

                            def f(t):
                                C.wait(t, "t", "se")
                                C.wait(t, "t", "ve")
                                C.inc(t.matmul(pn[:, :], ones128[:, :], sqs[:, :],
                                               start=True, stop=True), "te")
                            on("tensor", f)

                            def f(v, hi=hi, ti=ti, ch=ch):
                                C.wait(v, "v", "te")
                                o = (2 * hi + ti) * SC + ch
                                C.inc(v.tensor_reduce(out=nsc[:, o:o + 1], in_=pn[:, :],
                                                      axis=AXX, op=ALU.max), "ve")
                            on("vector", f)

                b = 4 * SC
                for hi in range(2):
                    def f(v, hi=hi, b=b):
                        C.wait(v, "v", "ve")
                        C.inc(v.tensor_reduce(out=nsc[:, b:b + 1],
                                              in_=nsc[:, 2 * hi * SC:(2 * hi + 1) * SC],
                                              axis=AXX, op=ALU.max), "ve")
                        C.inc(v.tensor_reduce(out=nsc[:, b + 1:b + 2],
                                              in_=nsc[:, (2 * hi + 1) * SC:(2 * hi + 2) * SC],
                                              axis=AXX, op=ALU.max), "ve")
                        C.wait(v, "v", "ve")
                        C.inc(v.tensor_tensor(out=nsc[:, b + 2 + hi:b + 3 + hi],
                                              in0=nsc[:, b:b + 1],
                                              in1=nsc[:, b + 1:b + 2],
                                              op=ALU.mult), "ve")
                    on("vector", f)

                def f(s, b=b):
                    C.wait(s, "s", "ve")
                    C.inc(s.activation(out=nsc[:, b + 4:b + 6],
                                       in_=nsc[:, b + 2:b + 4], func=AF.Sqrt), "se")
                on("scalar", f)

                def f(v, b=b):
                    C.wait(v, "v", "se")
                    C.inc(v.tensor_scalar(out=negM[:, 0:2],
                                          in0=nsc[:, b + 4:b + 6],
                                          scalar1=-INV_SQRT_DH, scalar2=None,
                                          op0=ALU.mult), "ve")
                on("vector", f)

            NEGM_TOK[0] = C.counts["ve"]

            with nc.sbuf_tensor("cbT", [128, 2 * S], BF16) as cbT:
                for hi in range(2):
                    attn_head(nc, C, on, cfg, hi, qrT, krT, vT, ident, gmix, negM, cbT)

                    def f(s, hi=hi):
                        C.wait(s, "sy", "ve")
                        C.dma(s, "sy", "cmi",
                              [(d_cmb_in[hi].ap()[:, :], cbT[:, hi * S:(hi + 1) * S])])
                    on("sync", f)
                    cmi_tok = C.counts["cmi"]

                    def f(g, hi=hi, cmi_tok=cmi_tok):
                        C.wait(g, "g", "cmi", cmi_tok)
                        C.inc(g.collective_compute(
                            "AllGather", ALU.bypass, replica_groups=[list(range(NCORES))],
                            ins=[d_cmb_in[hi].ap().opt()], outs=[d_cmb_out[hi].ap().opt()]), "c3")
                    on("gpsimd", f)

        with nc.sbuf_tensor("xlt2", [128, 16 * T], BF16) as xlt2:
            phase_barrier()
            phase_D(nc, C, on, cfg, ident, consts, d_cmb_out, d_woT, d_x, d_x1, xlt2)

            phase_barrier()
            phase_E(nc, C, on, cfg, d_wag_out, d_x1, d_y, xlt2)

        def f(s):
            for sn in ("st0", "st1", "st2", "st3"):
                C.wait(s, "sy", sn, C.counts[sn])
        on("sync", f)

    return nc


def attn_head(nc, C, on, cfg, hi, qrT, krT, vT, ident, gmix, negM, cbT):
    S, T, KT, SC = cfg.S, cfg.T, cfg.KT, cfg.SC
    base = hi * S
    with (nc.sbuf_tensor(f"va{hi}", [128, KT * 132], BF16) as vaug,
          nc.sbuf_tensor(f"mo{hi}", [128, KT * 130], BF16) as mo,
          nc.sbuf_tensor(f"sc{hi}", [128, 1024], F32) as sc,
          nc.sbuf_tensor(f"mem{hi}", [128, 132], BF16) as mem,
          nc.sbuf_tensor(f"pb{hi}", [128, 2 * 1024], BF16) as pbuf,
          nc.sbuf_tensor(f"pvs{hi}", [128, 4 * 132], F32) as pvs,
          nc.sbuf_tensor(f"rc{hi}", [128, 2], F32) as rcp):

        with (nc.sbuf_tensor(f"qf{hi}", [128, S], BF16) as qf,
              nc.sbuf_tensor(f"kfT{hi}", [128, S], BF16) as kfT,
              nc.sbuf_tensor(f"kn4{hi}", [128, 512], BF16) as kfn4,
              nc.psum_tensor(f"pm{hi}", [128, 132], F32) as pmm,
              nc.psum_tensor(f"pm2{hi}", [128, 132], F32) as pmm2,
              nc.psum_tensor(f"pt{hi}", [128, 512], BF16) as ptr,
              nc.psum_tensor(f"pt3{hi}", [128, 512], BF16) as ptrB,
              nc.psum_tensor(f"pt5{hi}", [128, 512], BF16) as ptrC,
              nc.psum_tensor(f"pt6{hi}", [128, 512], BF16) as ptrD):
            pairs = [(ptr, ptrB), (ptrC, ptrD)]
            pair_busy = [0, 0]
            pair_free = [0, 0]

            def f(v):
                C.wait(v, "v", "te")
                C.wait(v, "v", "se")
                C.inc(v.memset(vaug[:, 128:KT * 132:132], 1.0), "ve")
                C.inc(v.tensor_scalar(out=qf[:, :], in0=qrT[:, base:base + S],
                                      scalar1=0.0, scalar2=None, op0=ALU.min), "ve")
                C.inc(v.tensor_scalar(out=kfT[:, :], in0=krT[:, base:base + S],
                                      scalar1=0.0, scalar2=None, op0=ALU.min), "ve")
            on("vector", f)

            def f(s):
                C.wait(s, "s", "ve")
                C.inc(s.activation(out=qf[:, :], in_=qf[:, :], func=AF.Exp), "se")
                C.inc(s.activation(out=kfT[:, :], in_=kfT[:, :], func=AF.Exp), "se")
            on("scalar", f)

            def f(v):
                C.wait(v, "v", "se")
                C.inc(v.scalar_tensor_tensor(out=qf[:, :], in0=qrT[:, base:base + S],
                                             scalar=0.0, in1=qf[:, :],
                                             op0=ALU.max, op1=ALU.add), "ve")
                C.inc(v.scalar_tensor_tensor(out=kfT[:, :], in0=krT[:, base:base + S],
                                             scalar=0.0, in1=kfT[:, :],
                                             op0=ALU.max, op1=ALU.add), "ve")
            on("vector", f)
            kf_tok = C.counts["ve"]
            kn4_free = [0]

            for ch in range(SC):
                def f(t, ch=ch, kf_tok=kf_tok):
                    pA, pB = pairs[ch % 2]
                    C.wait(t, "t", "ve", kf_tok)
                    C.wait(t, "t", "ve", pair_free[ch % 2])
                    for q in range(4):
                        kt = ch * 4 + q
                        C.inc(t.transpose(pA[:, q * 128:(q + 1) * 128],
                                          vT[:, base + kt * 128:base + (kt + 1) * 128],
                                          ident[:, :]), "te")
                        C.inc(t.transpose(pB[:, q * 128:(q + 1) * 128],
                                          kfT[:, kt * 128:(kt + 1) * 128],
                                          ident[:, :]), "te")
                    pair_busy[ch % 2] = C.counts["te"]
                on("tensor", f)

                def f(v, ch=ch):
                    pA, pB = pairs[ch % 2]
                    C.wait(v, "v", "te", pair_busy[ch % 2])
                    C.wait(v, "v", "te", kn4_free[0])
                    for q in range(4):
                        kt = ch * 4 + q
                        C.inc(v.tensor_copy(out=vaug[:, kt * 132:kt * 132 + 128],
                                            in_=pA[:, q * 128:(q + 1) * 128]), "ve")
                    C.inc(v.tensor_copy(out=kfn4[:, :], in_=pB[:, :]), "ve")
                    pair_free[ch % 2] = C.counts["ve"]
                on("vector", f)

                def f(t, ch=ch):
                    C.wait(t, "t", "ve")
                    for q in range(4):
                        kt = ch * 4 + q
                        C.inc(t.matmul(pmm[:, 0:129], kfn4[:, q * 128:(q + 1) * 128],
                                       vaug[:, kt * 132:kt * 132 + 129],
                                       start=(kt == 0), stop=(kt == KT - 1)), "te")
                    kn4_free[0] = C.counts["te"]
                on("tensor", f)

            def f(v):
                C.wait(v, "v", "te")
                C.inc(v.tensor_copy(out=mem[:, 0:129], in_=pmm[:, 0:129]), "ve")
            on("vector", f)

            mo_busy = [0, 0]
            mo_free = [0, 0]
            with (nc.psum_tensor(f"po{hi}", [128, 396], F32) as pmo1,
                  nc.psum_tensor(f"po2{hi}", [128, 396], F32) as pmo2):
                pmos = [pmo1, pmo2]
                for g0 in range(0, KT, 3):
                    g2 = (g0 // 3) % 2
                    kts = tuple(range(g0, min(g0 + 3, KT)))

                    def f(t, g0=g0, g2=g2, kts=kts):
                        if g0 == 0:
                            C.wait(t, "t", "ve")
                        C.wait(t, "t", "ve", mo_free[g2])
                        for j, kt in enumerate(kts):
                            C.inc(t.matmul(pmos[g2][:, j * 132:j * 132 + 129],
                                           qf[:, kt * 128:(kt + 1) * 128],
                                           mem[:, 0:129], start=True, stop=True), "te")
                        mo_busy[g2] = C.counts["te"]
                    on("tensor", f)

                    def f(v, g2=g2, kts=kts):
                        C.wait(v, "v", "te", mo_busy[g2])
                        for j, kt in enumerate(kts):
                            C.inc(v.tensor_copy(out=mo[:, kt * 130:kt * 130 + 129],
                                                in_=pmos[g2][:, j * 132:j * 132 + 129]), "ve")
                        mo_free[g2] = C.counts["ve"]
                    on("vector", f)

        with nc.sbuf_tensor(f"cn{hi}", [128, S], BF16) as cbN:
            attn_softmax(nc, C, on, cfg, hi, qrT, krT, vT, ident, gmix, negM,
                         cbT, vaug, mo, sc, pbuf, rcp, cbN, pvs)


def attn_softmax(nc, C, on, cfg, hi, qrT, krT, vT, ident, gmix, negM,
                 cbT, vaug, mo, sc, pbuf, rcp, cbN, pvs):
    S, T, KT, SC = cfg.S, cfg.T, cfg.KT, cfg.SC
    base = hi * S
    if True:
        with (nc.psum_tensor(f"pSa{hi}", [128, 1024], F32) as pSa,
              nc.psum_tensor(f"pSb{hi}", [128, 1024], F32) as pSb,
              nc.psum_tensor(f"pv0{hi}", [128, 129], F32) as pv0,
              nc.psum_tensor(f"pv1{hi}", [128, 129], F32) as pv1,
              nc.psum_tensor(f"pv2{hi}", [128, 129], F32) as pv2,
              nc.psum_tensor(f"pv3{hi}", [128, 129], F32) as pv3):
            pS = [pSa, pSb]
            ppv = [pv0, pv1, pv2, pv3]
            ppv_free = [0]
            NG = KT // 2

            for qc in range(SC):
                TE0 = C.counts["te"]
                SE0 = C.counts["se"]

                def fT(t, qc=qc, TE0=TE0, SE0=SE0):
                    C.wait(t, "t", "ve", ppv_free[0])
                    if qc == 0:
                        C.wait(t, "t", "ve")
                    C.wait(t, "t", "se", SE0)

                    def smm(kg):
                        gs = kg % 2
                        if kg >= 2:
                            C.wait(t, "t", "se", SE0 + kg - 1)
                        for u in range(2):
                            kt = kg * 2 + u
                            C.inc(t.matmul(pS[gs][:, u * 512:(u + 1) * 512],
                                           krT[:, base + kt * 128:base + (kt + 1) * 128],
                                           qrT[:, base + qc * 512:base + (qc + 1) * 512],
                                           start=True, stop=True), "te")

                    def pv(kg):
                        gs = kg % 2
                        C.wait(t, "t", "se", SE0 + kg + 1)
                        for u in range(2):
                            kt = kg * 2 + u
                            for qs in range(4):
                                C.inc(t.matmul(
                                    ppv[qs][:, 0:129],
                                    pbuf[:, gs * 1024 + u * 512 + qs * 128:
                                         gs * 1024 + u * 512 + (qs + 1) * 128],
                                    vaug[:, kt * 132:kt * 132 + 129],
                                    start=(kt == 0), stop=(kt == KT - 1)), "te")

                    smm(0)
                    for kg in range(1, NG):
                        smm(kg)
                        pv(kg - 1)
                    pv(NG - 1)
                on("tensor", fT)

                def fS(s, hi=hi, TE0=TE0):
                    C.wait(s, "s", "ve", NEGM_TOK[0])
                    for kg in range(NG):
                        gs = kg % 2
                        C.wait(s, "s", "te", TE0 + (2 if kg == 0 else 10 * kg - 6))
                        C.inc(s.activation(out=pbuf[:, gs * 1024:(gs + 1) * 1024],
                                           in_=pS[gs][:, :], func=AF.Exp,
                                           bias=negM[:, hi:hi + 1], scale=INV_SQRT_DH), "se")
                on("scalar", fS)

                def f(v):
                    C.wait(v, "v", "te")
                    C.wait(v, "v", "ve")
                    for qs in range(4):
                        C.inc(v.tensor_copy(out=pvs[:, qs * 132:qs * 132 + 129],
                                            in_=ppv[qs][:, 0:129]), "ve")
                    ppv_free[0] = C.counts["ve"]
                on("vector", f)

                for qs in range(4):
                    tt = qc * 4 + qs

                    def f(v, hi=hi, qs=qs, tt=tt):
                        C.wait(v, "v", "ve")
                        C.inc(v.reciprocal(out=rcp[:, 0:1],
                                           in_=pvs[:, qs * 132 + 128:qs * 132 + 129]), "ve")
                        C.inc(v.reciprocal(out=rcp[:, 1:2],
                                           in_=mo[:, tt * 130 + 128:tt * 130 + 129]), "ve")
                        C.wait(v, "v", "ve")
                        C.inc(v.tensor_scalar(out=sc[:, 0:128],
                                              in0=pvs[:, qs * 132:qs * 132 + 128],
                                              scalar1=rcp[:, 0:1],
                                              scalar2=gmix[:, 2 * hi + 1:2 * hi + 2],
                                              op0=ALU.mult, op1=ALU.mult), "ve")
                        C.inc(v.tensor_scalar(out=sc[:, 128:256],
                                              in0=mo[:, tt * 130:tt * 130 + 128],
                                              scalar1=rcp[:, 1:2],
                                              scalar2=gmix[:, 2 * hi:2 * hi + 1],
                                              op0=ALU.mult, op1=ALU.mult), "ve")
                        C.wait(v, "v", "ve")
                        C.inc(v.tensor_tensor(out=cbN[:, tt * 128:(tt + 1) * 128],
                                              in0=sc[:, 0:128],
                                              in1=sc[:, 128:256], op=ALU.add), "ve")
                    on("vector", f)

        with (nc.psum_tensor(f"pt2{hi}", [128, 1024], BF16) as ptr2,
              nc.psum_tensor(f"pt4{hi}", [128, 1024], BF16) as ptr2b):
            p3 = [ptr2, ptr2b]
            s3_busy = [0, 0]
            s3_free = [0, 0]
            for g in range(KT // 8):
                def f(t, g=g):
                    if g == 0:
                        C.wait(t, "t", "ve")
                    C.wait(t, "t", "ve", s3_free[g % 2])
                    for j in range(8):
                        tt = g * 8 + j
                        C.inc(t.transpose(p3[g % 2][:, j * 128:(j + 1) * 128],
                                          cbN[:, tt * 128:(tt + 1) * 128],
                                          ident[:, :]), "te")
                    s3_busy[g % 2] = C.counts["te"]
                on("tensor", f)

                def f(v, hi=hi, g=g):
                    C.wait(v, "v", "te", s3_busy[g % 2])
                    C.inc(v.tensor_copy(out=cbT[:, hi * S + g * 1024:hi * S + (g + 1) * 1024],
                                        in_=p3[g % 2][:, :]), "ve")
                    s3_free[g % 2] = C.counts["ve"]
                on("vector", f)


def phase_D(nc, C, on, cfg, ident, consts, d_cmb_out, d_woT, d_x, d_x1, xlt2):
    S, T, TT = cfg.S, cfg.T, cfg.TT
    with (nc.sbuf_tensor("cstT", [128, 16 * T], BF16) as cstT,
          nc.sbuf_tensor("swo", [128, 16 * H], BF16) as swo,
          nc.sbuf_tensor("xa2", [128, 2 * H], F32) as xa2,
          nc.sbuf_tensor("x1t", [128, H], F32) as x1t,
          nc.sbuf_tensor("xln2", [128, H], BF16) as xln2,
          nc.sbuf_tensor("st6b", [128, 24], F32) as st6,
          nc.sbuf_tensor("mrs2", [128, 4], F32) as mrs,
          nc.psum_tensor("pwo", [128, 2048], F32) as pwo,
          nc.psum_tensor("ptE", [128, 1024], BF16) as ptE,
          nc.psum_tensor("ptF", [128, 1024], BF16) as ptF):
        ptEF = [ptE, ptF]

        def f(g):
            C.dma(g, "g", "swo",
                  [(swo[:, :].rearrange("p (a b) -> p a b", a=16),
                    d_woT.rearrange("(a p) b -> p a b", p=128))])
            pid = nc.partition_id(engines=[g.engine])
            off = pid * T
            C.wait(g, "g", "c3", 1)
            C.dma(g, "g", "cst",
                  [(cstT[:, r * T:(r + 1) * T],
                    d_cmb_out[0].ap()[r * 128:(r + 1) * 128, bass.ds(off, T)])
                   for r in range(NCORES)])
            C.wait(g, "g", "c3", 2)
            C.dma(g, "g", "cst2",
                  [(cstT[:, (r + 8) * T:(r + 9) * T],
                    d_cmb_out[1].ap()[r * 128:(r + 1) * 128, bass.ds(off, T)])
                   for r in range(NCORES)])
        on("gpsimd", f)
        swo_tok = C.counts["swo"]
        cst_tok = C.counts["cst"]
        cst2_tok = C.counts["cst2"]

        xa_tok = [0, 0]
        xa_free = [0, 0]
        pt_busy = [0, 0]
        pt_free = [0, 0]
        pwo_free = [0]
        x1sp_tok = [0]

        for i in range(TT):
            sl = i % 2

            def f(s, i=i, sl=sl):
                C.wait(s, "sy", "ve", xa_free[sl])
                xa_tok[sl] = C.dma(s, "sy", f"xa{sl}",
                                   [(xa2[:, sl * H:(sl + 1) * H],
                                     d_x[i * 128:(i + 1) * 128, :])])
            on("sync", f)

            def f(t, i=i):
                C.wait(t, "t", "swo", swo_tok)
                C.wait(t, "t", "cst", cst_tok)
                C.wait(t, "t", "ve", pwo_free[0])
                for dt in range(16):
                    if dt == 8:
                        C.wait(t, "t", "cst2", cst2_tok)
                    lhs = cstT[:, dt * T + i * 128:dt * T + (i + 1) * 128]
                    for ec in range(4):
                        C.inc(t.matmul(pwo[:, ec * 512:(ec + 1) * 512], lhs,
                                       swo[:, dt * H + ec * 512:dt * H + (ec + 1) * 512],
                                       start=(dt == 0), stop=(dt == 15)), "te")
            on("tensor", f)

            def f(v, sl=sl):
                C.wait(v, "v", "te")
                C.wait(v, "v", f"xa{sl}", xa_tok[sl])
                C.wait(v, "v", "x1s", x1sp_tok[0])
                C.wait(v, "v", "ve")
                C.wait(v, "v", "se")
                C.inc(v.tensor_tensor(out=x1t[:, :], in0=xa2[:, sl * H:(sl + 1) * H],
                                      in1=pwo[:, :], op=ALU.add), "ve")
                xa_free[sl] = C.counts["ve"]
                pwo_free[0] = C.counts["ve"]
                C.wait(v, "v", "ve")
                for c4 in range(4):
                    C.inc(v.bn_stats(out=st6[:, 6 * c4:6 * c4 + 6],
                                     in_=x1t[:, 512 * c4:512 * (c4 + 1)]), "ve")
                C.wait(v, "v", "ve")
                C.inc(v.bn_aggr(out=mrs[:, 0:2], in_=st6[:, :]), "ve")
            on("vector", f)
            x1_tok = C.counts["ve"] - 2

            def f(s, i=i, x1_tok=x1_tok):
                C.wait(s, "sy", "ve", x1_tok)
                x1sp_tok[0] = C.dma(s, "sy", "x1s",
                                    [(d_x1.ap()[i * 128:(i + 1) * 128, :], x1t[:, :])])
            on("sync", f)

            def f(s):
                C.wait(s, "s", "ve")
                C.inc(s.activation(out=mrs[:, 2:3], in_=mrs[:, 1:2], func=AF.Sqrt,
                                   bias=consts[:, 2:3], scale=1.0), "se")
            on("scalar", f)

            def f(v):
                C.wait(v, "v", "se")
                C.wait(v, "v", "te")
                C.inc(v.reciprocal(out=mrs[:, 3:4], in_=mrs[:, 2:3]), "ve")
                C.wait(v, "v", "ve")
                C.inc(v.tensor_scalar(out=xln2[:, :], in0=x1t[:, :],
                                      scalar1=mrs[:, 0:1], scalar2=mrs[:, 3:4],
                                      op0=ALU.subtract, op1=ALU.mult), "ve")
            on("vector", f)
            ln_tok = C.counts["ve"]

            for g8 in range(2):
                def f(t, g8=g8, ln_tok=ln_tok):
                    C.wait(t, "t", "ve", ln_tok)
                    C.wait(t, "t", "ve", pt_free[g8])
                    for j8 in range(8):
                        j = g8 * 8 + j8
                        C.inc(t.transpose(ptEF[g8][:, j8 * 128:(j8 + 1) * 128],
                                          xln2[:, j * 128:(j + 1) * 128],
                                          ident[:, :]), "te")
                    pt_busy[g8] = C.counts["te"]
                on("tensor", f)

                def f(v, i=i, g8=g8):
                    C.wait(v, "v", "te", pt_busy[g8])
                    for j8 in range(8):
                        j = g8 * 8 + j8
                        C.inc(v.tensor_copy(out=xlt2[:, j * T + i * 128:j * T + (i + 1) * 128],
                                            in_=ptEF[g8][:, j8 * 128:(j8 + 1) * 128]), "ve")
                    pt_free[g8] = C.counts["ve"]
                on("vector", f)


def phase_E(nc, C, on, cfg, d_wag_out, d_x1, d_y, xlt2):
    S, T, MLP, NTL, NTG, T2 = cfg.S, cfg.T, cfg.MLP, cfg.NTL, cfg.NTG, cfg.T2
    nblk = 3 * NTL
    TL2 = T2 // 128
    with (nc.sbuf_tensor("wgu", [128, 6 * H], BF16) as wgu,
          nc.sbuf_tensor("geb", [128, 512], BF16) as geb,
          nc.sbuf_tensor("upb", [128, 512], BF16) as upb,
          nc.sbuf_tensor("it", [128, NTG * T2], BF16) as it,
          nc.sbuf_tensor("wdb", [128, 4 * 512], BF16) as wdb,
          nc.sbuf_tensor("xr", [128, 4 * 512], F32) as xr,
          nc.sbuf_tensor("yo", [128, 4 * 512], F32) as yo):

        def rows_of(ntg, m):
            r, b = ntg // NTL, ntg % NTL
            return 128 * (nblk * r + m * NTL + b)

        wd_tok = [0, 0, 0, 0]
        wd_free = [0, 0, 0, 0]
        xr_tok = [0, 0, 0, 0]
        xr_free = [0, 0, 0, 0]
        for th in range(cfg.ffn_tsplit):
            t0 = th * T2
            wgu_tok = [0, 0, 0]
            wgu_tok2 = [0, 0, 0]
            wgu_free = [C.counts["te"]] * 3
            with (nc.psum_tensor(f"pg{th}", [128, 512], F32) as pg,
                  nc.psum_tensor(f"pu{th}", [128, 512], F32) as pu,
                  nc.psum_tensor(f"pg2{th}", [128, 512], F32) as pg2,
                  nc.psum_tensor(f"pu2{th}", [128, 512], F32) as pu2):
                pgs = [(pg, pu), (pg2, pu2)]
                pp_free = [C.counts["ve"], C.counts["ve"]]
                TE0e = C.counts["te"]
                VE0e = C.counts["ve"]
                SE0e = C.counts["se"]

                def fSg(s, TE0e=TE0e, VE0e=VE0e):
                    for ntg in range(NTG):
                        sl = ntg % 2
                        C.wait(s, "s", "te", TE0e + 32 * (ntg + 1))
                        if ntg >= 1:
                            C.wait(s, "s", "ve", VE0e + 2 * ntg)
                        C.inc(s.activation(out=geb[:, 0:T2], in_=pgs[sl][0][:, 0:T2],
                                           func=AF.Gelu_apprx_tanh), "se")
                on("scalar", fSg)

                for ntg in range(NTG):
                    sl = ntg % 3
                    ps_ = ntg % 2

                    def f(s, ntg=ntg, sl=sl):
                        C.wait(s, "sy", "c2", 1)
                        C.wait(s, "sy", "te", wgu_free[sl])
                        rg = rows_of(ntg, 0)
                        wgu_tok[sl] = C.dma(s, "sy", f"wg{sl}",
                            [(wgu[:, sl * 2 * H:sl * 2 * H + H],
                              d_wag_out.ap()[rg:rg + 128, :])])
                    on("sync", f)

                    def f(g, ntg=ntg, sl=sl):
                        C.wait(g, "g", "c2", 1)
                        C.wait(g, "g", "te", wgu_free[sl])
                        ru = rows_of(ntg, 1)
                        wgu_tok2[sl] = C.dma(g, "g", f"wu{sl}",
                            [(wgu[:, sl * 2 * H + H:(sl + 1) * 2 * H],
                              d_wag_out.ap()[ru:ru + 128, :])])
                    on("gpsimd", f)

                    def f(t, ntg=ntg, sl=sl, ps_=ps_, t0=t0):
                        C.wait(t, "t", f"wg{sl}", wgu_tok[sl])
                        C.wait(t, "t", f"wu{sl}", wgu_tok2[sl])
                        C.wait(t, "t", "ve", pp_free[ps_])
                        for j in range(16):
                            rhs = xlt2[:, j * T + t0:j * T + t0 + T2]
                            C.inc(t.matmul(
                                pgs[ps_][0][:, 0:T2],
                                wgu[:, sl * 2 * H + j * 128:sl * 2 * H + (j + 1) * 128],
                                rhs, start=(j == 0), stop=(j == 15)), "te")
                            C.inc(t.matmul(
                                pgs[ps_][1][:, 0:T2],
                                wgu[:, sl * 2 * H + H + j * 128:sl * 2 * H + H + (j + 1) * 128],
                                rhs, start=(j == 0), stop=(j == 15)), "te")
                        wgu_free[sl] = C.counts["te"]
                    on("tensor", f)
                    mm_tok = C.counts["te"]

                    def f(v, ntg=ntg, ps_=ps_, mm_tok=mm_tok, SE0e=SE0e):
                        C.wait(v, "v", "te", mm_tok)
                        C.wait(v, "v", "ve")
                        C.inc(v.tensor_copy(out=upb[:, 0:T2], in_=pgs[ps_][1][:, 0:T2]), "ve")
                        C.wait(v, "v", "se", SE0e + ntg + 1)
                        C.wait(v, "v", "ve")
                        C.inc(v.tensor_tensor(out=it[:, ntg * T2:(ntg + 1) * T2],
                                              in0=geb[:, 0:T2], in1=upb[:, 0:T2],
                                              op=ALU.mult), "ve")
                        pp_free[ps_] = C.counts["ve"]
                    on("vector", f)

            it_tok = C.counts["ve"]

            with (nc.psum_tensor(f"pd{th}", [128, TL2 * 512], F32) as pd_,
                  nc.psum_tensor(f"pe{th}", [128, TL2 * 512], F32) as pe_):
                pd_pair = [pd_, pe_]
                pd_free = [C.counts["ve"], C.counts["ve"]]
                se_snap = C.counts["se"]
                for hc in range(4):
                    pd = pd_pair[hc % 2]

                    def f(t, hc=hc, se_snap=se_snap):
                        C.wait(t, "t", "se", se_snap)
                        C.wait(t, "t", "ve", pd_free[hc % 2])
                    on("tensor", f)

                    for tt2 in range(TL2):
                        def f(s, th=th, hc=hc, tt2=tt2):
                            C.wait(s, "sc", "ve", xr_free[tt2])
                            C.wait(s, "sc", "x1s", C.counts["x1s"])
                            row = th * T2 + tt2 * 128
                            xr_tok[tt2] = C.dma(s, "sc", f"xr{tt2}",
                                [(xr[:, tt2 * 512:(tt2 + 1) * 512],
                                  d_x1.ap()[row:row + 128, hc * 512:(hc + 1) * 512])])
                        on("scalar", f)

                    for ntg in range(NTG):
                        sl = ntg % 4
                        qeng, qkey = (("sync", "sy") if sl % 2 == 0
                                      else ("scalar", "sc"))

                        def f(s, ntg=ntg, sl=sl, hc=hc, qkey=qkey):
                            C.wait(s, qkey, "te", wd_free[sl])
                            rd = rows_of(ntg, 2)
                            wd_tok[sl] = C.dma(s, qkey, f"wd{sl}",
                                [(wdb[:, sl * 512:(sl + 1) * 512],
                                  d_wag_out.ap()[rd:rd + 128, hc * 512:(hc + 1) * 512])])
                        on(qeng, f)

                        def f(t, ntg=ntg, sl=sl, it_tok=it_tok):
                            C.wait(t, "t", f"wd{sl}", wd_tok[sl])
                            C.wait(t, "t", "ve", it_tok)
                            for tt2 in range(TL2):
                                C.inc(t.matmul(
                                    pd[:, tt2 * 512:(tt2 + 1) * 512],
                                    it[:, ntg * T2 + tt2 * 128:ntg * T2 + (tt2 + 1) * 128],
                                    wdb[:, sl * 512:(sl + 1) * 512],
                                    start=(ntg == 0), stop=(ntg == NTG - 1)), "te")
                            wd_free[sl] = C.counts["te"]
                        on("tensor", f)

                    for tt2 in range(TL2):
                        def f(v, tt2=tt2, pd=pd):
                            C.wait(v, "v", "te")
                            C.wait(v, "v", f"xr{tt2}", xr_tok[tt2])
                            C.wait(v, "v", f"st{tt2}", C.counts[f"st{tt2}"])
                            C.inc(v.tensor_tensor(out=yo[:, tt2 * 512:(tt2 + 1) * 512],
                                                  in0=xr[:, tt2 * 512:(tt2 + 1) * 512],
                                                  in1=pd[:, tt2 * 512:(tt2 + 1) * 512],
                                                  op=ALU.add), "ve")
                            xr_free[tt2] = C.counts["ve"]
                        on("vector", f)
                        yo_tok = C.counts["ve"]

                        def f(s, th=th, hc=hc, tt2=tt2, yo_tok=yo_tok):
                            C.wait(s, "sy", "ve", yo_tok)
                            row = th * T2 + tt2 * 128
                            C.dma(s, "sy", f"st{tt2}",
                                  [(d_y[row:row + 128, hc * 512:(hc + 1) * 512],
                                    yo[:, tt2 * 512:(tt2 + 1) * 512])])
                        on("sync", f)
                    pd_free[hc % 2] = C.counts["ve"]



def _host_layouts(inputs, cfg: Cfg):
    x = np.ascontiguousarray(np.asarray(inputs["L0_attn_hidden_states"],
                                        dtype=np.float32).reshape(cfg.S, H))
    wqT = np.ascontiguousarray(np.asarray(inputs["L0_attn_Wq"], dtype=np.float32).T)
    wkT = np.ascontiguousarray(np.asarray(inputs["L0_attn_Wk"], dtype=np.float32).T)
    wvT = np.ascontiguousarray(np.asarray(inputs["L0_attn_Wv"], dtype=np.float32).T)
    wo = np.asarray(inputs["L0_attn_Wo"], dtype=np.float32)
    gate = np.asarray(inputs["L0_attn_gate"], dtype=np.float32).reshape(NH)

    woT = np.ascontiguousarray(wo.T)

    inv_freq = 1.0 / 10000.0 ** (np.arange(0, H, 2, dtype=np.float64) / H)
    t = np.arange(cfg.S, dtype=np.float64)
    ang = t[None, :] * inv_freq[:, None]
    cosT = np.cos(ang).astype(np.float32)
    sinT = np.sin(ang).astype(np.float32)

    def codes_u8(a):
        a = np.ascontiguousarray(np.asarray(a, dtype=np.int32))
        return np.ascontiguousarray(a.view(np.uint8)[:, ::4])

    gcod = codes_u8(inputs["L0_ffn_gate_W_packed"])
    ucod = codes_u8(inputs["L0_ffn_up_W_packed"])
    dcod = codes_u8(inputs["L0_ffn_down_W_packed"])
    gsc = np.ascontiguousarray(np.asarray(inputs["L0_ffn_gate_scales"], dtype=np.float32))
    usc = np.ascontiguousarray(np.asarray(inputs["L0_ffn_up_scales"], dtype=np.float32))
    dsc = np.ascontiguousarray(np.asarray(inputs["L0_ffn_down_scales"], dtype=np.float32))

    NPC = cfg.MLP // NCORES
    in_maps = []
    for c in range(NCORES):
        dsl = slice(c * DH, (c + 1) * DH)
        dsl2 = slice(H // 2 + c * DH, H // 2 + (c + 1) * DH)
        in_maps.append({
            "x": np.ascontiguousarray(x[c * cfg.T:(c + 1) * cfg.T]),
            "wqT": np.ascontiguousarray(np.concatenate([wqT[:, dsl], wqT[:, dsl2]], axis=1)),
            "wkT": np.ascontiguousarray(np.concatenate([wkT[:, dsl], wkT[:, dsl2]], axis=1)),
            "wvT": np.ascontiguousarray(np.concatenate([wvT[:, dsl], wvT[:, dsl2]], axis=1)),
            "woT": woT,
            "cosT": np.ascontiguousarray(cosT[c * DH:(c + 1) * DH]),
            "sinT": np.ascontiguousarray(sinT[c * DH:(c + 1) * DH]),
            "gate2": np.ascontiguousarray(np.broadcast_to(gate[[c, c + 8]].reshape(1, 2), (128, 2))),
            "gcod": np.ascontiguousarray(gcod[c * NPC:(c + 1) * NPC]),
            "ucod": np.ascontiguousarray(ucod[c * NPC:(c + 1) * NPC]),
            "dcod": np.ascontiguousarray(dcod[:, c * (NPC // 2):(c + 1) * (NPC // 2)]),
            "gsc": np.ascontiguousarray(gsc[c * NPC:(c + 1) * NPC]),
            "usc": np.ascontiguousarray(usc[c * NPC:(c + 1) * NPC]),
            "dsc": np.ascontiguousarray(dsc[:, c * (NPC // 32):(c + 1) * (NPC // 32)]),
        })
    return in_maps


def kernel(**inputs):
    cfg = Cfg()
    in_maps = _host_layouts(inputs, cfg)
    nc = build_program(cfg)
    res = run_bass_kernel_spmd(nc, in_maps, core_ids=list(range(NCORES)))
    y = np.concatenate([np.asarray(r["y"]) for r in res.results], axis=0)
    return np.ascontiguousarray(y.reshape(1, cfg.S, H), dtype=np.float32)
